# version 21
# baseline (speedup 1.0000x reference)
import numpy as np
import ml_dtypes
from contextlib import ExitStack

import concourse.bass as bass
import concourse.mybir as mybir
from concourse.bass_utils import run_bass_kernel_spmd
from concourse.alu_op_type import AluOpType as ALU

F32 = mybir.dt.float32
BF16 = mybir.dt.bfloat16
AF = mybir.ActivationFunctionType
AX = mybir.AxisListType.X
NEG = -3.0e38

TT = 4096
DM = 1024
NPT = DM // 128
NTC = TT // 512
BF = ml_dtypes.bfloat16

FVALS = [0.0, 1.0]

N_LAUNCH = [0]


class Buf:
    __slots__ = ("name", "w", "r")

    def __init__(self, name=""):
        self.name = name
        self.w = None
        self.r = {}

    def __repr__(self):
        return f"Buf({self.name})"


DMAQ = ("QS", "QP", "QA")
SEQ = {"T": "T", "V": "V", "S": "S", "P": "P", "QS": "QS", "QP": "P", "QA": "S"}
ALL_E = ("T", "V", "S", "P", "QS", "QP", "QA")


NROT = 8


class Ctx:
    def __init__(self, nc, stack):
        self.nc = nc
        self.stack = stack
        self.sem = {e: stack.enter_context(nc.semaphore("sem_" + e))
                    for e in ("T", "V", "S", "P")}
        for q in DMAQ:
            for k in range(NROT):
                self.sem[f"{q}{k}"] = stack.enter_context(nc.semaphore(f"sem_{q}{k}"))
        self.cnt = {e: 0 for e in ("T", "V", "S", "P")}
        self.qcnt = {q: 0 for q in DMAQ}
        self.eng = {"T": nc.tensor, "V": nc.vector, "S": nc.scalar, "P": nc.gpsimd,
                    "QS": nc.sync, "QP": nc.gpsimd, "QA": nc.scalar}
        self.waited = {s: {} for s in ("T", "V", "S", "P", "QS")}

    @staticmethod
    def _dep(deps, ec):
        e, c = ec
        if deps.get(e, 0) < c:
            deps[e] = c

    def emit(self, e, fn, reads=(), writes=()):
        seq = SEQ[e]
        is_dma = e in DMAQ
        deps = {}
        for b in reads:
            if b.w:
                self._dep(deps, b.w)
        for b in writes:
            if b.w:
                self._dep(deps, b.w)
            for pe, pc in b.r.items():
                self._dep(deps, (pe, pc))
        if is_dma:
            i = self.qcnt[e]
            tok = f"{e}{i % NROT}"
            if i >= NROT:
                self._dep(deps, (tok, i // NROT))
        w = self.waited[seq]
        eng = self.eng[e]
        for pe in sorted(deps):
            pc = deps[pe]
            if pe == "T" and e == "T":
                continue
            if w.get(pe, 0) >= pc:
                continue
            eng.wait_ge(self.sem[pe], pc * (1 if pe in self.cnt else 16))
            w[pe] = pc
        ins = fn(eng)
        if is_dma:
            self.qcnt[e] = i + 1
            ins.then_inc(self.sem[tok], 16)
            me = (tok, i // NROT + 1)
        else:
            self.cnt[e] += 1
            ins.then_inc(self.sem[e], 1)
            me = (e, self.cnt[e])
        for b in reads:
            if b.r.get(me[0], 0) < me[1]:
                b.r[me[0]] = me[1]
        for b in writes:
            b.w = me
            b.r = {}
        return ins

    def _tokens(self):
        toks = dict(self.cnt)
        for q in DMAQ:
            i = self.qcnt[q]
            for k in range(NROT):
                v = (i - k + NROT - 1) // NROT if i > k else 0
                v = (i - 1 - k) // NROT + 1 if i > k else 0
                toks[f"{q}{k}"] = v
        return toks

    def barrier(self):
        cur = self._tokens()
        for seq in ("T", "V", "S", "P", "QS"):
            w = self.waited[seq]
            eng = self.eng[seq]
            for pe, pc in cur.items():
                if pc and w.get(pe, 0) < pc:
                    eng.wait_ge(self.sem[pe], pc * (1 if pe in self.cnt else 16))
                    w[pe] = pc

    from contextlib import contextmanager as _cm

    @_cm
    def scope(self):
        old = self.stack
        st = ExitStack()
        self.stack = st
        try:
            yield st
        finally:
            st.close()
            self.stack = old
            self.barrier()

    def dma(self, q, dst_ap, src_ap, reads=(), writes=()):
        return self.emit(q, lambda E: E.dma_start(dst_ap, src_ap), reads, writes)

    def _uniq(self, name):
        self._names = getattr(self, "_names", {})
        n = self._names.get(name, 0)
        self._names[name] = n + 1
        return name if n == 0 else f"{name}_u{n}"

    def sb(self, name, shape, dtype):
        name = self._uniq(name)
        t = self.stack.enter_context(self.nc.sbuf_tensor(name, shape, dtype))
        return t, Buf(name)

    def ps(self, name, shape, dtype=F32):
        name = self._uniq(name)
        t = self.stack.enter_context(self.nc.psum_tensor(name, shape, dtype))
        return t, Buf(name)


def host_consts():
    c = {}
    c["c_ident"] = np.eye(128, dtype=BF)
    c["c_identf"] = np.eye(128, dtype=np.float32)
    c["c_ones1x"] = np.ones((1, 128), dtype=BF)
    c["c_ones1xf"] = np.ones((1, 128), dtype=np.float32)
    c["c_onesk"] = np.ones((128, 1), dtype=BF)
    c["c_oneskf"] = np.ones((128, 1), dtype=np.float32)
    c["c_onesb"] = np.ones((128, 128), dtype=BF)
    c["c_onesbf"] = np.ones((128, 128), dtype=np.float32)
    c["c_one11"] = np.ones((1, 1), dtype=BF)
    c["c_one11f"] = np.ones((1, 1), dtype=np.float32)
    b2 = np.zeros((128, 2), dtype=BF)
    b2[0:64, 0] = 1
    b2[64:128, 1] = 1
    c["c_blk2"] = b2
    c["c_blk2T"] = np.ascontiguousarray(b2.T)
    c["c_blk2f"] = b2.astype(np.float32)
    c["c_blk2Tf"] = np.ascontiguousarray(b2.T).astype(np.float32)
    ii = np.arange(128)
    c["c_maskd"] = np.where(ii[None, :] > ii[:, None], NEG, 0.0).astype(np.float32)
    mT = np.zeros((128, 4 * 512), dtype=np.float32)
    for r in range(4):
        j = ii[:, None]
        i = np.arange(512)[None, :]
        mT[:, r * 512:(r + 1) * 512] = np.where(i < 128 * r + j, NEG, 0.0)
    c["c_maskdT"] = mT
    c["c_masklow"] = np.where(ii[None, :] < ii[:, None], NEG, 0.0).astype(np.float32)
    mb = np.zeros((128, 4 * 512), dtype=np.float32)
    for r in range(4):
        j = ii[:, None]
        i = np.arange(512)[None, :]
        mb[:, r * 512:(r + 1) * 512] = np.where(i > 128 * r + j, NEG, 0.0)
    c["c_maskbT"] = mb
    rot = np.zeros((128, 128), dtype=BF)
    for blk in range(2):
        o = blk * 64
        for f in range(32):
            rot[o + f + 32, o + f] = -1
            rot[o + f, o + f + 32] = 1
    c["c_rot"] = rot
    c["c_iota"] = np.arange(128, dtype=np.float32).reshape(1, 128)
    c["c_ustrict"] = np.triu(np.ones((128, 128), dtype=BF), 1)
    c["c_fvals"] = np.tile(np.array([FVALS], dtype=np.float32), (128, 1))
    c["c_fvalsb"] = np.tile(np.array([FVALS], dtype=BF), (128, 1))
    return c


DRAM_CONSTS = ("c_maskdT", "c_maskbT", "c_maskd", "c_masklow")


def declare_consts(nc, ctx, stack):
    out = {}
    for name, arr in host_consts().items():
        dt = BF16 if arr.dtype == BF else F32
        d = nc.dram_tensor(name, list(arr.shape), dt, kind="ExternalInput").ap()
        b = Buf(name)
        if name in DRAM_CONSTS:
            out[name] = (d, b)
            continue
        t = stack.enter_context(nc.sbuf_tensor("sb_" + name, list(arr.shape), dt))
        ctx.dma("QS", t[:], d, writes=(b,))
        out[name] = (t, b)
    fv = out["c_fvals"][0]
    fvb = out["c_fvalsb"][0]
    for i, v in enumerate(FVALS):
        nc.const_aps.register(F32, float(v), fv[:, i:i + 1])
        nc.const_aps.register(BF16, float(v), fvb[:, i:i + 1])
    ctx.barrier()
    return out


def mm_group(ctx, out_ap, pieces, reads, wbuf):
    n = len(pieces)

    def fn(E):
        ins = None
        for i, (lhs, rhs) in enumerate(pieces):
            ins = E.matmul(out_ap, lhs, rhs, start=(i == 0), stop=(i == n - 1))
        return ins

    return ctx.emit("T", fn, reads=reads, writes=(wbuf,))


def row_to_col(ctx, consts, dst_ap, src_row_ap, reads, wbuf):
    one11, one11_b = consts["c_one11"]
    return ctx.emit("T", lambda E: E.matmul(dst_ap, src_row_ap, one11[:, :], start=True, stop=True),
                    reads=tuple(reads) + (one11_b,), writes=(wbuf,))


def ln_produce(ctx, consts, sc, x_dram_ap_fn, x_dram_bufs, xb_tiles, xn_tiles,
               eps=1e-5, n_pt=NPT, width=TT, xlo_tiles=None, zout=None):
    nc = ctx.nc
    ntc = width // 512
    (stg, stg_b), (stg2, stg2_b) = sc["stg"]
    x2s = sc["x2"]
    (rsum, rsum_b) = sc["rsum"]
    (rsq, rsq_b) = sc["rsq"]
    (rtmp, rtmp_b) = sc["rtmp"]
    (psum_s, psum_s_b) = sc["ps_a"]
    (psum_q, psum_q_b) = sc["ps_b"]
    (ps_bc, ps_bc_b) = sc["ps_c"]
    onesk, onesk_b = consts["c_onesk"]
    ones1xf, ones1xf_b = consts["c_ones1xf"]

    ctx.emit("V", lambda E: E.memset(rsum[:], 0.0), writes=(rsum_b,))
    ctx.emit("V", lambda E: E.memset(rsq[:], 0.0), writes=(rsq_b,))
    sq_jobs = []
    for pt in range(n_pt):
        sg, sg_b = (stg, stg_b) if pt % 2 == 0 else (stg2, stg2_b)
        xb_t, xb_b = xb_tiles[pt]
        ctx.dma("QS", sg[:], x_dram_ap_fn(pt), reads=(x_dram_bufs[pt],), writes=(sg_b,))
        ctx.emit("V", lambda E, sg=sg, xb_t=xb_t: E.tensor_copy(xb_t[:], sg[:]),
                 reads=(sg_b,), writes=(xb_b,))

        def do_mm(pt2, tc, x2, x2_b, xb_t=None, xb_b=None):
            t, b = xb_tiles[pt2]
            mm_group(ctx, psum_s[:], [(onesk[:, :], t[:, tc * 512:(tc + 1) * 512])],
                     reads=(onesk_b, b), wbuf=psum_s_b)
            mm_group(ctx, psum_q[:], [(onesk[:, :], x2[:, :])],
                     reads=(onesk_b, x2_b), wbuf=psum_q_b)
            s_tc = rsum[:, tc * 512:(tc + 1) * 512]
            q_tc = rsq[:, tc * 512:(tc + 1) * 512]
            ctx.emit("V", lambda E, s_tc=s_tc: E.tensor_tensor(s_tc, s_tc, psum_s[:], ALU.add),
                     reads=(psum_s_b, rsum_b), writes=(rsum_b,))
            ctx.emit("V", lambda E, q_tc=q_tc: E.tensor_tensor(q_tc, q_tc, psum_q[:], ALU.add),
                     reads=(psum_q_b, rsq_b), writes=(rsq_b,))

        for tc in range(ntc):
            x2, x2_b = x2s[(pt * ntc + tc) % len(x2s)]
            ctx.emit("S", lambda E, xb_t=xb_t, x2=x2, tc=tc:
                     E.activation(x2[:], xb_t[:, tc * 512:(tc + 1) * 512], AF.Square),
                     reads=(xb_b,), writes=(x2_b,))
            sq_jobs.append((pt, tc, x2, x2_b))
            if len(sq_jobs) > 2:
                do_mm(*sq_jobs.pop(0))
    while sq_jobs:
        do_mm(*sq_jobs.pop(0))
    dinv = 1.0 / (n_pt * 128)
    ctx.emit("V", lambda E: E.tensor_scalar(rsum[:], rsum[:], dinv, None, ALU.mult),
             reads=(rsum_b,), writes=(rsum_b,))
    ctx.emit("V", lambda E: E.tensor_scalar(rsq[:], rsq[:], dinv, None, ALU.mult),
             reads=(rsq_b,), writes=(rsq_b,))
    for tcc in range(ntc):
        sl = slice(tcc * 512, (tcc + 1) * 512)
        ctx.emit("V", lambda E, sl=sl: E.scalar_tensor_tensor(
            rtmp[:], rsum[:, sl], -1.0, rsum[:, sl], ALU.mult, ALU.mult),
            reads=(rsum_b,), writes=(rtmp_b,))
        ctx.emit("V", lambda E, sl=sl: E.tensor_tensor(rsq[:, sl], rsq[:, sl], rtmp[:], ALU.add),
                 reads=(rsq_b, rtmp_b), writes=(rsq_b,))
    ctx.emit("V", lambda E: E.tensor_scalar(rsq[:], rsq[:], float(eps), None, ALU.add),
             reads=(rsq_b,), writes=(rsq_b,))
    ctx.emit("S", lambda E: E.activation(rsq[:], rsq[:], AF.Ln),
             reads=(rsq_b,), writes=(rsq_b,))
    ctx.emit("S", lambda E: E.activation(rsq[:], rsq[:], AF.Exp, scale=-0.5),
             reads=(rsq_b,), writes=(rsq_b,))
    ctx.emit("V", lambda E: E.scalar_tensor_tensor(rsum[:], rsum[:], -1.0, rsq[:], ALU.mult, ALU.mult),
             reads=(rsum_b, rsq_b), writes=(rsum_b,))
    fstg = sc.get("fstg")
    (ps_bc2, ps_bc2_b) = sc["ps_d"]
    for tc in range(ntc):
        sl = slice(tc * 512, (tc + 1) * 512)
        mm_group(ctx, ps_bc[:], [(ones1xf[:, :], rsq[:, sl])],
                 reads=(ones1xf_b, rsq_b), wbuf=ps_bc_b)
        mm_group(ctx, ps_bc2[:], [(ones1xf[:, :], rsum[:, sl])],
                 reads=(ones1xf_b, rsum_b), wbuf=ps_bc2_b)
        for pt in range(n_pt):
            xb_t, xb_b = xb_tiles[pt]
            xn_t, xn_b = xn_tiles[pt]
            ft, ft_b = fstg[pt % 2]
            ctx.emit("V", lambda E, xb_t=xb_t, ft=ft, sl=sl:
                     E.tensor_tensor(ft[:], xb_t[:, sl], ps_bc[:], ALU.mult),
                     reads=(xb_b, ps_bc_b), writes=(ft_b,))
            ctx.emit("V", lambda E, ft=ft: E.tensor_tensor(ft[:], ft[:], ps_bc2[:], ALU.add),
                     reads=(ft_b, ps_bc2_b), writes=(ft_b,))
            if zout is not None:
                ctx.dma("QS", zout[0][pt * 128:(pt + 1) * 128, sl], ft[:],
                        reads=(ft_b,), writes=(zout[1],))
            if xlo_tiles is None:
                ctx.emit("V", lambda E, xn_t=xn_t, ft=ft, sl=sl:
                         E.tensor_copy(xn_t[:, sl], ft[:]),
                         reads=(ft_b,), writes=(xn_b,))
            else:
                xl_t, xl_b = xlo_tiles[pt]
                ctx.emit("V", lambda E, xn_t=xn_t, ft=ft, sl=sl:
                         E.tensor_copy(xn_t[:, sl], ft[:]),
                         reads=(ft_b,), writes=(xn_b,))
                ctx.emit("V", lambda E, xn_t=xn_t, xl_t=xl_t, ft=ft, sl=sl:
                         E.tensor_tensor(xl_t[:, sl], ft[:], xn_t[:, sl], ALU.subtract),
                         reads=(ft_b, xn_b), writes=(xl_b,))


def ln_scratch(ctx, tag, width=TT):
    sc = {}
    sc["stg"] = [ctx.sb(f"{tag}_stg{i}", [128, width], F32) for i in range(2)]
    sc["x2"] = [ctx.sb(f"{tag}_x2{i}", [128, 512], BF16) for i in range(4)]
    sc["rsum"] = ctx.sb(f"{tag}_rsum", [1, width], F32)
    sc["rsq"] = ctx.sb(f"{tag}_rsq", [1, width], F32)
    sc["rtmp"] = ctx.sb(f"{tag}_rtmp", [1, 512], F32)
    sc["fstg"] = [ctx.sb(f"{tag}_fs{i}", [128, 512], F32) for i in range(2)]
    sc["ps_a"] = ctx.ps(f"{tag}_psa", [1, 512], F32)
    sc["ps_d"] = ctx.ps(f"{tag}_psd", [128, 512], F32)
    sc["ps_b"] = ctx.ps(f"{tag}_psb", [1, 512], F32)
    sc["ps_c"] = ctx.ps(f"{tag}_psc", [128, 512], F32)
    return sc


def linear_fm(ctx, get_w, get_x, evict, n_k, obs, tcs, ps_pairs, tcw=512):
    i = 0
    for ob in obs:
        if ob_begin is not None:
            ob_begin(ob)
        for tc in tcs:
            ps_t, ps_b = ps_pairs[i % len(ps_pairs)]
            i += 1
            pieces = []
            reads = []
            for kc in range(n_k):
                wap, wb = get_w(kc, ob)
                xap, xb = get_x(kc, tc)
                pieces.append((wap, xap))
                reads += [wb, xb]
            fw = pieces[0][0].shape[-1] if hasattr(pieces[0][0], "shape") else 128
            mm_group(ctx, ps_t[0:fw, 0:tcw], pieces, reads=tuple(reads), wbuf=ps_b)
            evict(ob, tc, ps_t, ps_b)
        if ob_end is not None:
            ob_end(ob)


def bcast_row_f32(ctx, consts, ps_pair, dst_ap, dst_buf, row_ap, row_bufs):
    ones1xf, ones1xf_b = consts["c_ones1xf"]
    ps_t, ps_b = ps_pair
    n = 512
    mm_group(ctx, ps_t[:, 0:n], [(ones1xf[:, :], row_ap)],
             reads=(ones1xf_b,) + tuple(row_bufs), wbuf=ps_b)
    ctx.emit("V", lambda E: E.tensor_copy(dst_ap, ps_t[:, 0:n]),
             reads=(ps_b,), writes=(dst_buf,))


def rowsq_stats(ctx, consts, src_tiles, ps_row_pair, sq_stage, tc, n_pt=NPT):
    onesk, onesk_b = consts["c_onesk"]
    ps_t, ps_b = ps_row_pair
    sl = slice(tc * 512, (tc + 1) * 512)

    for pt in range(n_pt):
        st_t, st_b = sq_stage[pt % 2]
        t, b = src_tiles[pt]
        ctx.emit("S", lambda E, t=t, st_t=st_t: E.activation(st_t[:], t[:, sl], AF.Square),
                 reads=(b,), writes=(st_b,))

        def fn(E, st_t=st_t, pt=pt):
            return E.matmul(ps_t[:, :], onesk[:, :], st_t[:, :],
                            start=(pt == 0), stop=(pt == n_pt - 1))

        ctx.emit("T", fn, reads=(onesk_b, st_b), writes=(ps_b,))
    return ps_t, ps_b


def ternarize_weight(ctx, consts, tag, w_ap, w_buf, n_pt, cols,
                     wt_tiles, cache_dram_ap, cache_dram_buf):
    nc = ctx.nc
    oneskf, oneskf_b = consts["c_oneskf"]
    ones1xf, ones1xf_b = consts["c_ones1xf"]
    with ctx.scope():
        own = wt_tiles is None
        ccw = min(cols, 1024)
        ncc = cols // ccw
        stg = [ctx.sb(f"{tag}_stg{i}", [128, cols], F32) for i in range(2)]
        red = ctx.sb(f"{tag}_red", [128, 1], F32)
        acc = ctx.sb(f"{tag}_acc", [128, 1], F32)
        arow = ctx.sb(f"{tag}_arow", [1, 1], F32)
        thr = ctx.sb(f"{tag}_thr", [128, 1], F32)
        sgn = ctx.sb(f"{tag}_sgn", [128, ccw], F32)
        msk = ctx.sb(f"{tag}_msk", [128, ccw], F32)
        wt_local = None if not own else [ctx.sb(f"{tag}_wt{i}", [128, cols], BF16)
                                         for i in range(2)]
        ps_a = ctx.ps(f"{tag}_psa", [1, 1], F32)
        ps_f = ctx.ps(f"{tag}_psf", [128, 1], F32)

        ctx.emit("V", lambda E: E.memset(acc[0][:], 0.0), writes=(acc[1],))
        for rt in range(n_pt):
            sg, sg_b = stg[rt % 2]
            ctx.dma("QS", sg[:], w_ap[rt * 128:(rt + 1) * 128, :],
                    reads=(w_buf,), writes=(sg_b,))
            ctx.emit("V", lambda E, sg=sg: E.reduce_sum(red[0][:], sg[:], axis=AX,
                                                        apply_absolute_value=True),
                     reads=(sg_b,), writes=(red[1],))
            ctx.emit("V", lambda E: E.tensor_tensor(acc[0][:], acc[0][:], red[0][:], ALU.add),
                     reads=(acc[1], red[1]), writes=(acc[1],))
        mm_group(ctx, ps_a[0][:, :], [(oneskf[:, :], acc[0][:, :])],
                 reads=(oneskf_b, acc[1]), wbuf=ps_a[1])
        ctx.emit("V", lambda E: E.tensor_scalar(arow[0][:, :], ps_a[0][:, :],
                                                0.5 / (n_pt * 128 * cols), None, ALU.mult),
                 reads=(ps_a[1],), writes=(arow[1],))
        mm_group(ctx, ps_f[0][:, :], [(ones1xf[:, :], arow[0][:, :])],
                 reads=(ones1xf_b, arow[1]), wbuf=ps_f[1])
        ctx.emit("V", lambda E: E.tensor_copy(thr[0][:], ps_f[0][:, :]),
                 reads=(ps_f[1],), writes=(thr[1],))

        for rt in range(n_pt):
            sg, sg_b = stg[rt % 2]
            ctx.dma("QS", sg[:], w_ap[rt * 128:(rt + 1) * 128, :],
                    reads=(w_buf,), writes=(sg_b,))
            wt, wt_b = wt_tiles[rt] if not own else wt_local[rt % 2]
            for cc in range(ncc):
                cs = slice(cc * ccw, (cc + 1) * ccw)
                ctx.emit("S", lambda E, sg=sg, cs=cs: E.sign(sgn[0][:], sg[:, cs]),
                         reads=(sg_b,), writes=(sgn[1],))
                ctx.emit("S", lambda E, sg=sg, cs=cs: E.activation(msk[0][:], sg[:, cs], AF.Abs),
                         reads=(sg_b,), writes=(msk[1],))
                ctx.emit("V", lambda E: E.tensor_scalar(msk[0][:], msk[0][:], thr[0][:],
                                                        None, ALU.is_gt),
                         reads=(msk[1], thr[1]), writes=(msk[1],))
                ctx.emit("V", lambda E, wt=wt, cs=cs: E.tensor_tensor(
                    wt[:, cs], sgn[0][:], msk[0][:], ALU.mult),
                    reads=(sgn[1], msk[1]), writes=(wt_b,))
            ctx.dma("QS", cache_dram_ap[:, rt * cols:(rt + 1) * cols], wt[:],
                    reads=(wt_b,), writes=(cache_dram_buf,))


class ResidPt:

    def __init__(self, ctx, tag, xin_ap, xin_bufs, xout_ap, xout_bufs, c0=0,
                 cw=TT, nbuf=2):
        self.ctx = ctx
        self.xin_ap, self.xin_bufs = xin_ap, xin_bufs
        self.xout_ap, self.xout_bufs = xout_ap, xout_bufs
        self.c0, self.cw = c0, cw
        self.tiles = [ctx.sb(f"{tag}_rp{i}", [128, cw], F32) for i in range(nbuf)]
        self.cur = {}

    def begin(self, ob):
        t, b = self.tiles[ob % len(self.tiles)]
        self.cur[ob] = (t, b)
        self.ctx.dma("QS", t[:],
                     self.xin_ap[ob * 128:(ob + 1) * 128, self.c0:self.c0 + self.cw],
                     reads=(self.xin_bufs[ob],), writes=(b,))

    def sl(self, ob, tc):
        t, b = self.cur[ob]
        return t[:, tc * 512 - self.c0:(tc + 1) * 512 - self.c0], b

    def end(self, ob):
        t, b = self.cur[ob]
        self.ctx.dma("QS",
                     self.xout_ap[ob * 128:(ob + 1) * 128, self.c0:self.c0 + self.cw],
                     t[:], reads=(b,), writes=(self.xout_bufs[ob],))


def resid_evict(ctx, xin_ap, xin_bufs, xout_ap, xout_bufs, xsl_pool, ob, tc,
                prep, scale=0.5, width=512):
    xt, xb = xsl_pool[(ob * 8 + tc) % len(xsl_pool)]
    sl = slice(tc * width, (tc + 1) * width)
    ctx.dma("QS", xt[:, 0:width], xin_ap[ob * 128:(ob + 1) * 128, sl],
            reads=(xin_bufs[ob],), writes=(xb,))
    prep(xt[:, 0:width], xb)
    ctx.dma("QS", xout_ap[ob * 128:(ob + 1) * 128, sl], xt[:, 0:width],
            reads=(xb,), writes=(xout_bufs[ob],))


def emit_l0_ffn(ctx, consts, X, W, width=TT):
    nc = ctx.nc
    xin_ap, xin_bufs = X["in"]
    xout_ap, xout_bufs = X["out"]
    DF = 4096
    ntc = width // 512
    onesk, onesk_b = consts["c_onesk"]
    with ctx.scope():
        xn = [ctx.sb(f"l0f_xn{p}", [128, width], BF16) for p in range(NPT)]
        c1b = ctx.sb("l0f_c1b", [128, width], F32)
        with ctx.scope():
            sc = ln_scratch(ctx, "l0f_ln", width)
            ln_produce(ctx, consts, sc, lambda pt: xin_ap[pt * 128:(pt + 1) * 128, :],
                       xin_bufs, xn, xn, width=width)
        with ctx.scope():
            sqst = [ctx.sb(f"l0f_sq{i}", [128, 512], BF16) for i in range(2)]
            psr = ctx.ps("l0f_psr", [1, 512], F32)
            psc = ctx.ps("l0f_psc", [128, 512], F32)
            rrow = ctx.sb("l0f_rms", [1, width], F32)
            for tc in range(ntc):
                rowsq_stats(ctx, consts, xn, psr, sqst, tc)
                rt, rb = rrow
                sl = slice(tc * 512, (tc + 1) * 512)
                ctx.emit("S", lambda E, sl=sl: E.activation(rt[:, sl], psr[0][:, :],
                                                            AF.Sqrt, scale=1.0 / DM),
                         reads=(psr[1],), writes=(rb,))
                ctx.emit("V", lambda E, sl=sl: E.tensor_scalar(rt[:, sl], rt[:, sl],
                                                               1e-8, None, ALU.add),
                         reads=(rb,), writes=(rb,))
                ctx.emit("V", lambda E, sl=sl: E.reciprocal(rt[:, sl], rt[:, sl]),
                         reads=(rb,), writes=(rb,))
                bcast_row_f32(ctx, consts, psc, c1b[0][:, sl], c1b[1], rt[:, sl], (rb,))
        if W["first"]:
            ternarize_weight(ctx, consts, "l0f_td", W["w_downT"][0], W["w_downT"][1],
                             32, 1024, None, W["cd"][0], W["cd"][1])
        wu = [ctx.sb(f"l0f_wu{p}", [128, DF], BF16) for p in range(NPT)]
        if W["first"]:
            ternarize_weight(ctx, consts, "l0f_tu", W["w_upT"][0], W["w_upT"][1],
                             NPT, DF, wu, W["cu"][0], W["cu"][1])
        else:
            for p in range(NPT):
                ctx.dma("QP", wu[p][0][:], W["cu"][0][:, p * DF:(p + 1) * DF],
                        reads=(W["cu"][1],), writes=(wu[p][1],))
        with ctx.scope():
            hb = [ctx.sb(f"l0f_h{i}", [128, 512], BF16) for i in range(32)]
            t1 = [ctx.sb(f"l0f_t1{i}", [128, 512], F32) for i in range(2)]
            h2 = [ctx.sb(f"l0f_h2{i}", [128, 512], BF16) for i in range(3)]
            wd = [ctx.sb(f"l0f_wd{i}", [128, 4096], BF16) for i in range(1)]
            xsl = [ctx.sb(f"l0f_xsl{i}", [128, 512], F32) for i in range(4)]
            c2b = ctx.sb("l0f_c2b", [128, 512], F32)
            c2r = ctx.sb("l0f_c2r", [1, 512], F32)
            psm = [ctx.ps(f"l0f_mm{i}", [128, 512]) for i in range(2)]
            psd = [ctx.ps(f"l0f_dn{i}", [128, 512]) for i in range(4)]
            psq = ctx.ps("l0f_psq", [1, 512])
            psb2 = ctx.ps("l0f_psb2", [128, 512])
            for tc in range(ntc):
                sl = slice(tc * 512, (tc + 1) * 512)
                sqq = []

                def do_sq(fb, st_t, st_b):
                    ctx.emit("T", lambda E, st_t=st_t, fb=fb: E.matmul(
                        psq[0][:, :], onesk[:, :], st_t[:],
                        start=(fb == 0), stop=(fb == 31)),
                        reads=(onesk_b, st_b), writes=(psq[1],))

                for fb in range(32):
                    ps_t, ps_b = psm[fb % 2]
                    pieces = [(wu[kc][0][:, fb * 128:(fb + 1) * 128], xn[kc][0][:, sl])
                              for kc in range(NPT)]
                    reads = tuple(wu[kc][1] for kc in range(NPT)) + \
                        tuple(xn[kc][1] for kc in range(NPT))
                    mm_group(ctx, ps_t[:, :], pieces, reads=reads, wbuf=ps_b)
                    tt, tb = t1[fb % 2]
                    ctx.emit("V", lambda E, tt=tt, ps_t=ps_t, sl=sl:
                             E.tensor_tensor(tt[:], ps_t[:, :], c1b[0][:, sl], ALU.mult),
                             reads=(ps_b, c1b[1]), writes=(tb,))
                    ht, hbuf = hb[fb]
                    ctx.emit("S", lambda E, ht=ht, tt=tt: E.activation(ht[:], tt[:], AF.Gelu),
                             reads=(tb,), writes=(hbuf,))
                    st_t, st_b = h2[fb % len(h2)]
                    ctx.emit("S", lambda E, st_t=st_t, ht=ht:
                             E.activation(st_t[:], ht[:], AF.Square),
                             reads=(hbuf,), writes=(st_b,))
                    sqq.append((fb, st_t, st_b))
                    if len(sqq) > 2:
                        do_sq(*sqq.pop(0))
                while sqq:
                    do_sq(*sqq.pop(0))
                ctx.emit("S", lambda E: E.activation(c2r[0][:, :], psq[0][:, :],
                                                     AF.Sqrt, scale=1.0 / DF),
                         reads=(psq[1],), writes=(c2r[1],))
                ctx.emit("V", lambda E: E.tensor_scalar(c2r[0][:, :], c2r[0][:, :],
                                                        1e-8, None, ALU.add),
                         reads=(c2r[1],), writes=(c2r[1],))
                ctx.emit("V", lambda E: E.reciprocal(c2r[0][:, :], c2r[0][:, :]),
                         reads=(c2r[1],), writes=(c2r[1],))
                bcast_row_f32(ctx, consts, psb2, c2b[0][:, :], c2b[1],
                              c2r[0][:, :], (c2r[1],))
                for half in range(2):
                    obs = list(range(half * 4, half * 4 + 4))
                    for j, ob in enumerate(obs):
                        xt, xb = xsl[j % len(xsl)] if False else xsl[j]
                        ctx.dma("QS", xt[:], xin_ap[ob * 128:(ob + 1) * 128,
                                                    tc * 512:(tc + 1) * 512],
                                reads=(xin_bufs[ob],), writes=(xb,))
                    for g in range(8):
                        wdt, wdb = wd[0]
                        ctx.dma("QS", wdt[:], W["cd"][0][:, g * 4096:(g + 1) * 4096],
                                reads=(W["cd"][1],), writes=(wdb,))
                        for k4 in range(4):
                            kc = g * 4 + k4
                            for j, ob in enumerate(obs):
                                ctx.emit("T", lambda E, wdt=wdt, ob=ob, j=j, kc=kc, k4=k4:
                                         E.matmul(
                                    psd[j][0][:, :],
                                    wdt[:, k4 * 1024 + ob * 128:k4 * 1024 + (ob + 1) * 128],
                                    hb[kc][0][:, :], start=(kc == 0), stop=(kc == 31)),
                                    reads=(wdb, hb[kc][1]), writes=(psd[j][1],))
                    for j, ob in enumerate(obs):
                        xt, xb = xsl[j]
                        tt, tb = t1[j % 2]
                        ctx.emit("V", lambda E, tt=tt, j=j: E.tensor_tensor(
                            tt[:], psd[j][0][:, :], c2b[0][:, :], ALU.mult),
                            reads=(psd[j][1], c2b[1]), writes=(tb,))
                        ctx.emit("V", lambda E, xt=xt, tt=tt:
                                 E.scalar_tensor_tensor(xt[:], tt[:], 0.5, xt[:],
                                                        ALU.mult, ALU.add),
                                 reads=(tb, xb), writes=(xb,))
                        ctx.dma("QS", xout_ap[ob * 128:(ob + 1) * 128,
                                              tc * 512:(tc + 1) * 512], xt[:],
                                reads=(xb,), writes=(xout_bufs[ob],))


def build_sublayer_nc(emit_fn, weight_specs, internal_specs, width=TT,
                      extra_outputs=()):
    nc = bass.Bass("TRN2", target_bir_lowering=False)
    with ExitStack() as stack:
        ctx = Ctx(nc, stack)
        xin = nc.dram_tensor("xres_in", [DM, width], F32, kind="ExternalInput").ap()
        xout = nc.dram_tensor("xres_out", [DM, width], F32, kind="ExternalOutput").ap()
        X = {"in": (xin, [Buf(f"xin{p}") for p in range(NPT)]),
             "out": (xout, [Buf(f"xout{p}") for p in range(NPT)])}
        W = {}
        for name, shape, dt in weight_specs:
            ap = nc.dram_tensor(name, list(shape), dt, kind="ExternalInput").ap()
            W[name] = (ap, Buf(name))
        for name, shape, dt in internal_specs:
            ap = nc.dram_tensor(name, list(shape), dt, kind="Internal").ap()
            W[name] = (ap, Buf(name))
        for name, shape, dt in extra_outputs:
            ap = nc.dram_tensor(name, list(shape), dt, kind="ExternalOutput").ap()
            W[name] = (ap, Buf(name))
        consts = declare_consts(nc, ctx, stack)
        emit_fn(ctx, consts, X, W)
        ctx.barrier()
    return nc


def run_sublayer(nc, xres_in, weights, n_core=1):
    ins = dict(weights)
    ins["xres_in"] = np.ascontiguousarray(xres_in)
    for k, v in host_consts().items():
        ins[k] = v
    N_LAUNCH[0] += 1
    res = run_bass_kernel_spmd(nc, [ins] * n_core, core_ids=list(range(n_core)))
    return [r["xres_out"] for r in res.results]


def l0_ffn_nc(first=True, width=TT):
    def fn(ctx, consts, X, W):
        Wd = {"w_upT": W["w_upT"], "w_downT": W["w_downT"], "cu": W["l0f_cu"],
              "cd": W["l0f_cd"], "first": first}
        emit_l0_ffn(ctx, consts, X, Wd, width=width)
    return build_sublayer_nc(
        fn,
        [("w_upT", [1024, 4096], F32), ("w_downT", [4096, 1024], F32)],
        [("l0f_cu", [128, 8 * 4096], BF16), ("l0f_cd", [128, 32 * 1024], BF16)],
        width=width)


def attn64_core(ctx, consts, tag, q_dram, k_dram, v_dram, z_dram, heads, width,
                causal, local=False):
    nc = ctx.nc
    ntt = width // 128
    nqb = width // 512
    hstride = ntt * 64
    h65 = ntt * 65
    ones1x, ones1x_b = consts["c_ones1x"]
    onesbf, onesbf_b = consts["c_onesbf"]
    ident, ident_b = consts["c_ident"]
    identf, identf_b = consts["c_identf"]
    with ctx.scope():
        q2t = [ctx.sb(f"{tag}_q2t{i}", [128, width], BF16) for i in range(2)]
        q2l = [ctx.sb(f"{tag}_q2l{i}", [128, width], BF16) for i in range(2)]
        k2t = [ctx.sb(f"{tag}_k2t{i}", [128, width], BF16) for i in range(2)]
        k2l = [ctx.sb(f"{tag}_k2l{i}", [128, width], BF16) for i in range(2)]
        vt_p = [ctx.sb(f"{tag}_vt{i}", [128, h65], BF16) for i in range(4)]
        ptile = [ctx.sb(f"{tag}_pt{i}", [128, 512], BF16) for i in range(4)]
        mrow = ctx.sb(f"{tag}_mrow", [1, 1024], BF16)
        mcf = ctx.sb(f"{tag}_mcf", [128, 8], F32)
        mcb = ctx.sb(f"{tag}_mcb", [128, 8], F32)
        tmpc = ctx.sb(f"{tag}_tmpc", [128, 2], F32)
        rdt = ctx.sb(f"{tag}_rdt", [65, 512], F32)
        rdsb = ctx.sb(f"{tag}_rdsb", [64, 512], F32)
        zstg = [ctx.sb(f"{tag}_zstg{i}", [64, 512], BF16) for i in range(2)]
        ps_s = ctx.ps(f"{tag}_pss", [128, 2048], F32)
        ps_s_h = [Buf(f"{tag}_pssh{i}") for i in range(4)]
        ps_st = [ctx.ps(f"{tag}_psst{i}", [128, 512], F32) for i in range(2)]
        ps_pv = ctx.ps(f"{tag}_pspv", [65, 512], F32)
        ps_pv2 = ctx.ps(f"{tag}_pspv2", [65, 512], F32)
        ps_rd = ps_st[1]
        mdiag = mTs = mlow = mbTs = None
        if causal or local:
            mdiag = ctx.sb(f"{tag}_mdg", [128, 128], F32)
            mTs = [ctx.sb(f"{tag}_mT{r}", [128, 512], F32) for r in range(4)]
            cm, cm_b = consts["c_maskd"]
            ctx.dma("QS", mdiag[0][:], cm, reads=(cm_b,), writes=(mdiag[1],))
            cmT, cmT_b = consts["c_maskdT"]
            for r in range(4):
                ctx.dma("QS", mTs[r][0][:], cmT[:, r * 512:(r + 1) * 512],
                        reads=(cmT_b,), writes=(mTs[r][1],))
        if local:
            mlow = ctx.sb(f"{tag}_mlw", [128, 128], F32)
            cl, cl_b = consts["c_masklow"]
            ctx.dma("QS", mlow[0][:], cl, reads=(cl_b,), writes=(mlow[1],))
            mbTs = [ctx.sb(f"{tag}_mbT{r}", [128, 512], F32) for r in range(4)]
            cb, cb_b = consts["c_maskbT"]
            for r in range(4):
                ctx.dma("QS", mbTs[r][0][:], cb[:, r * 512:(r + 1) * 512],
                        reads=(cb_b,), writes=(mbTs[r][1],))
        lo0 = heads * 64

        for hp in range(heads // 2):
            qt_t, qt_b = q2t[hp % 2]
            ql_t, ql_b = q2l[hp % 2]
            kt_t, kt_b = k2t[hp % 2]
            kl_t, kl_b = k2l[hp % 2]
            vts = [vt_p[(2 * hp + u) % 4] for u in range(2)]
            for u in range(2):
                h = 2 * hp + u
                pb = slice(64 * u, 64 * u + 64)
                ctx.dma("QS", qt_t[pb, :], q_dram[0][h * 64:(h + 1) * 64, :],
                        reads=(q_dram[1],), writes=(qt_b,))
                ctx.dma("QS", ql_t[pb, :], q_dram[0][lo0 + h * 64:lo0 + (h + 1) * 64, :],
                        reads=(q_dram[1],), writes=(ql_b,))
                ctx.dma("QS", kt_t[pb, :], k_dram[0][h * 64:(h + 1) * 64, :],
                        reads=(k_dram[1],), writes=(kt_b,))
                ctx.dma("QS", kl_t[pb, :], k_dram[0][lo0 + h * 64:lo0 + (h + 1) * 64, :],
                        reads=(k_dram[1],), writes=(kl_b,))
                vt, vt_b = vts[u]
                ctx.emit("V", lambda E, vt=vt: E.memset(
                    vt[:].rearrange("p (c s) -> p c s", s=65)[:, :, 64:65], 1.0),
                    writes=(vt_b,))
                ctx.dma("QP", vt[:].rearrange("p (c s) -> p c s", s=65)[:, :, 0:64],
                        v_dram[0][:, h * hstride:(h + 1) * hstride].rearrange(
                            "p (c n) -> p c n", n=64),
                        reads=(v_dram[1],), writes=(vt_b,))
            for qb in range(nqb):
                for qt4 in range(4):
                    qt = qb * 4 + qt4
                    if local:
                        kj_lo = max(0, qt - 4) * 128
                        nk = 128 * (qt + 1)
                    else:
                        kj_lo = 0
                        nk = 128 * (qt + 1) if causal else width
                    first = True
                    for ri, k0 in enumerate(range(kj_lo, nk, 512)):
                        kw = min(512, nk - k0)
                        bk = ri % 2
                        pssA = ps_s[0][:, bk * 512:bk * 512 + 512]
                        pssB = ps_s[0][:, 1024 + bk * 512:1024 + bk * 512 + 512]
                        wbA, wbB = ps_s_h[bk], ps_s_h[2 + bk]

                        def fn(E, k0=k0, kw=kw, qt=qt, pssA=pssA, pssB=pssB):
                            E.matmul(pssA[:, 0:kw], qt_t[0:64, 128 * qt:128 * qt + 128],
                                     kt_t[0:64, k0:k0 + kw], start=True, stop=True)
                            return E.matmul(pssB[:, 0:kw],
                                            qt_t[64:128, 128 * qt:128 * qt + 128],
                                            kt_t[64:128, k0:k0 + kw],
                                            start=True, stop=True)

                        ctx.emit("T", fn, reads=(qt_b, kt_b), writes=(wbA, wbB))
                        for u, (pss, wb) in enumerate(((pssA, wbA), (pssB, wbB))):
                            if (causal or local) and k0 + kw == nk:
                                ctx.emit("V", lambda E, kw=kw, pss=pss: E.tensor_tensor(
                                    pss[:, kw - 128:kw], pss[:, kw - 128:kw],
                                    mdiag[0][:], ALU.add),
                                    reads=(wb, mdiag[1]), writes=(wb,))
                            if local and k0 == kj_lo and qt >= 4:
                                ctx.emit("V", lambda E, pss=pss: E.tensor_tensor(
                                    pss[:, 0:128], pss[:, 0:128],
                                    mlow[0][:], ALU.add),
                                    reads=(wb, mlow[1]), writes=(wb,))
                            ci = u * 4 + qt4
                            if first and u == 1:
                                first = False
                            if ri == 0:
                                ctx.emit("V", lambda E, kw=kw, ci=ci, pss=pss:
                                         E.reduce_max(
                                    mcf[0][:, ci:ci + 1], pss[:, 0:kw], axis=AX),
                                    reads=(wb,), writes=(mcf[1],))
                            else:
                                ctx.emit("V", lambda E, kw=kw, u=u, pss=pss:
                                         E.reduce_max(
                                    tmpc[0][:, u:u + 1], pss[:, 0:kw], axis=AX),
                                    reads=(wb,), writes=(tmpc[1],))
                                ctx.emit("V", lambda E, ci=ci, u=u: E.tensor_tensor(
                                    mcf[0][:, ci:ci + 1], mcf[0][:, ci:ci + 1],
                                    tmpc[0][:, u:u + 1], ALU.max),
                                    reads=(mcf[1], tmpc[1]), writes=(mcf[1],))
                ctx.emit("V", lambda E: E.tensor_scalar(mcb[0][:], mcf[0][:], -1.0,
                                                        None, ALU.mult),
                         reads=(mcf[1],), writes=(mcb[1],))
                for u in range(2):
                    for qt4 in range(4):
                        ci = u * 4 + qt4
                        ctx.emit("T", lambda E, ci=ci: E.transpose(
                            ps_s[0][0:1, 1920:2048], mcb[0][:, ci:ci + 1], identf[:, :]),
                            reads=(mcb[1], identf_b), writes=(ps_s_h[3],))
                        ctx.emit("V", lambda E, u=u, qt4=qt4: E.tensor_copy(
                            mrow[0][0:1, u * 512 + qt4 * 128:u * 512 + (qt4 + 1) * 128],
                            ps_s[0][0:1, 1920:2048]),
                            reads=(ps_s_h[3],), writes=(mrow[1],))
                if local:
                    kc0 = max(0, 4 * qb - 4)
                    kc1 = 4 * qb + 4
                else:
                    kc0 = 0
                    kc1 = 4 * (qb + 1) if causal else ntt
                kcs = list(range(kc0, kc1))
                qsl = slice(512 * qb, 512 * qb + 512)
                n2 = 2 * len(kcs)
                st_slots = [(ps_st[0][0], ps_st[0][1]), (ps_st[1][0], ps_st[1][1]),
                            (None, None), (None, None)]

                def st_slot(i2):
                    j = i2 % 4
                    if j < 2:
                        return ps_st[j][0][:, :], ps_st[j][1]
                    return ps_s[0][:, (j - 2) * 512:(j - 1) * 512], ps_s_h[j - 2]

                def emit_st(i2):
                    kc, u = kcs[i2 // 2], i2 % 2
                    pst, pst_b = st_slot(i2)
                    ksl = slice(128 * kc, 128 * kc + 128)
                    ph = slice(64 * u, 64 * u + 64)
                    pieces = [(ones1x[:, :], mrow[0][0:1, u * 512:u * 512 + 512]),
                              (kt_t[ph, ksl], qt_t[ph, qsl]),
                              (kt_t[ph, ksl], ql_t[ph, qsl]),
                              (kl_t[ph, ksl], qt_t[ph, qsl])]
                    mm_group(ctx, pst, pieces,
                             reads=(ones1x_b, mrow[1], qt_b, kt_b, ql_b, kl_b),
                             wbuf=pst_b)
                    if local:
                        rr = kc - (4 * qb - 4)
                        mt = mbTs[rr] if rr < 4 else mTs[rr - 4]
                        ctx.emit("V", lambda E, pst=pst, mt=mt: E.tensor_tensor(
                            pst, pst, mt[0][:], ALU.add),
                            reads=(pst_b, mt[1]), writes=(pst_b,))
                    elif causal and kc >= 4 * qb:
                        r = kc - 4 * qb
                        ctx.emit("V", lambda E, pst=pst, r=r: E.tensor_tensor(
                            pst, pst, mTs[r][0][:], ALU.add),
                            reads=(pst_b, mTs[r][1]), writes=(pst_b,))

                def emit_exp_pv(i2):
                    kc, u = kcs[i2 // 2], i2 % 2
                    ki = i2 // 2
                    pst, pst_b = st_slot(i2)
                    pt, pt_b = ptile[i2 % 4]
                    ctx.emit("S", lambda E, pt=pt, pst=pst: E.activation(
                        pt[:], pst, AF.Exp),
                        reads=(pst_b,), writes=(pt_b,))
                    vt, vt_b = vts[u]
                    pvdst = ps_pv[0][:, :] if u == 0 else ps_pv2[0][:, :]
                    pvb = ps_pv[1] if u == 0 else ps_pv2[1]
                    ctx.emit("T", lambda E, pt=pt, kc=kc, ki=ki, vt=vt, pvdst=pvdst:
                             E.matmul(pvdst, vt[:, kc * 65:(kc + 1) * 65], pt[:],
                                      start=(ki == 0), stop=(ki == len(kcs) - 1)),
                             reads=(pt_b, vt_b), writes=(pvb,))

                for i2 in range(n2):
                    emit_st(i2)
                    if i2 >= 2:
                        emit_exp_pv(i2 - 2)
                for i2 in range(max(0, n2 - 2), n2):
                    emit_exp_pv(i2)
                for u in range(2):
                    h = 2 * hp + u
                    pv = ps_pv if u == 0 else ps_pv2
                    ctx.emit("V", lambda E, pv=pv: E.reciprocal(
                        rdt[0][64:65, :], pv[0][64:65, :]),
                        reads=(pv[1],), writes=(rdt[1],))
                    ctx.emit("T", lambda E: E.matmul(
                        ps_rd[0][0:64, :], onesbf[64:65, 0:64], rdt[0][64:65, :],
                        start=True, stop=True),
                        reads=(rdt[1], onesbf_b), writes=(ps_rd[1],))
                    ctx.emit("V", lambda E: E.tensor_copy(rdsb[0][:], ps_rd[0][0:64, :]),
                             reads=(ps_rd[1],), writes=(rdsb[1],))
                    zs, zs_b = zstg[u]
                    ctx.emit("V", lambda E, zs=zs, pv=pv: E.tensor_tensor(
                        zs[:], pv[0][0:64, :], rdsb[0][:], ALU.mult),
                        reads=(pv[1], rdsb[1]), writes=(zs_b,))
                    ctx.dma("QS", z_dram[0][h * 64:(h + 1) * 64, qb * 512:(qb + 1) * 512],
                            zs[:], reads=(zs_b,), writes=(z_dram[1],))


def fm_linear(ctx, w_tiles, x_tiles, n_k, obs, tcs, ps_pairs, evict,
              extra_pieces=None, ob_begin=None, ob_end=None):
    i = 0
    for ob in obs:
        if ob_begin is not None:
            ob_begin(ob)
        for tc in tcs:
            ps_t, ps_b = ps_pairs[i % len(ps_pairs)]
            i += 1
            pieces = [(w_tiles[kc][0][:, ob * 128:(ob + 1) * 128],
                       x_tiles[kc][0][:, tc * 512:(tc + 1) * 512])
                      for kc in range(n_k)]
            reads = tuple(w_tiles[kc][1] for kc in range(n_k)) + \
                tuple(x_tiles[kc][1] for kc in range(n_k))
            if extra_pieces is not None:
                ep, eb = extra_pieces(tc)
                pieces += ep
                reads += tuple(eb)
            mm_group(ctx, ps_t[:, :], pieces, reads=reads, wbuf=ps_b)
            evict(ob, tc, ps_t, ps_b)
        if ob_end is not None:
            ob_end(ob)


def emit_l1_attn(ctx, consts, X, W, width=TT):
    nc = ctx.nc
    xin_ap, xin_bufs = X["in"]
    xout_ap, xout_bufs = X["out"]
    ntc = width // 512
    ntt = width // 128
    heads = 16
    hstride = ntt * 64
    ones1x, ones1x_b = consts["c_ones1x"]
    blk2, blk2_b = consts["c_blk2f"]
    blk2T, blk2T_b = consts["c_blk2Tf"]
    with ctx.scope():
      with ctx.scope():
        xn = [ctx.sb(f"l1a_xn{p}", [128, width], BF16) for p in range(NPT)]
        xl = [ctx.sb(f"l1a_xl{p}", [128, width], BF16) for p in range(NPT)]
        with ctx.scope():
            sc = ln_scratch(ctx, "l1a_ln", width)
            ln_produce(ctx, consts, sc, lambda pt: xin_ap[pt * 128:(pt + 1) * 128, :],
                       xin_bufs, xn, xn, width=width, xlo_tiles=xl)
        with ctx.scope():
            wqk = [ctx.sb(f"l1a_wqk{p}", [128, 1024], BF16) for p in range(NPT)]
            wql = [ctx.sb(f"l1a_wql{p}", [128, 1024], BF16) for p in range(NPT)]
            wfs = [ctx.sb(f"l1a_wfs{i}", [128, 1024], F32) for i in range(2)]

            def load_sec_w(sec):
                for p in range(NPT):
                    csl = slice(sec * 1024, (sec + 1) * 1024)
                    ctx.dma("QP", wqk[p][0][:], W["wqkvT"][0][p * 128:(p + 1) * 128, csl],
                            reads=(W["wqkvT"][1],), writes=(wqk[p][1],))
                    wf, wf_b = wfs[p % 2]
                    ctx.dma("QS", wf[:], W["wqkvT"][0][p * 128:(p + 1) * 128, csl],
                            reads=(W["wqkvT"][1],), writes=(wf_b,))
                    ctx.emit("V", lambda E, p=p, wf=wf: E.tensor_tensor(
                        wql[p][0][:], wf[:], wqk[p][0][:], ALU.subtract),
                        reads=(wf_b, wqk[p][1]), writes=(wql[p][1],))
            bqk = ctx.sb("l1a_bqk", [128, 16], F32)
            ctx.dma("QS", bqk[0][:], W["bqk_cols"][0], reads=(W["bqk_cols"][1],),
                    writes=(bqk[1],))
            wqwk = ctx.sb("l1a_wqwk", [128, 2], F32)
            ctx.dma("QS", wqwk[0][:], W["wqwk128"][0], reads=(W["wqwk128"][1],),
                    writes=(wqwk[1],))
            scol = ctx.sb("l1a_scol", [128, 1], F32)
            ctx.emit("V", lambda E: E.tensor_tensor(scol[0][:], wqwk[0][:, 0:1],
                                                    wqwk[0][:, 1:2], ALU.mult),
                     reads=(wqwk[1],), writes=(scol[1],))
            qf = [ctx.sb(f"l1a_qf{i}", [128, 512], F32) for i in range(3)]
            qlo_st = [ctx.sb(f"l1a_qlo{i}", [128, 512], BF16) for i in range(3)]
            q2 = [ctx.sb(f"l1a_q2{i}", [128, 512], F32) for i in range(3)]
            qo = [ctx.sb(f"l1a_qo{i}", [128, 512], BF16) for i in range(3)]
            rr = ctx.sb("l1a_rr", [2, 512], F32)
            psm = [ctx.ps(f"l1a_mm{i}", [128, 512]) for i in range(3)]
            ps2 = ctx.ps("l1a_ps2", [2, 512])
            psb = ctx.ps("l1a_psb", [128, 512])

            def qk_evict(sec):
                def evict(ob, tc, gi, ps_t, ps_b):
                    f, f_b = qf[gi % 3]
                    sq, sq_b = q2[gi % 3]
                    bcol = bqk[0][:, sec * 8 + ob:sec * 8 + ob + 1]
                    ctx.emit("V", lambda E: E.tensor_scalar(f[:], ps_t[:, :], bcol,
                                                            None, ALU.add),
                             reads=(ps_b, bqk[1]), writes=(f_b,))
                    ctx.emit("S", lambda E: E.activation(sq[:], f[:], AF.Square),
                             reads=(f_b,), writes=(sq_b,))
                    mm_group(ctx, ps2[0][:, :], [(blk2[:, :], sq[:])],
                             reads=(blk2_b, sq_b), wbuf=ps2[1])
                    ctx.emit("V", lambda E: E.tensor_scalar(rr[0][:, :], ps2[0][:, :],
                                                            1.0 / 64, 1e-6, ALU.mult, ALU.add),
                             reads=(ps2[1],), writes=(rr[1],))
                    ctx.emit("S", lambda E: E.activation(rr[0][:, :], rr[0][:, :], AF.Ln),
                             reads=(rr[1],), writes=(rr[1],))
                    ctx.emit("S", lambda E: E.activation(rr[0][:, :], rr[0][:, :],
                                                         AF.Exp, scale=-0.5),
                             reads=(rr[1],), writes=(rr[1],))
                    mm_group(ctx, psb[0][:, :], [(blk2T[:, :], rr[0][:, :])],
                             reads=(blk2T_b, rr[1]), wbuf=psb[1])
                    ctx.emit("V", lambda E: E.tensor_tensor(f[:], f[:], psb[0][:, :],
                                                            ALU.mult),
                             reads=(f_b, psb[1]), writes=(f_b,))
                    zs, zs_b = qo[gi % 3]
                    zl, zl_b = qlo_st[gi % 3]
                    if sec == 0:
                        ctx.emit("V", lambda E: E.tensor_scalar(
                            f[:], f[:], scol[0][:, 0:1], None, ALU.mult),
                            reads=(f_b, scol[1]), writes=(f_b,))
                    ctx.emit("V", lambda E: E.tensor_copy(zs[:], f[:]),
                             reads=(f_b,), writes=(zs_b,))
                    ctx.emit("V", lambda E: E.tensor_tensor(zl[:], f[:], zs[:], ALU.subtract),
                             reads=(f_b, zs_b), writes=(zl_b,))
                    dram = W["q_sc"] if sec == 0 else W["k_sc"]
                    for half in range(2):
                        h = ob * 2 + half
                        ctx.dma("QS", dram[0][h * 64:(h + 1) * 64, tc * 512:(tc + 1) * 512],
                                zs[64 * half:64 * half + 64, :], reads=(zs_b,),
                                writes=(dram[1],))
                        ctx.dma("QS", dram[0][1024 + h * 64:1024 + (h + 1) * 64,
                                              tc * 512:(tc + 1) * 512],
                                zl[64 * half:64 * half + 64, :], reads=(zl_b,),
                                writes=(dram[1],))
                return evict

            for sec in range(2):
                load_sec_w(sec)
                ev = qk_evict(sec)
                from collections import deque
                pend = deque()
                gi = 0
                for ob in range(8):
                    col = slice(ob * 128, ob * 128 + 128)
                    for tc in range(ntc):
                        ps_t, ps_b = psm[gi % 3]
                        tsl = slice(tc * 512, (tc + 1) * 512)
                        pieces = []
                        reads = []
                        for kc in range(NPT):
                            pieces += [(wqk[kc][0][:, col], xn[kc][0][:, tsl]),
                                       (wqk[kc][0][:, col], xl[kc][0][:, tsl]),
                                       (wql[kc][0][:, col], xn[kc][0][:, tsl])]
                            reads += [wqk[kc][1], wql[kc][1], xn[kc][1], xl[kc][1]]
                        mm_group(ctx, ps_t[:, :], pieces, reads=tuple(reads), wbuf=ps_b)
                        pend.append((ob, tc, gi, ps_t, ps_b))
                        gi += 1
                        if len(pend) > 2:
                            ev(*pend.popleft())
                while pend:
                    ev(*pend.popleft())
        with ctx.scope():
            wv = [ctx.sb(f"l1a_wv{p}", [128, 1024], BF16) for p in range(NPT)]
            for p in range(NPT):
                ctx.dma("QP", wv[p][0][:], W["wqkvT"][0][p * 128:(p + 1) * 128, 2048:3072],
                        reads=(W["wqkvT"][1],), writes=(wv[p][1],))
            bvr = ctx.sb("l1a_bvr", [1, 1024], BF16)
            ctx.dma("QP", bvr[0][:], W["bv_row"][0], reads=(W["bv_row"][1],),
                    writes=(bvr[1],))
            vstg = [ctx.sb(f"l1a_vs{i}", [128, 512], BF16) for i in range(3)]
            psm = [ctx.ps(f"l1a_vmm{i}", [128, 512]) for i in range(2)]
            for tb in range(ntt):
                for fb in range(2):
                    ps_t, ps_b = psm[fb % 2]
                    pieces = [(xn[kc][0][:, tb * 128:(tb + 1) * 128],
                               wv[kc][0][:, fb * 512:(fb + 1) * 512])
                              for kc in range(NPT)]
                    pieces.append((ones1x[:, :], bvr[0][0:1, fb * 512:(fb + 1) * 512]))
                    reads = tuple(xn[kc][1] for kc in range(NPT)) + \
                        tuple(wv[kc][1] for kc in range(NPT)) + (ones1x_b, bvr[1])
                    mm_group(ctx, ps_t[:, :], pieces, reads=reads, wbuf=ps_b)
                    vs, vs_b = vstg[(tb * 2 + fb) % 3]
                    ctx.emit("V", lambda E, vs=vs, ps_t=ps_t: E.tensor_copy(vs[:], ps_t[:, :]),
                             reads=(ps_b,), writes=(vs_b,))
                    for hh in range(8):
                        h = fb * 8 + hh
                        ctx.dma("QS", W["v_sc"][0][:, h * hstride + tb * 64:
                                                   h * hstride + (tb + 1) * 64],
                                vs[:, hh * 64:(hh + 1) * 64],
                                reads=(vs_b,), writes=(W["v_sc"][1],))
      attn64_core(ctx, consts, "l1a", W["q_sc"], W["k_sc"], W["v_sc"], W["z_sc"],
                  heads, width, causal=False)
      with ctx.scope():
          zT = [ctx.sb(f"l1a_zT{p}", [128, width], BF16) for p in range(NPT)]
          for p in range(NPT):
              for half in range(2):
                  ctx.dma("QS", zT[p][0][64 * half:64 * half + 64, :],
                          W["z_sc"][0][(2 * p + half) * 64:(2 * p + half + 1) * 64, :],
                          reads=(W["z_sc"][1],), writes=(zT[p][1],))
          wo = [ctx.sb(f"l1a_wo{p}", [128, 1024], BF16) for p in range(NPT)]
          for p in range(NPT):
              ctx.dma("QP", wo[p][0][:], W["woT"][0][p * 128:(p + 1) * 128, :],
                      reads=(W["woT"][1],), writes=(wo[p][1],))
          bo = ctx.sb("l1a_bo", [128, 8], F32)
          ctx.dma("QS", bo[0][:], W["bo_cols"][0], reads=(W["bo_cols"][1],),
                  writes=(bo[1],))
          psm = [ctx.ps(f"l1a_omm{i}", [128, 512]) for i in range(4)]
          rp = ResidPt(ctx, "l1a", xin_ap, xin_bufs, xout_ap, xout_bufs, cw=width)

          def o_evict(ob, tc, ps_t, ps_b):
              dst, dst_b = rp.sl(ob, tc)
              ctx.emit("V", lambda E: E.scalar_tensor_tensor(
                  dst, ps_t[:, :], bo[0][:, ob:ob + 1], dst, ALU.add, ALU.add),
                  reads=(ps_b, bo[1], dst_b), writes=(dst_b,))

          fm_linear(ctx, wo, zT, NPT, range(NPT), range(ntc), psm, o_evict,
                    ob_begin=rp.begin, ob_end=rp.end)


def l1_attn_nc(width=TT):
    hstride = (width // 128) * 64

    def fn(ctx, consts, X, W):
        emit_l1_attn(ctx, consts, X, W, width=width)
    return build_sublayer_nc(
        fn,
        [("wqkvT", [1024, 3072], F32), ("bqk_cols", [128, 16], F32),
         ("bv_row", [1, 1024], F32), ("woT", [1024, 1024], F32),
         ("bo_cols", [128, 8], F32), ("wqwk128", [128, 2], F32)],
        [("q_sc", [2048, width], BF16), ("k_sc", [2048, width], BF16),
         ("v_sc", [128, 16 * hstride], BF16), ("z_sc", [1024, width], BF16)],
        width=width)


def prep_l1_attn(Wqkv, bqkv, Wo, bo, wq, wk):
    bqk = bqkv[0:2048].reshape(16, 128).T
    return {
        "wqkvT": np.ascontiguousarray(Wqkv.T),
        "bqk_cols": np.ascontiguousarray(bqk),
        "bv_row": np.ascontiguousarray(bqkv[2048:3072].reshape(1, 1024)),
        "woT": np.ascontiguousarray(Wo.T),
        "bo_cols": np.ascontiguousarray(bo.reshape(8, 128).T),
        "wqwk128": np.ascontiguousarray(
            np.stack([np.tile(wq, 2), np.tile(wk, 2)], axis=1)),
    }


def emit_l3_attn(ctx, consts, X, W, width=TT):
    nc = ctx.nc
    xin_ap, xin_bufs = X["in"]
    xout_ap, xout_bufs = X["out"]
    ntc = width // 512
    ntt = width // 128
    heads = 16
    hstride = ntt * 64
    ones1x, ones1x_b = consts["c_ones1x"]
    blk2, blk2_b = consts["c_blk2f"]
    blk2T, blk2T_b = consts["c_blk2Tf"]
    with ctx.scope():
      with ctx.scope():
        xn = [ctx.sb(f"l3a_xn{p}", [128, width], BF16) for p in range(NPT)]
        xl = [ctx.sb(f"l3a_xl{p}", [128, width], BF16) for p in range(NPT)]
        with ctx.scope():
            sc = ln_scratch(ctx, "l3a_ln", width)
            ln_produce(ctx, consts, sc, lambda pt: xin_ap[pt * 128:(pt + 1) * 128, :],
                       xin_bufs, xn, xn, width=width, xlo_tiles=xl)
        with ctx.scope():
            wqk = [ctx.sb(f"l3a_wqk{p}", [128, 1024], BF16) for p in range(NPT)]
            wql = [ctx.sb(f"l3a_wql{p}", [128, 1024], BF16) for p in range(NPT)]
            wfs = [ctx.sb(f"l3a_wfs{i}", [128, 1024], F32) for i in range(2)]

            def load_sec_w(sec):
                for p in range(NPT):
                    csl = slice(sec * 1024, (sec + 1) * 1024)
                    ctx.dma("QP", wqk[p][0][:], W["wqkvT"][0][p * 128:(p + 1) * 128, csl],
                            reads=(W["wqkvT"][1],), writes=(wqk[p][1],))
                    wf, wf_b = wfs[p % 2]
                    ctx.dma("QS", wf[:], W["wqkvT"][0][p * 128:(p + 1) * 128, csl],
                            reads=(W["wqkvT"][1],), writes=(wf_b,))
                    ctx.emit("V", lambda E, p=p, wf=wf: E.tensor_tensor(
                        wql[p][0][:], wf[:], wqk[p][0][:], ALU.subtract),
                        reads=(wf_b, wqk[p][1]), writes=(wql[p][1],))

            qsc = ctx.sb("l3a_qsc", [2, 16], F32)
            ctx.dma("QS", qsc[0][:], W["qk_scales"][0], reads=(W["qk_scales"][1],),
                    writes=(qsc[1],))
            qf = [ctx.sb(f"l3a_qf{i}", [128, 512], F32) for i in range(3)]
            qlo_st = [ctx.sb(f"l3a_qlo{i}", [128, 512], BF16) for i in range(3)]
            q2 = [ctx.sb(f"l3a_q2{i}", [128, 512], F32) for i in range(3)]
            qo = [ctx.sb(f"l3a_qo{i}", [128, 512], BF16) for i in range(3)]
            rr = ctx.sb("l3a_rr", [2, 512], F32)
            rr2 = ctx.sb("l3a_rr2", [2, 512], F32)
            psm = [ctx.ps(f"l3a_mm{i}", [128, 512]) for i in range(3)]
            ps2 = ctx.ps("l3a_ps2", [2, 512])
            psb = ctx.ps("l3a_psb", [128, 512])

            def ev(sec, ob, tc, gi, ps_t, ps_b):
                f, f_b = qf[gi % 3]
                sq, sq_b = q2[gi % 3]
                ctx.emit("S", lambda E: E.activation(f[:], ps_t[:, :], AF.Silu),
                         reads=(ps_b,), writes=(f_b,))
                ctx.emit("S", lambda E: E.activation(sq[:], f[:], AF.Square),
                         reads=(f_b,), writes=(sq_b,))
                mm_group(ctx, ps2[0][:, :], [(blk2[:, :], sq[:])],
                         reads=(blk2_b, sq_b), wbuf=ps2[1])
                ctx.emit("S", lambda E: E.activation(rr[0][:, :], ps2[0][:, :], AF.Sqrt),
                         reads=(ps2[1],), writes=(rr[1],))
                ctx.emit("V", lambda E: E.tensor_scalar(rr2[0][:, :], rr[0][:, :],
                                                        1e-6, None, ALU.add),
                         reads=(rr[1],), writes=(rr2[1],))
                ctx.emit("V", lambda E: E.reciprocal(rr2[0][:, :], rr2[0][:, :]),
                         reads=(rr2[1],), writes=(rr2[1],))
                ctx.emit("V", lambda E: E.tensor_tensor(rr[0][:, :], rr[0][:, :],
                                                        rr2[0][:, :], ALU.mult),
                         reads=(rr[1], rr2[1]), writes=(rr[1],))
                ctx.emit("V", lambda E, sec=sec, ob=ob: E.tensor_scalar(
                    rr[0][:, :], rr[0][:, :], qsc[0][:, sec * 8 + ob:sec * 8 + ob + 1],
                    None, ALU.mult),
                    reads=(rr[1], qsc[1]), writes=(rr[1],))
                mm_group(ctx, psb[0][:, :], [(blk2T[:, :], rr[0][:, :])],
                         reads=(blk2T_b, rr[1]), wbuf=psb[1])
                ctx.emit("V", lambda E: E.tensor_tensor(f[:], f[:], psb[0][:, :], ALU.mult),
                         reads=(f_b, psb[1]), writes=(f_b,))
                zs, zs_b = qo[gi % 3]
                zl, zl_b = qlo_st[gi % 3]
                ctx.emit("V", lambda E: E.tensor_copy(zs[:], f[:]),
                         reads=(f_b,), writes=(zs_b,))
                ctx.emit("V", lambda E: E.tensor_tensor(zl[:], f[:], zs[:], ALU.subtract),
                         reads=(f_b, zs_b), writes=(zl_b,))
                dram = W["q_sc"] if sec == 0 else W["k_sc"]
                for half in range(2):
                    h = ob * 2 + half
                    ctx.dma("QS", dram[0][h * 64:(h + 1) * 64, tc * 512:(tc + 1) * 512],
                            zs[64 * half:64 * half + 64, :], reads=(zs_b,),
                            writes=(dram[1],))
                    ctx.dma("QS", dram[0][1024 + h * 64:1024 + (h + 1) * 64,
                                          tc * 512:(tc + 1) * 512],
                            zl[64 * half:64 * half + 64, :], reads=(zl_b,),
                            writes=(dram[1],))

            for sec in range(2):
                load_sec_w(sec)
                from collections import deque
                pend = deque()
                gi = 0
                for ob in range(8):
                    col = slice(ob * 128, ob * 128 + 128)
                    for tc in range(ntc):
                        ps_t, ps_b = psm[gi % 3]
                        tsl = slice(tc * 512, (tc + 1) * 512)
                        pieces = []
                        reads = []
                        for kc in range(NPT):
                            pieces += [(wqk[kc][0][:, col], xn[kc][0][:, tsl]),
                                       (wqk[kc][0][:, col], xl[kc][0][:, tsl]),
                                       (wql[kc][0][:, col], xn[kc][0][:, tsl])]
                            reads += [wqk[kc][1], wql[kc][1], xn[kc][1], xl[kc][1]]
                        mm_group(ctx, ps_t[:, :], pieces, reads=tuple(reads), wbuf=ps_b)
                        pend.append((ob, tc, gi, ps_t, ps_b))
                        gi += 1
                        if len(pend) > 2:
                            ev(sec, *pend.popleft())
                while pend:
                    ev(sec, *pend.popleft())
        with ctx.scope():
            wv = [ctx.sb(f"l3a_wv{p}", [128, 1024], BF16) for p in range(NPT)]
            for p in range(NPT):
                ctx.dma("QP", wv[p][0][:], W["wqkvT"][0][p * 128:(p + 1) * 128, 2048:3072],
                        reads=(W["wqkvT"][1],), writes=(wv[p][1],))
            vstg = [ctx.sb(f"l3a_vs{i}", [128, 512], BF16) for i in range(3)]
            psm = [ctx.ps(f"l3a_vmm{i}", [128, 512]) for i in range(2)]
            for tb in range(ntt):
                for fb in range(2):
                    ps_t, ps_b = psm[fb % 2]
                    pieces = [(xn[kc][0][:, tb * 128:(tb + 1) * 128],
                               wv[kc][0][:, fb * 512:(fb + 1) * 512])
                              for kc in range(NPT)]
                    reads = tuple(xn[kc][1] for kc in range(NPT)) + \
                        tuple(wv[kc][1] for kc in range(NPT))
                    mm_group(ctx, ps_t[:, :], pieces, reads=reads, wbuf=ps_b)
                    vs, vs_b = vstg[(tb * 2 + fb) % 3]
                    ctx.emit("S", lambda E, vs=vs, ps_t=ps_t: E.activation(
                        vs[:], ps_t[:, :], AF.Silu),
                        reads=(ps_b,), writes=(vs_b,))
                    for hh in range(8):
                        h = fb * 8 + hh
                        ctx.dma("QS", W["v_sc"][0][:, h * hstride + tb * 64:
                                                   h * hstride + (tb + 1) * 64],
                                vs[:, hh * 64:(hh + 1) * 64],
                                reads=(vs_b,), writes=(W["v_sc"][1],))
      attn64_core(ctx, consts, "l3a", W["q_sc"], W["k_sc"], W["v_sc"], W["z_sc"],
                  heads, width, causal=True)
      with ctx.scope():
          zT = [ctx.sb(f"l3a_zT{p}", [128, width], BF16) for p in range(NPT)]
          for p in range(NPT):
              for half in range(2):
                  ctx.dma("QS", zT[p][0][64 * half:64 * half + 64, :],
                          W["z_sc"][0][(2 * p + half) * 64:(2 * p + half + 1) * 64, :],
                          reads=(W["z_sc"][1],), writes=(zT[p][1],))
          wo = [ctx.sb(f"l3a_wo{p}", [128, 1024], BF16) for p in range(NPT)]
          for p in range(NPT):
              ctx.dma("QP", wo[p][0][:], W["woT"][0][p * 128:(p + 1) * 128, :],
                      reads=(W["woT"][1],), writes=(wo[p][1],))
          psm = [ctx.ps(f"l3a_omm{i}", [128, 512]) for i in range(4)]
          rp = ResidPt(ctx, "l3a", xin_ap, xin_bufs, xout_ap, xout_bufs, cw=width)

          def o_evict(ob, tc, ps_t, ps_b):
              dst, dst_b = rp.sl(ob, tc)
              ctx.emit("V", lambda E: E.tensor_tensor(dst, ps_t[:, :], dst, ALU.add),
                       reads=(ps_b, dst_b), writes=(dst_b,))

          fm_linear(ctx, wo, zT, NPT, range(NPT), range(ntc), psm, o_evict,
                    ob_begin=rp.begin, ob_end=rp.end)


def l3_attn_nc(width=TT):
    hstride = (width // 128) * 64

    def fn(ctx, consts, X, W):
        emit_l3_attn(ctx, consts, X, W, width=width)
    return build_sublayer_nc(
        fn,
        [("wqkvT", [1024, 3072], F32), ("woT", [1024, 1024], F32),
         ("qk_scales", [2, 16], F32)],
        [("q_sc", [2048, width], BF16), ("k_sc", [2048, width], BF16),
         ("v_sc", [128, 16 * hstride], BF16), ("z_sc", [1024, width], BF16)],
        width=width)


def prep_l3_attn(Wq, Wk, Wv, Wo, q_scale, k_scale):
    wqkvT = np.concatenate([Wq.T, Wk.T, Wv.T], axis=1)
    qs = np.stack([q_scale.reshape(8, 2).T, k_scale.reshape(8, 2).T], axis=0)
    return {
        "wqkvT": np.ascontiguousarray(wqkvT),
        "woT": np.ascontiguousarray(Wo.T),
        "qk_scales": np.ascontiguousarray(qs.transpose(1, 0, 2).reshape(2, 16)),
    }


def emit_l0_attn(ctx, consts, X, W, width=TT):
    nc = ctx.nc
    xin_ap, xin_bufs = X["in"]
    xout_ap, xout_bufs = X["out"]
    ntc = width // 512
    ntt = width // 128
    ones1x, ones1x_b = consts["c_ones1x"]
    ones1xf, ones1xf_b = consts["c_ones1xf"]
    onesk, onesk_b = consts["c_onesk"]
    ident, ident_b = consts["c_ident"]
    with ctx.scope():
        sT = [ctx.sb(f"l0a_sT{p}", [128, width], BF16) for p in range(NPT)]
        gib = ctx.sb("l0a_gib", [128, 8], F32)
        btc = ctx.sb("l0a_btc", [128, 1], F32)
        ctx.dma("QS", btc[0][:], W["beta128"][0], reads=(W["beta128"][1],),
                writes=(btc[1],))
        with ctx.scope():
            xn = [ctx.sb(f"l0a_xn{p}", [128, width], BF16) for p in range(NPT)]
            with ctx.scope():
                sc = ln_scratch(ctx, "l0a_ln", width)
                ln_produce(ctx, consts, sc, lambda pt: xin_ap[pt * 128:(pt + 1) * 128, :],
                           xin_bufs, xn, xn, width=width)
            with ctx.scope():
                ws = [ctx.sb(f"l0a_ws{p}", [128, 1024], BF16) for p in range(NPT)]
                for p in range(NPT):
                    ctx.dma("QP", ws[p][0][:], W["wspinT"][0][p * 128:(p + 1) * 128, :],
                            reads=(W["wspinT"][1],), writes=(ws[p][1],))
                bsc = ctx.sb("l0a_bsc", [128, 8], F32)
                ctx.dma("QS", bsc[0][:], W["bspin_cols"][0], reads=(W["bspin_cols"][1],),
                        writes=(bsc[1],))
                psm = [ctx.ps(f"l0a_mm{i}", [128, 512]) for i in range(2)]

                def s_evict(ob, tc, ps_t, ps_b):
                    ctx.emit("V", lambda E, ob=ob, tc=tc: E.tensor_scalar(
                        sT[ob][0][:, tc * 512:(tc + 1) * 512], ps_t[:, :],
                        bsc[0][:, ob:ob + 1], None, ALU.add),
                        reads=(ps_b, bsc[1]), writes=(sT[ob][1],))

                fm_linear(ctx, ws, xn, NPT, range(NPT), range(ntc), psm, s_evict)
        stm = [ctx.sb(f"l0a_stm{t}", [128, 1024], BF16) for t in range(ntt)]
        with ctx.scope():
            psc = [ctx.ps(f"l0a_tc{i}", [128, 128], BF16) for i in range(2)]
            for pt in range(NPT):
                for tb in range(ntt):
                    pc, pc_b = psc[tb % 2]
                    ctx.emit("T", lambda E, pc=pc, pt=pt, tb=tb: E.transpose(
                        pc[:, :], sT[pt][0][:, tb * 128:(tb + 1) * 128], ident[:, :]),
                        reads=(sT[pt][1], ident_b), writes=(pc_b,))
                    ctx.emit("V", lambda E, pc=pc, pt=pt, tb=tb: E.tensor_copy(
                        stm[tb][0][:, pt * 128:(pt + 1) * 128], pc[:, :]),
                        reads=(pc_b,), writes=(stm[tb][1],))
        with ctx.scope():
            wg = [ctx.sb(f"l0a_wg{p}", [128, 1024], BF16) for p in range(NPT)]
            for p in range(NPT):
                ctx.dma("QP", wg[p][0][:], W["wglobT"][0][p * 128:(p + 1) * 128, :],
                        reads=(W["wglobT"][1],), writes=(wg[p][1],))
            mfc = ctx.sb("l0a_mfc", [128, 8], BF16)
            mff = ctx.sb("l0a_mff", [128, 1], F32)
            psg = ctx.ps("l0a_psg", [128, 1], F32)
            for p in range(NPT):
                ctx.emit("V", lambda E, p=p: E.reduce_sum(mff[0][:], sT[p][0][:], axis=AX),
                         reads=(sT[p][1],), writes=(mff[1],))
                ctx.emit("V", lambda E, p=p: E.tensor_scalar(
                    mfc[0][:, p:p + 1], mff[0][:], 1.0 / width, None, ALU.mult),
                    reads=(mff[1],), writes=(mfc[1],))
            for ob in range(NPT):
                pieces = [(wg[kc][0][:, ob * 128:(ob + 1) * 128], mfc[0][:, kc:kc + 1])
                          for kc in range(NPT)]
                reads = tuple(wg[kc][1] for kc in range(NPT)) + (mfc[1],)
                mm_group(ctx, psg[0][:, :], pieces, reads=reads, wbuf=psg[1])
                ctx.emit("V", lambda E, ob=ob: E.tensor_tensor(
                    gib[0][:, ob:ob + 1], psg[0][:, :], btc[0][:], ALU.mult),
                    reads=(psg[1], btc[1]), writes=(gib[1],))
        with ctx.scope():
            wj = [ctx.sb(f"l0a_wj{p}", [128, 1024], BF16) for p in range(NPT)]
            for p in range(NPT):
                ctx.dma("QP", wj[p][0][:], W["wjT"][0][p * 128:(p + 1) * 128, :],
                        reads=(W["wjT"][1],), writes=(wj[p][1],))
            efT = [ctx.sb(f"l0a_ef{p}", [128, 512], BF16) for p in range(NPT)]
            ptile = [ctx.sb(f"l0a_pt{i}", [128, 512], BF16) for i in range(ntt)]
            mrow = ctx.sb("l0a_mrow", [1, 512], BF16)
            mcf = ctx.sb("l0a_mcf", [128, 4], F32)
            mcb = ctx.sb("l0a_mcb", [128, 4], BF16)
            tmpc = ctx.sb("l0a_tmpc", [128, 1], F32)
            rdt = ctx.sb("l0a_rdt", [1, 512], F32)
            rdsb = ctx.sb("l0a_rdsb", [128, 512], F32)
            t2s = [ctx.sb(f"l0a_t2{i}", [128, 512], F32) for i in range(2)]
            xsl = [ctx.sb(f"l0a_xsl{i}", [128, 512], F32) for i in range(3)]
            ps_s = ctx.ps("l0a_pss", [128, 2048], F32)
            ps_s_h = [ps_s[1], ps_s[1]]
            ps_st = [ctx.ps(f"l0a_psst{i}", [128, 512], F32) for i in range(2)]
            ps_pv = ctx.ps("l0a_pspv", [128, 512], F32)
            ps_m = ctx.ps("l0a_psm", [1, 128], BF16)

            for qb in range(ntc):
                for ob in range(NPT):
                    pst, pst_b = ps_st[ob % 2]
                    pieces = [(wj[kc][0][:, ob * 128:(ob + 1) * 128],
                               sT[kc][0][:, qb * 512:(qb + 1) * 512])
                              for kc in range(NPT)]
                    reads = tuple(wj[kc][1] for kc in range(NPT)) + \
                        tuple(sT[kc][1] for kc in range(NPT))
                    mm_group(ctx, pst[:, :], pieces, reads=reads, wbuf=pst_b)
                    ctx.emit("V", lambda E, ob=ob, pst=pst: E.tensor_scalar(
                        efT[ob][0][:], pst[:, :], 1.0 / 32.0, None, ALU.mult),
                        reads=(pst_b,), writes=(efT[ob][1],))
                for qt4 in range(4):
                    qt = qb * 4 + qt4
                    first = True
                    for k0 in range(0, width, 2048):
                        kw = min(2048, width - k0)

                        def fn(E, k0=k0, kw=kw, qt4=qt4):
                            ins = None
                            for j in range(0, kw, 512):
                                for kc in range(NPT):
                                    ins = E.matmul(
                                        ps_s[0][:, j:j + 512],
                                        efT[kc][0][:, 128 * qt4:128 * qt4 + 128],
                                        sT[kc][0][:, k0 + j:k0 + j + 512],
                                        start=(kc == 0), stop=(kc == NPT - 1))
                            return ins

                        ctx.emit("T", fn,
                                 reads=tuple(efT[kc][1] for kc in range(NPT)) +
                                 tuple(sT[kc][1] for kc in range(NPT)),
                                 writes=(ps_s[1],))
                        for j in range(0, kw, 512):
                            if first:
                                ctx.emit("V", lambda E, j=j, qt4=qt4: E.reduce_max(
                                    mcf[0][:, qt4:qt4 + 1], ps_s[0][:, j:j + 512], axis=AX),
                                    reads=(ps_s[1],), writes=(mcf[1],))
                                first = False
                            else:
                                ctx.emit("V", lambda E, j=j: E.reduce_max(
                                    tmpc[0][:], ps_s[0][:, j:j + 512], axis=AX),
                                    reads=(ps_s[1],), writes=(tmpc[1],))
                                ctx.emit("V", lambda E, qt4=qt4: E.tensor_tensor(
                                    mcf[0][:, qt4:qt4 + 1], mcf[0][:, qt4:qt4 + 1],
                                    tmpc[0][:], ALU.max),
                                    reads=(mcf[1], tmpc[1]), writes=(mcf[1],))
                ctx.emit("V", lambda E: E.tensor_scalar(mcb[0][:], mcf[0][:], -1.0,
                                                        None, ALU.mult),
                         reads=(mcf[1],), writes=(mcb[1],))
                for qt4 in range(4):
                    ctx.emit("T", lambda E, qt4=qt4: E.transpose(
                        ps_m[0][:, :], mcb[0][:, qt4:qt4 + 1], ident[:, :]),
                        reads=(mcb[1], ident_b), writes=(ps_m[1],))
                    ctx.emit("V", lambda E, qt4=qt4: E.tensor_copy(
                        mrow[0][0:1, qt4 * 128:(qt4 + 1) * 128], ps_m[0][0:1, :]),
                        reads=(ps_m[1],), writes=(mrow[1],))
                for mc in range(ntt):
                    pst, pst_b = ps_st[mc % 2]
                    pieces = [(ones1x[:, :], mrow[0][0:1, :])]
                    pieces += [(sT[kc][0][:, 128 * mc:128 * mc + 128],
                                efT[kc][0][:, 0:512])
                               for kc in range(NPT)]
                    reads = (ones1x_b, mrow[1]) + \
                        tuple(sT[kc][1] for kc in range(NPT)) + \
                        tuple(efT[kc][1] for kc in range(NPT))
                    mm_group(ctx, pst[:, :], pieces, reads=reads, wbuf=pst_b)
                    pt, pt_b = ptile[mc]
                    ctx.emit("S", lambda E, pt=pt, pst=pst: E.activation(
                        pt[:], pst[:, :], AF.Exp),
                        reads=(pst_b,), writes=(pt_b,))
                    ctx.emit("T", lambda E, pt=pt, mc=mc: E.matmul(
                        ps_s[0][0:1, 0:512], onesk[:, :], pt[:],
                        start=(mc == 0), stop=(mc == ntt - 1)),
                        reads=(pt_b, onesk_b), writes=(ps_s[1],))
                ctx.emit("V", lambda E: E.reciprocal(rdt[0][0:1, :], ps_s[0][0:1, 0:512]),
                         reads=(ps_s_h[0],), writes=(rdt[1],))
                ctx.emit("T", lambda E: E.matmul(ps_st[0][0][:, :], ones1xf[:, :],
                                                 rdt[0][0:1, :], start=True, stop=True),
                         reads=(rdt[1], ones1xf_b), writes=(ps_st[0][1],))
                ctx.emit("V", lambda E: E.tensor_copy(rdsb[0][:], ps_st[0][0][:, :]),
                         reads=(ps_st[0][1],), writes=(rdsb[1],))
                for db in range(NPT):
                    def fnpv(E, db=db):
                        ins = None
                        for mc in range(ntt):
                            ins = E.matmul(ps_pv[0][:, :],
                                           stm[mc][0][:, db * 128:(db + 1) * 128],
                                           ptile[mc][0][:],
                                           start=(mc == 0), stop=(mc == ntt - 1))
                        return ins

                    ctx.emit("T", fnpv,
                             reads=tuple(ptile[mc][1] for mc in range(ntt)) +
                             tuple(stm[mc][1] for mc in range(ntt)),
                             writes=(ps_pv[1],))
                    t2, t2_b = t2s[db % 2]
                    ctx.emit("V", lambda E, t2=t2: E.tensor_tensor(
                        t2[:], ps_pv[0][:, :], rdsb[0][:], ALU.mult),
                        reads=(ps_pv[1], rdsb[1]), writes=(t2_b,))
                    ctx.emit("V", lambda E, t2=t2, db=db, qb=qb: E.tensor_tensor(
                        t2[:], t2[:], sT[db][0][:, qb * 512:(qb + 1) * 512], ALU.add),
                        reads=(t2_b, sT[db][1]), writes=(t2_b,))

                    def prep(dst_ap, dst_b, t2=t2, t2_b=t2_b, db=db):
                        ts, ts_b = t2s[(db + 1) % 2]
                        ctx.emit("S", lambda E: E.activation(
                            ts[:], t2[:], AF.Tanh, bias=gib[0][:, db:db + 1],
                            scale=btc[0][:, 0:1]),
                            reads=(t2_b, gib[1], btc[1]), writes=(ts_b,))
                        ctx.emit("V", lambda E: E.tensor_tensor(dst_ap, ts[:], dst_ap,
                                                                ALU.add),
                                 reads=(ts_b, dst_b), writes=(dst_b,))

                    resid_evict(ctx, xin_ap, xin_bufs, xout_ap, xout_bufs,
                                xsl, db, qb, prep)


def l0_attn_nc(width=TT):
    def fn(ctx, consts, X, W):
        emit_l0_attn(ctx, consts, X, W, width=width)
    return build_sublayer_nc(
        fn,
        [("wspinT", [1024, 1024], F32), ("bspin_cols", [128, 8], F32),
         ("wglobT", [1024, 1024], F32),
         ("wjT", [1024, 1024], F32), ("beta128", [128, 1], F32)],
        [],
        width=width)


def prep_l0_attn(W_spin, b_spin, W_global, W_J, beta):
    return {
        "wspinT": np.ascontiguousarray(W_spin.T),
        "bspin_cols": np.ascontiguousarray(b_spin.reshape(8, 128).T),
        "wglobT": np.ascontiguousarray(W_global.T),
        "wjT": np.ascontiguousarray(W_J.T),
        "beta128": np.ascontiguousarray(np.broadcast_to(beta.reshape(1, 1), (128, 1))),
    }


def emit_l3_ffn(ctx, consts, X, W, width=TT):
    nc = ctx.nc
    xin_ap, xin_bufs = X["in"]
    xout_ap, xout_bufs = X["out"]
    ntc = width // 512
    ntt = width // 128
    NE = 8
    ones1x, ones1x_b = consts["c_ones1x"]
    ident, ident_b = consts["c_ident"]
    identf, identf_b = consts["c_identf"]
    with ctx.scope():
        gates = ctx.sb("moe_g", [128, NE * ntt], F32)
        with ctx.scope():
            xn = [ctx.sb(f"moe_xn{p}", [128, width], BF16) for p in range(NPT)]
            xl = [ctx.sb(f"moe_xl{p}", [128, width], BF16) for p in range(NPT)]
            with ctx.scope():
                sc = ln_scratch(ctx, "moe_ln", width)
                ln_produce(ctx, consts, sc, lambda pt: xin_ap[pt * 128:(pt + 1) * 128, :],
                           xin_bufs, xn, xn, width=width, xlo_tiles=xl)
            for p in range(NPT):
                ctx.dma("QS", W["xn_sc"][0][p * 128:(p + 1) * 128, :], xn[p][0][:],
                        reads=(xn[p][1],), writes=(W["xn_sc"][1],))
            wg = ctx.sb("moe_wg", [128, NPT * 8], BF16)
            wgl = ctx.sb("moe_wgl", [128, NPT * 8], BF16)
            wgf = ctx.sb("moe_wgf", [128, NPT * 8], F32)
            ctx.dma("QP", wg[0][:], W["wgT"][0], reads=(W["wgT"][1],), writes=(wg[1],))
            ctx.dma("QS", wgf[0][:], W["wgT"][0], reads=(W["wgT"][1],), writes=(wgf[1],))
            ctx.emit("V", lambda E: E.tensor_tensor(wgl[0][:], wgf[0][:], wg[0][:],
                                                    ALU.subtract),
                     reads=(wgf[1], wg[1]), writes=(wgl[1],))
            lg = ctx.sb("moe_lg", [128, 8], F32)
            lgg = ctx.sb("moe_lgg", [128, 8], F32)
            m1 = ctx.sb("moe_m1", [128, 1], F32)
            m2 = ctx.sb("moe_m2", [128, 1], F32)
            w1c = ctx.sb("moe_w1c", [128, 1], F32)
            msk1 = ctx.sb("moe_msk1", [128, 8], F32)
            msk2 = ctx.sb("moe_msk2", [128, 8], F32)
            psg = ctx.ps("moe_psg", [128, 8], F32)
            for tb in range(ntt):
                tsl = slice(tb * 128, (tb + 1) * 128)
                pieces = []
                reads = []
                for kc in range(NPT):
                    ksl = slice(kc * 8, (kc + 1) * 8)
                    pieces += [(xn[kc][0][:, tsl], wg[0][:, ksl]),
                               (xl[kc][0][:, tsl], wg[0][:, ksl]),
                               (xn[kc][0][:, tsl], wgl[0][:, ksl])]
                    reads += [xn[kc][1], xl[kc][1], wg[1], wgl[1]]
                mm_group(ctx, psg[0][:, :], pieces, reads=tuple(reads), wbuf=psg[1])
                ctx.emit("V", lambda E: E.tensor_copy(lg[0][:], psg[0][:, :]),
                         reads=(psg[1],), writes=(lg[1],))
                ctx.emit("V", lambda E: E.reduce_max(m1[0][:], lg[0][:], axis=AX),
                         reads=(lg[1],), writes=(m1[1],))
                ctx.emit("V", lambda E: E.tensor_scalar(msk1[0][:], lg[0][:], m1[0][:],
                                                        None, ALU.is_ge),
                         reads=(lg[1], m1[1]), writes=(msk1[1],))
                ctx.emit("V", lambda E: E.scalar_tensor_tensor(lgg[0][:], msk1[0][:],
                                                               -1e30, lg[0][:],
                                                               ALU.mult, ALU.add),
                         reads=(msk1[1], lg[1]), writes=(lgg[1],))
                ctx.emit("V", lambda E: E.reduce_max(m2[0][:], lgg[0][:], axis=AX),
                         reads=(lgg[1],), writes=(m2[1],))
                ctx.emit("V", lambda E: E.tensor_scalar(msk2[0][:], lgg[0][:], m2[0][:],
                                                        None, ALU.is_ge),
                         reads=(lgg[1], m2[1]), writes=(msk2[1],))
                ctx.emit("V", lambda E: E.tensor_tensor(w1c[0][:], m1[0][:], m2[0][:],
                                                        ALU.subtract),
                         reads=(m1[1], m2[1]), writes=(w1c[1],))
                ctx.emit("S", lambda E: E.activation(w1c[0][:], w1c[0][:], AF.Sigmoid),
                         reads=(w1c[1],), writes=(w1c[1],))
                ctx.emit("V", lambda E: E.tensor_scalar(msk1[0][:], msk1[0][:],
                                                        w1c[0][:], None, ALU.mult),
                         reads=(msk1[1], w1c[1]), writes=(msk1[1],))
                ctx.emit("V", lambda E: E.tensor_scalar(w1c[0][:], w1c[0][:], -1.0, 1.0,
                                                        ALU.mult, ALU.add),
                         reads=(w1c[1],), writes=(w1c[1],))
                ctx.emit("V", lambda E: E.tensor_scalar(msk2[0][:], msk2[0][:],
                                                        w1c[0][:], None, ALU.mult),
                         reads=(msk2[1], w1c[1]), writes=(msk2[1],))
                ctx.emit("V", lambda E: E.tensor_tensor(msk1[0][:], msk1[0][:],
                                                        msk2[0][:], ALU.add),
                         reads=(msk1[1], msk2[1]), writes=(msk1[1],))
                for e in range(NE):
                    ctx.emit("V", lambda E, e=e, tb=tb: E.tensor_copy(
                        gates[0][:, e * ntt + tb:e * ntt + tb + 1],
                        msk1[0][:, e:e + 1]),
                        reads=(msk1[1],), writes=(gates[1],))
        with ctx.scope():
            ntc4 = min(4, ntc)
            nhalf = ntc // ntc4
            w1t = [ctx.sb(f"moe_w1{p}", [128, 4096], BF16) for p in range(NPT)]
            w2t = [ctx.sb(f"moe_w2{i}", [128, 4096], BF16) for i in range(2)]
            xnt = [ctx.sb(f"moe_xnt{p}", [128, 512], BF16) for p in range(NPT)]
            hT = [ctx.sb(f"moe_h{i}", [128, 512], BF16) for i in range(32)]
            b1c = ctx.sb("moe_b1c", [128, 32], F32)
            b2r = ctx.sb("moe_b2r", [1, 1024], BF16)
            yacc = [ctx.sb(f"moe_y{i}", [128, 1024], F32) for i in range(4 * ntc4)]
            psb = [ctx.ps(f"moe_ps{i}", [128, 512], F32) for i in range(8)]
            rp_moe = [None]
            for half in range(nhalf):
                for e in range(NE):
                    for p in range(NPT):
                        ctx.dma("QP", w1t[p][0][:],
                                W["w1s"][0][e * 1024 + p * 128:e * 1024 + (p + 1) * 128, :],
                                reads=(W["w1s"][1],), writes=(w1t[p][1],))
                    ctx.dma("QS", b1c[0][:], W["b1cols"][0][e * 128:(e + 1) * 128, :],
                            reads=(W["b1cols"][1],), writes=(b1c[1],))
                    ctx.dma("QP", b2r[0][:], W["b2rows"][0][0:1, e * 1024:(e + 1) * 1024],
                            reads=(W["b2rows"][1],), writes=(b2r[1],))
                    for tc4 in range(ntc4):
                        tc = half * ntc4 + tc4
                        tsl = slice(tc * 512, (tc + 1) * 512)
                        for p in range(NPT):
                            ctx.dma("QS", xnt[p][0][:],
                                    W["xn_sc"][0][p * 128:(p + 1) * 128, tsl],
                                    reads=(W["xn_sc"][1],), writes=(xnt[p][1],))
                        for fb in range(32):
                            ps_t, ps_b = psb[fb % 2]
                            pieces = [(w1t[kc][0][:, fb * 128:(fb + 1) * 128],
                                       xnt[kc][0][:]) for kc in range(NPT)]
                            reads = tuple(w1t[kc][1] for kc in range(NPT)) + \
                                tuple(xnt[kc][1] for kc in range(NPT))
                            mm_group(ctx, ps_t[:, :], pieces, reads=reads, wbuf=ps_b)
                            ht, ht_b = hT[fb]
                            ctx.emit("S", lambda E, ht=ht, ps_t=ps_t, fb=fb: E.activation(
                                ht[:], ps_t[:, :], AF.Gelu, bias=b1c[0][:, fb:fb + 1]),
                                reads=(ps_b, b1c[1]), writes=(ht_b,))
                        for g in range(8):
                            wt, wt_b = w2t[g % 2]
                            ctx.dma("QP", wt[:],
                                    W["w2s"][0][:, e * 32768 + g * 4096:
                                                e * 32768 + (g + 1) * 4096],
                                    reads=(W["w2s"][1],), writes=(wt_b,))
                            for k4 in range(4):
                                fc = g * 4 + k4
                                for ob2 in range(2):
                                    for tb4 in range(4):
                                        j = ob2 * 4 + tb4

                                        def fnw2(E, fc=fc, k4=k4, ob2=ob2, tb4=tb4,
                                                 j=j, wt=wt):
                                            ins = E.matmul(
                                                psb[j][0][:, :],
                                                hT[fc][0][:, tb4 * 128:(tb4 + 1) * 128],
                                                wt[:, k4 * 1024 + ob2 * 512:
                                                   k4 * 1024 + (ob2 + 1) * 512],
                                                start=(fc == 0), stop=False)
                                            if fc == 31:
                                                ins = E.matmul(
                                                    psb[j][0][:, :], ones1x[:, :],
                                                    b2r[0][0:1, ob2 * 512:(ob2 + 1) * 512],
                                                    start=False, stop=True)
                                            return ins

                                        ctx.emit("T", fnw2,
                                                 reads=(hT[fc][1], wt_b, ones1x_b, b2r[1]),
                                                 writes=(psb[j][1],))
                        for ob2 in range(2):
                            for tb4 in range(4):
                                j = ob2 * 4 + tb4
                                tb = tc * 4 + tb4
                                yt, yt_b = yacc[(tc4 * 4 + tb4)]
                                gcol = gates[0][:, e * ntt + tb:e * ntt + tb + 1]
                                osl = slice(ob2 * 512, (ob2 + 1) * 512)
                                if e == 0:
                                    ctx.emit("V", lambda E, yt=yt, j=j, osl=osl, gcol=gcol:
                                             E.tensor_scalar(yt[:, osl], psb[j][0][:, :],
                                                             gcol, None, ALU.mult),
                                             reads=(psb[j][1], gates[1]), writes=(yt_b,))
                                else:
                                    ctx.emit("V", lambda E, yt=yt, j=j, osl=osl, gcol=gcol:
                                             E.scalar_tensor_tensor(yt[:, osl],
                                                                    psb[j][0][:, :], gcol,
                                                                    yt[:, osl],
                                                                    ALU.mult, ALU.add),
                                             reads=(psb[j][1], gates[1], yt_b),
                                             writes=(yt_b,))

                if rp_moe[0] is None:
                    rp_moe[0] = ResidPt(ctx, "moerp", xin_ap, xin_bufs, xout_ap,
                                        xout_bufs, c0=0, cw=ntc4 * 512)
                rp = rp_moe[0]
                rp.c0 = half * ntc4 * 512
                for pt in range(NPT):
                    rp.begin(pt)
                    for tc4 in range(ntc4):
                        tc = half * ntc4 + tc4
                        dst_ap, dst_b = rp.sl(pt, tc)
                        for tb4 in range(4):
                            yt, yt_b = yacc[tc4 * 4 + tb4]
                            pj = psb[6 + tb4 % 2]
                            ctx.emit("T", lambda E, yt=yt, pt=pt, pj=pj: E.transpose(
                                pj[0][0:128, 0:128],
                                yt[:, pt * 128:(pt + 1) * 128], identf[:, :]),
                                reads=(yt_b, identf_b), writes=(pj[1],))
                            ctx.emit("V", lambda E, tb4=tb4, pj=pj, dst_ap=dst_ap:
                                     E.scalar_tensor_tensor(
                                dst_ap[:, tb4 * 128:(tb4 + 1) * 128],
                                pj[0][0:128, 0:128], 0.5,
                                dst_ap[:, tb4 * 128:(tb4 + 1) * 128],
                                ALU.mult, ALU.add),
                                reads=(pj[1], dst_b), writes=(dst_b,))
                    rp.end(pt)


def l3_ffn_nc(width=TT):
    def fn(ctx, consts, X, W):
        emit_l3_ffn(ctx, consts, X, W, width=width)
    return build_sublayer_nc(
        fn,
        [("wgT", [128, 64], F32), ("w1s", [8 * 1024, 4096], F32),
         ("w2s", [128, 8 * 32 * 1024], F32), ("b1cols", [8 * 128, 32], F32),
         ("b2rows", [1, 8 * 1024], F32)],
        [("xn_sc", [1024, width], BF16)],
        width=width)


def prep_l3_ffn(Wg, W1, b1, W2, b2):
    wgT = Wg.T.reshape(8, 128, 8).transpose(1, 0, 2).reshape(128, 64)
    return {
        "wgT": np.ascontiguousarray(wgT),
        "w1s": np.ascontiguousarray(W1.reshape(8 * 1024, 4096)),
        "w2s": np.ascontiguousarray(
            W2.reshape(8, 32, 128, 1024).transpose(2, 0, 1, 3).reshape(128, 8 * 32 * 1024)),
        "b1cols": np.ascontiguousarray(
            b1.reshape(8, 32, 128).transpose(0, 2, 1).reshape(8 * 128, 32)),
        "b2rows": np.ascontiguousarray(b2.reshape(1, 8 * 1024)),
    }


def emit_l2_attn(ctx, consts, X, W, width=TT):
    nc = ctx.nc
    xin_ap, xin_bufs = X["in"]
    xout_ap, xout_bufs = X["out"]
    ntc = width // 512
    ntt = width // 128
    ones1x, ones1x_b = consts["c_ones1x"]
    ones1xf, ones1xf_b = consts["c_ones1xf"]
    onesk, onesk_b = consts["c_onesk"]
    ident, ident_b = consts["c_ident"]
    iota, iota_b = consts["c_iota"]
    ustrict, ustrict_b = consts["c_ustrict"]
    with ctx.scope():
        wrow = ctx.sb("l2a_wrow", [1, 1024], F32)
        urow = ctx.sb("l2a_urow", [1, 1024], F32)
        e127 = ctx.sb("l2a_e127", [1, 1024], F32)
        e128 = ctx.sb("l2a_e128", [1, 1024], F32)
        ssum = ctx.sb("l2a_ssum", [1, 1024], F32)
        ssq = ctx.sb("l2a_ssq", [1, 1024], F32)
        ctx.dma("QS", urow[0][:], W["tf_row"][0], reads=(W["tf_row"][1],),
                writes=(urow[1],))
        ctx.dma("QS", wrow[0][:], W["td_row"][0], reads=(W["td_row"][1],),
                writes=(wrow[1],))
        ctx.emit("S", lambda E: E.activation(wrow[0][:], wrow[0][:], AF.Exp),
                 reads=(wrow[1],), writes=(wrow[1],))
        ctx.emit("V", lambda E: E.tensor_scalar(wrow[0][:], wrow[0][:], -1.0,
                                                None, ALU.mult),
                 reads=(wrow[1],), writes=(wrow[1],))
        ctx.emit("S", lambda E: E.activation(e127[0][:], wrow[0][:], AF.Exp, scale=127.0),
                 reads=(wrow[1],), writes=(e127[1],))
        ctx.emit("S", lambda E: E.activation(e128[0][:], wrow[0][:], AF.Exp, scale=128.0),
                 reads=(wrow[1],), writes=(e128[1],))
        with ctx.scope():
            xn = [ctx.sb(f"l2a_xn{p}", [128, width], BF16) for p in range(NPT)]
            with ctx.scope():
                sc = ln_scratch(ctx, "l2a_ln", width)
                ln_produce(ctx, consts, sc, lambda pt: xin_ap[pt * 128:(pt + 1) * 128, :],
                           xin_bufs, xn, xn, width=width)
            tmc = ctx.sb("l2a_tmc", [128, 24], F32)
            ctx.dma("QS", tmc[0][:], W["tm_cols"][0], reads=(W["tm_cols"][1],),
                    writes=(tmc[1],))
            xm = [ctx.sb(f"l2a_xm{p}", [128, width], BF16) for p in range(NPT)]
            sh = ctx.sb("l2a_sh", [128, width], BF16)
            dd = ctx.sb("l2a_dd", [128, width], BF16)
            wt = [ctx.sb(f"l2a_wt{p}", [128, 1024], BF16) for p in range(NPT)]
            vstg = [ctx.sb(f"l2a_vs{i}", [128, 512], F32) for i in range(3)]
            psm = [ctx.ps(f"l2a_mm{i}", [128, 512]) for i in range(2)]
            for m, (wname, dram) in enumerate(
                    (("wrT", "r_sc"), ("wkT", "k_sc"), ("wvT", "v_sc"))):
                for p in range(NPT):
                    ctx.dma("QP", wt[p][0][:], W[wname][0][p * 128:(p + 1) * 128, :],
                            reads=(W[wname][1],), writes=(wt[p][1],))
                    ctx.emit("V", lambda E, p=p: E.tensor_copy(
                        sh[0][:, 1:width], xn[p][0][:, 0:width - 1]),
                        reads=(xn[p][1],), writes=(sh[1],))
                    ctx.emit("V", lambda E: E.memset(sh[0][:, 0:1], 0.0),
                             reads=(), writes=(sh[1],))
                    ctx.emit("V", lambda E, p=p: E.tensor_tensor(
                        dd[0][:], xn[p][0][:], sh[0][:], ALU.subtract),
                        reads=(xn[p][1], sh[1]), writes=(dd[1],))
                    ctx.emit("V", lambda E, p=p, m=m: E.scalar_tensor_tensor(
                        xm[p][0][:], dd[0][:], tmc[0][:, m * 8 + p:m * 8 + p + 1],
                        sh[0][:], ALU.mult, ALU.add),
                        reads=(dd[1], tmc[1], sh[1]), writes=(xm[p][1],))
                for tb in range(ntt):
                    for fb in range(2):
                        ps_t, ps_b = psm[fb % 2]
                        pieces = [(xm[kc][0][:, tb * 128:(tb + 1) * 128],
                                   wt[kc][0][:, fb * 512:(fb + 1) * 512])
                                  for kc in range(NPT)]
                        reads = tuple(xm[kc][1] for kc in range(NPT)) + \
                            tuple(wt[kc][1] for kc in range(NPT))
                        mm_group(ctx, ps_t[:, :], pieces, reads=reads, wbuf=ps_b)
                        vs, vs_b = vstg[(tb * 2 + fb) % 3]
                        if m == 0:
                            ctx.emit("S", lambda E, vs=vs, ps_t=ps_t: E.activation(
                                vs[:], ps_t[:, :], AF.Sigmoid),
                                reads=(ps_b,), writes=(vs_b,))
                        else:
                            ctx.emit("V", lambda E, vs=vs, ps_t=ps_t: E.tensor_copy(
                                vs[:], ps_t[:, :]),
                                reads=(ps_b,), writes=(vs_b,))
                        ctx.dma("QS", W[dram][0][tb * 128:(tb + 1) * 128,
                                                 fb * 512:(fb + 1) * 512],
                                vs[:], reads=(vs_b,), writes=(W[dram][1],))
        with ctx.scope():
            IW = ctx.sb("l2a_IW", [128, 1024], F32)
            E1 = ctx.sb("l2a_E1", [128, 1024], F32)
            E2 = ctx.sb("l2a_E2", [128, 1024], F32)
            UB = ctx.sb("l2a_UB", [128, 1024], F32)
            EWB = ctx.sb("l2a_EWB", [128, 1024], F32)
            with ctx.scope():
                negw = ctx.sb("l2a_negw", [1, 1024], F32)
                psw = [ctx.ps(f"l2a_psw{i}", [128, 512]) for i in range(2)]
                ctx.emit("V", lambda E: E.tensor_scalar(negw[0][:], wrow[0][:], -1.0,
                                                        None, ALU.mult),
                         reads=(wrow[1],), writes=(negw[1],))
                for hf in range(2):
                    csl = slice(hf * 512, (hf + 1) * 512)
                    ps_t, ps_b = psw[hf]
                    mm_group(ctx, ps_t[:, :], [(iota[:, :], negw[0][0:1, csl])],
                             reads=(iota_b, negw[1]), wbuf=ps_b)
                    ctx.emit("V", lambda E, ps_t=ps_t, csl=csl: E.tensor_copy(
                        IW[0][:, csl], ps_t[:, :]),
                        reads=(ps_b,), writes=(IW[1],))
                    mm_group(ctx, ps_t[:, :], [(ones1xf[:, :], wrow[0][0:1, csl])],
                             reads=(ones1xf_b, wrow[1]), wbuf=ps_b)
                    ctx.emit("V", lambda E, ps_t=ps_t, csl=csl: E.tensor_tensor(
                        E1[0][:, csl], ps_t[:, :], IW[0][:, csl], ALU.add),
                        reads=(ps_b, IW[1]), writes=(E1[1],))
                    ctx.emit("S", lambda E, csl=csl: E.activation(
                        E1[0][:, csl], E1[0][:, csl], AF.Exp, scale=-1.0),
                        reads=(E1[1],), writes=(E1[1],))
                    ctx.emit("S", lambda E, csl=csl: E.activation(
                        E2[0][:, csl], IW[0][:, csl], AF.Exp, scale=-1.0),
                        reads=(IW[1],), writes=(E2[1],))
                    mm_group(ctx, ps_t[:, :], [(ones1xf[:, :], urow[0][0:1, csl])],
                             reads=(ones1xf_b, urow[1]), wbuf=ps_b)
                    ctx.emit("V", lambda E, ps_t=ps_t, csl=csl: E.tensor_copy(
                        UB[0][:, csl], ps_t[:, :]),
                        reads=(ps_b,), writes=(UB[1],))
                    mm_group(ctx, ps_t[:, :], [(ones1xf[:, :], wrow[0][0:1, csl])],
                             reads=(ones1xf_b, wrow[1]), wbuf=ps_b)
                    ctx.emit("V", lambda E, ps_t=ps_t, csl=csl: E.tensor_copy(
                        EWB[0][:, csl], ps_t[:, :]),
                        reads=(ps_b,), writes=(EWB[1],))
                    ctx.emit("S", lambda E, csl=csl: E.activation(
                        EWB[0][:, csl], EWB[0][:, csl], AF.Exp),
                        reads=(EWB[1],), writes=(EWB[1],))
            Nr = ctx.sb("l2a_Nr", [1, 1024], F32)
            Dr = ctx.sb("l2a_Dr", [1, 1024], F32)
            t1r = ctx.sb("l2a_t1r", [1, 1024], F32)
            ctx.emit("V", lambda E: E.memset(Nr[0][:], 0.0), writes=(Nr[1],))
            ctx.emit("V", lambda E: E.memset(Dr[0][:], 0.0), writes=(Dr[1],))
            kt = [ctx.sb(f"l2a_kt{i}", [128, 1024], F32) for i in range(2)]
            vt = [ctx.sb(f"l2a_vt{i}", [128, 1024], F32) for i in range(2)]
            rt = [ctx.sb(f"l2a_rt{i}", [128, 1024], F32) for i in range(2)]
            ek = ctx.sb("l2a_ek", [128, 1024], F32)
            yh = ctx.sb("l2a_yh", [128, 1024], BF16)
            yl = ctx.sb("l2a_yl", [128, 1024], BF16)
            zh = ctx.sb("l2a_zh", [128, 1024], BF16)
            zl = ctx.sb("l2a_zl", [128, 1024], BF16)
            yf = ctx.sb("l2a_yf", [128, 1024], F32)
            nt = ctx.sb("l2a_nt", [128, 1024], F32)
            dt_ = ctx.sb("l2a_dt", [128, 1024], F32)
            eu = ctx.sb("l2a_eu", [128, 1024], F32)
            ekr = ctx.sb("l2a_ekr", [128, 1024], F32)
            ta = ctx.sb("l2a_ta", [128, 1024], F32)
            tb_ = ctx.sb("l2a_tb", [128, 1024], F32)
            crel = ctx.sb("l2a_crel", [128, 1024], F32)
            t2r = ctx.sb("l2a_t2r", [1, 1024], F32)
            wkv = ctx.sb("l2a_wkv", [128, 1024], F32)
            wsq = ctx.sb("l2a_wsq", [128, 1024], BF16)
            wkb = ctx.sb("l2a_wkb", [128, 1024], BF16)
            ps_py = ctx.ps("l2a_ppy", [128, 1024], F32)
            ps_pz = ctx.ps("l2a_ppz", [128, 1024], F32)
            ps_sy = ctx.ps("l2a_psy", [1, 1024], F32)
            ps_sz = ctx.ps("l2a_psz", [1, 1024], F32)
            ctx.emit("V", lambda E: E.memset(ssum[0][:], 0.0), writes=(ssum[1],))
            ctx.emit("V", lambda E: E.memset(ssq[0][:], 0.0), writes=(ssq[1],))
            for cch in range(ntt):
                ka, ka_b = kt[cch % 2]
                va, va_b = vt[cch % 2]
                ra, ra_b = rt[cch % 2]
                rsl = slice(cch * 128, (cch + 1) * 128)
                ctx.dma("QS", ka[:], W["k_sc"][0][rsl, :], reads=(W["k_sc"][1],),
                        writes=(ka_b,))
                ctx.dma("QS", va[:], W["v_sc"][0][rsl, :], reads=(W["v_sc"][1],),
                        writes=(va_b,))
                ctx.dma("QS", ra[:], W["r_sc"][0][rsl, :], reads=(W["r_sc"][1],),
                        writes=(ra_b,))
                ctx.emit("V", lambda E, ka=ka: E.tensor_tensor(ek[0][:], ka[:],
                                                               IW[0][:], ALU.add),
                         reads=(ka_b, IW[1]), writes=(ek[1],))
                ctx.emit("S", lambda E: E.activation(ek[0][:], ek[0][:], AF.Exp),
                         reads=(ek[1],), writes=(ek[1],))
                ctx.emit("V", lambda E: E.tensor_copy(zh[0][:], ek[0][:]),
                         reads=(ek[1],), writes=(zh[1],))
                ctx.emit("V", lambda E: E.tensor_tensor(zl[0][:], ek[0][:], zh[0][:],
                                                        ALU.subtract),
                         reads=(ek[1], zh[1]), writes=(zl[1],))
                for hf in range(2):
                    csl = slice(hf * 512, (hf + 1) * 512)

                    def fnz(E, csl=csl):
                        E.matmul(ps_pz[0][:, csl], ustrict[:, :], zh[0][:, csl],
                                 start=True, stop=False)
                        E.matmul(ps_pz[0][:, csl], ustrict[:, :], zl[0][:, csl],
                                 start=False, stop=True)
                        E.matmul(ps_sz[0][0:1, csl], onesk[:, :], zh[0][:, csl],
                                 start=True, stop=False)
                        return E.matmul(ps_sz[0][0:1, csl], onesk[:, :], zl[0][:, csl],
                                        start=False, stop=True)

                    ctx.emit("T", fnz, reads=(ustrict_b, onesk_b, zh[1], zl[1]),
                             writes=(ps_pz[1], ps_sz[1]))
                ctx.emit("V", lambda E: E.tensor_tensor(dt_[0][:], ps_pz[0][:, :],
                                                        E1[0][:], ALU.mult),
                         reads=(ps_pz[1], E1[1]), writes=(dt_[1],))
                for hf in range(2):
                    csl = slice(hf * 512, (hf + 1) * 512)
                    mm_group(ctx, ps_pz[0][:, csl], [(ones1xf[:, :], Dr[0][0:1, csl])],
                             reads=(ones1xf_b, Dr[1]), wbuf=ps_pz[1])
                ctx.emit("V", lambda E: E.tensor_tensor(yf[0][:], ps_pz[0][:, :],
                                                        E2[0][:], ALU.mult),
                         reads=(ps_pz[1], E2[1]), writes=(yf[1],))
                ctx.emit("V", lambda E: E.tensor_tensor(dt_[0][:], dt_[0][:], yf[0][:],
                                                        ALU.add),
                         reads=(dt_[1], yf[1]), writes=(dt_[1],))
                ctx.emit("V", lambda E: E.tensor_tensor(t1r[0][:], ps_sz[0][0:1, :],
                                                        e127[0][:], ALU.mult),
                         reads=(ps_sz[1], e127[1]), writes=(t1r[1],))
                ctx.emit("V", lambda E: E.tensor_tensor(Dr[0][:], Dr[0][:], e128[0][:],
                                                        ALU.mult),
                         reads=(Dr[1], e128[1]), writes=(Dr[1],))
                ctx.emit("V", lambda E: E.tensor_tensor(Dr[0][:], Dr[0][:], t1r[0][:],
                                                        ALU.add),
                         reads=(Dr[1], t1r[1]), writes=(Dr[1],))
                ctx.emit("V", lambda E: E.tensor_tensor(ekr[0][:], ek[0][:], E2[0][:],
                                                        ALU.mult),
                         reads=(ek[1], E2[1]), writes=(ekr[1],))
                ctx.emit("V", lambda E: E.tensor_tensor(ta[0][:], dt_[0][:], EWB[0][:],
                                                        ALU.mult),
                         reads=(dt_[1], EWB[1]), writes=(ta[1],))
                ctx.emit("V", lambda E: E.tensor_tensor(tb_[0][:], ta[0][:], ekr[0][:],
                                                        ALU.add),
                         reads=(ta[1], ekr[1]), writes=(tb_[1],))
                ctx.emit("V", lambda E: E.tensor_tensor(ta[0][:], ta[0][:], ekr[0][:],
                                                        ALU.max),
                         reads=(ta[1], ekr[1]), writes=(ta[1],))
                ctx.emit("V", lambda E: E.reciprocal(ta[0][:], ta[0][:]),
                         reads=(ta[1],), writes=(ta[1],))
                ctx.emit("V", lambda E: E.tensor_tensor(ta[0][:], ta[0][:], tb_[0][:],
                                                        ALU.mult),
                         reads=(ta[1], tb_[1]), writes=(ta[1],))
                ctx.emit("S", lambda E: E.activation(ta[0][:], ta[0][:], AF.Ln),
                         reads=(ta[1],), writes=(ta[1],))
                ctx.emit("V", lambda E: E.tensor_copy(yh[0][:], ta[0][:]),
                         reads=(ta[1],), writes=(yh[1],))
                ctx.emit("V", lambda E: E.tensor_tensor(yl[0][:], ta[0][:], yh[0][:],
                                                        ALU.subtract),
                         reads=(ta[1], yh[1]), writes=(yl[1],))
                for hf in range(2):
                    csl = slice(hf * 512, (hf + 1) * 512)

                    def fnl(E, csl=csl):
                        E.matmul(ps_py[0][:, csl], ustrict[:, :], yh[0][:, csl],
                                 start=True, stop=False)
                        E.matmul(ps_py[0][:, csl], ustrict[:, :], yl[0][:, csl],
                                 start=False, stop=True)
                        E.matmul(ps_sy[0][0:1, csl], onesk[:, :], yh[0][:, csl],
                                 start=True, stop=False)
                        return E.matmul(ps_sy[0][0:1, csl], onesk[:, :], yl[0][:, csl],
                                        start=False, stop=True)

                    ctx.emit("T", fnl, reads=(ustrict_b, onesk_b, yh[1], yl[1]),
                             writes=(ps_py[1], ps_sy[1]))
                ctx.emit("S", lambda E: E.activation(crel[0][:], ps_py[0][:, :], AF.Exp),
                         reads=(ps_py[1],), writes=(crel[1],))
                ctx.emit("S", lambda E: E.activation(t2r[0][:], ps_sy[0][0:1, :], AF.Exp),
                         reads=(ps_sy[1],), writes=(t2r[1],))
                ctx.emit("V", lambda E, va=va: E.tensor_tensor(yf[0][:], ek[0][:],
                                                               va[:], ALU.mult),
                         reads=(ek[1], va_b), writes=(yf[1],))
                ctx.emit("V", lambda E: E.reciprocal(ta[0][:], crel[0][:]),
                         reads=(crel[1],), writes=(ta[1],))
                ctx.emit("V", lambda E: E.tensor_tensor(yf[0][:], yf[0][:], ta[0][:],
                                                        ALU.mult),
                         reads=(yf[1], ta[1]), writes=(yf[1],))
                ctx.emit("V", lambda E: E.tensor_copy(yh[0][:], yf[0][:]),
                         reads=(yf[1],), writes=(yh[1],))
                ctx.emit("V", lambda E: E.tensor_tensor(yl[0][:], yf[0][:], yh[0][:],
                                                        ALU.subtract),
                         reads=(yf[1], yh[1]), writes=(yl[1],))
                for hf in range(2):
                    csl = slice(hf * 512, (hf + 1) * 512)

                    def fny(E, csl=csl):
                        E.matmul(ps_py[0][:, csl], ustrict[:, :], yh[0][:, csl],
                                 start=True, stop=False)
                        E.matmul(ps_py[0][:, csl], ustrict[:, :], yl[0][:, csl],
                                 start=False, stop=True)
                        E.matmul(ps_sy[0][0:1, csl], onesk[:, :], yh[0][:, csl],
                                 start=True, stop=False)
                        return E.matmul(ps_sy[0][0:1, csl], onesk[:, :], yl[0][:, csl],
                                        start=False, stop=True)

                    ctx.emit("T", fny, reads=(ustrict_b, onesk_b, yh[1], yl[1]),
                             writes=(ps_py[1], ps_sy[1]))
                ctx.emit("V", lambda E: E.tensor_tensor(nt[0][:], ps_py[0][:, :],
                                                        E1[0][:], ALU.mult),
                         reads=(ps_py[1], E1[1]), writes=(nt[1],))
                for hf in range(2):
                    csl = slice(hf * 512, (hf + 1) * 512)
                    mm_group(ctx, ps_py[0][:, csl], [(ones1xf[:, :], Nr[0][0:1, csl])],
                             reads=(ones1xf_b, Nr[1]), wbuf=ps_py[1])
                ctx.emit("V", lambda E: E.tensor_tensor(yf[0][:], ps_py[0][:, :],
                                                        E2[0][:], ALU.mult),
                         reads=(ps_py[1], E2[1]), writes=(yf[1],))
                ctx.emit("V", lambda E: E.tensor_tensor(nt[0][:], nt[0][:], yf[0][:],
                                                        ALU.add),
                         reads=(nt[1], yf[1]), writes=(nt[1],))
                ctx.emit("V", lambda E: E.tensor_tensor(nt[0][:], nt[0][:], crel[0][:],
                                                        ALU.mult),
                         reads=(nt[1], crel[1]), writes=(nt[1],))
                ctx.emit("V", lambda E: E.tensor_tensor(t1r[0][:], ps_sy[0][0:1, :],
                                                        e127[0][:], ALU.mult),
                         reads=(ps_sy[1], e127[1]), writes=(t1r[1],))
                ctx.emit("V", lambda E: E.tensor_tensor(Nr[0][:], Nr[0][:], e128[0][:],
                                                        ALU.mult),
                         reads=(Nr[1], e128[1]), writes=(Nr[1],))
                ctx.emit("V", lambda E: E.tensor_tensor(Nr[0][:], Nr[0][:], t1r[0][:],
                                                        ALU.add),
                         reads=(Nr[1], t1r[1]), writes=(Nr[1],))
                ctx.emit("V", lambda E: E.tensor_tensor(Nr[0][:], Nr[0][:], t2r[0][:],
                                                        ALU.mult),
                         reads=(Nr[1], t2r[1]), writes=(Nr[1],))
                ctx.emit("V", lambda E, ka=ka: E.tensor_tensor(eu[0][:], ka[:],
                                                               UB[0][:], ALU.add),
                         reads=(ka_b, UB[1]), writes=(eu[1],))
                ctx.emit("S", lambda E: E.activation(eu[0][:], eu[0][:], AF.Exp),
                         reads=(eu[1],), writes=(eu[1],))
                ctx.emit("V", lambda E, va=va: E.tensor_tensor(yf[0][:], eu[0][:],
                                                               va[:], ALU.mult),
                         reads=(eu[1], va_b), writes=(yf[1],))
                ctx.emit("V", lambda E: E.tensor_tensor(nt[0][:], nt[0][:], yf[0][:],
                                                        ALU.add),
                         reads=(nt[1], yf[1]), writes=(nt[1],))
                ctx.emit("V", lambda E: E.tensor_tensor(dt_[0][:], dt_[0][:], eu[0][:],
                                                        ALU.add),
                         reads=(dt_[1], eu[1]), writes=(dt_[1],))
                ctx.emit("V", lambda E: E.reciprocal(dt_[0][:], dt_[0][:]),
                         reads=(dt_[1],), writes=(dt_[1],))
                ctx.emit("V", lambda E: E.tensor_tensor(wkv[0][:], nt[0][:], dt_[0][:],
                                                        ALU.mult),
                         reads=(nt[1], dt_[1]), writes=(wkv[1],))
                ctx.emit("V", lambda E, ra=ra: E.tensor_tensor(wkv[0][:], wkv[0][:],
                                                               ra[:], ALU.mult),
                         reads=(wkv[1], ra_b), writes=(wkv[1],))
                ctx.emit("V", lambda E: E.tensor_copy(wkb[0][:], wkv[0][:]),
                         reads=(wkv[1],), writes=(wkb[1],))
                ctx.dma("QS", W["wkv_sc"][0][rsl, :], wkb[0][:],
                        reads=(wkb[1],), writes=(W["wkv_sc"][1],))
                ctx.emit("S", lambda E: E.activation(wsq[0][:], wkv[0][:], AF.Square),
                         reads=(wkv[1],), writes=(wsq[1],))
                for hf in range(2):
                    csl = slice(hf * 512, (hf + 1) * 512)

                    def fnst(E, csl=csl):
                        E.matmul(ps_sy[0][0:1, csl], onesk[:, :], wkb[0][:, csl],
                                 start=True, stop=True)
                        return E.matmul(ps_sz[0][0:1, csl], onesk[:, :], wsq[0][:, csl],
                                        start=True, stop=True)

                    ctx.emit("T", fnst, reads=(onesk_b, wkb[1], wsq[1]),
                             writes=(ps_sy[1], ps_sz[1]))
                    ctx.emit("V", lambda E, csl=csl: E.tensor_tensor(
                        ssum[0][0:1, csl], ssum[0][0:1, csl], ps_sy[0][0:1, csl], ALU.add),
                        reads=(ssum[1], ps_sy[1]), writes=(ssum[1],))
                    ctx.emit("V", lambda E, csl=csl: E.tensor_tensor(
                        ssq[0][0:1, csl], ssq[0][0:1, csl], ps_sz[0][0:1, csl], ALU.add),
                        reads=(ssq[1], ps_sz[1]), writes=(ssq[1],))
        with ctx.scope():
            hsum = ctx.sb("l2a_hsum", [1, 16], F32)
            hsq = ctx.sb("l2a_hsq", [1, 16], F32)
            for h in range(16):
                ctx.emit("V", lambda E, h=h: E.reduce_sum(
                    hsum[0][0:1, h:h + 1], ssum[0][0:1, h * 64:(h + 1) * 64], axis=AX),
                    reads=(ssum[1],), writes=(hsum[1],))
                ctx.emit("V", lambda E, h=h: E.reduce_sum(
                    hsq[0][0:1, h:h + 1], ssq[0][0:1, h * 64:(h + 1) * 64], axis=AX),
                    reads=(ssq[1],), writes=(hsq[1],))
            nelem = float(width * 64)
            ctx.emit("V", lambda E: E.tensor_scalar(hsum[0][:], hsum[0][:], 1.0 / nelem,
                                                    None, ALU.mult),
                     reads=(hsum[1],), writes=(hsum[1],))
            ctx.emit("V", lambda E: E.tensor_scalar(hsq[0][:], hsq[0][:], 1.0 / nelem,
                                                    None, ALU.mult),
                     reads=(hsq[1],), writes=(hsq[1],))
            ctx.emit("V", lambda E: E.scalar_tensor_tensor(t1r[0][0:1, 0:16], hsum[0][:],
                                                           -1.0, hsum[0][:],
                                                           ALU.mult, ALU.mult),
                     reads=(hsum[1],), writes=(t1r[1],))
            ctx.emit("V", lambda E: E.tensor_tensor(hsq[0][:], hsq[0][:],
                                                    t1r[0][0:1, 0:16], ALU.add),
                     reads=(hsq[1], t1r[1]), writes=(hsq[1],))
            ctx.emit("V", lambda E: E.tensor_scalar(hsq[0][:], hsq[0][:], 1e-5,
                                                    None, ALU.add),
                     reads=(hsq[1],), writes=(hsq[1],))
            ctx.emit("S", lambda E: E.activation(hsq[0][:], hsq[0][:], AF.Ln),
                     reads=(hsq[1],), writes=(hsq[1],))
            ctx.emit("S", lambda E: E.activation(hsq[0][:], hsq[0][:], AF.Exp, scale=-0.5),
                     reads=(hsq[1],), writes=(hsq[1],))
            ctx.emit("V", lambda E: E.scalar_tensor_tensor(hsum[0][:], hsum[0][:], -1.0,
                                                           hsq[0][:], ALU.mult, ALU.mult),
                     reads=(hsum[1], hsq[1]), writes=(hsum[1],))
            acol = ctx.sb("l2a_acol", [128, 32], F32)
            psf = ctx.ps("l2a_psf", [128, 1], F32)
            for which, row in ((0, hsq), (16, hsum)):
                for h in range(16):
                    mm_group(ctx, psf[0][:, :],
                             [(ones1xf[:, :], row[0][0:1, h:h + 1])],
                             reads=(row[1], ones1xf_b), wbuf=psf[1])
                    ctx.emit("V", lambda E, which=which, h=h: E.tensor_copy(
                        acol[0][:, which + h:which + h + 1], psf[0][:, :]),
                        reads=(psf[1],), writes=(acol[1],))
            znT = [ctx.sb(f"l2a_zn{p}", [128, width], BF16) for p in range(NPT)]
            psc = [ctx.ps(f"l2a_ptc{i}", [128, 128], BF16) for i in range(2)]
            nkt = [ctx.sb(f"l2a_nk{i}", [128, 1024], BF16) for i in range(2)]
            nvt = [ctx.sb(f"l2a_nv{i}", [128, 1024], F32) for i in range(2)]
            wkb = ctx.sb("l2a_wkb2", [128, 1024], BF16)
            for cch in range(ntt):
                wa, wa_b = nkt[cch % 2]
                ctx.dma("QS", wa[:], W["wkv_sc"][0][cch * 128:(cch + 1) * 128, :],
                        reads=(W["wkv_sc"][1],), writes=(wa_b,))
                nb, nb_b = nvt[cch % 2]
                for h in range(16):
                    hsl = slice(h * 64, (h + 1) * 64)
                    ctx.emit("V", lambda E, wa=wa, nb=nb, hsl=hsl, h=h: E.tensor_scalar(
                        nb[:, hsl], wa[:, hsl], acol[0][:, h:h + 1],
                        acol[0][:, 16 + h:16 + h + 1], ALU.mult, ALU.add),
                        reads=(wa_b, acol[1]), writes=(nb_b,))
                ctx.emit("V", lambda E, nb=nb, wa=wa: E.tensor_copy(
                    wkb[0][:], nb[:]),
                    reads=(nb_b,), writes=(wkb[1],))
                for pt in range(NPT):
                    pc, pc_b = psc[pt % 2]
                    ctx.emit("T", lambda E, pc=pc, pt=pt: E.transpose(
                        pc[:, :], wkb[0][:, pt * 128:(pt + 1) * 128], ident[:, :]),
                        reads=(wkb[1], ident_b), writes=(pc_b,))
                    ctx.emit("V", lambda E, pc=pc, pt=pt, cch=cch: E.tensor_copy(
                        znT[pt][0][:, cch * 128:(cch + 1) * 128], pc[:, :]),
                        reads=(pc_b,), writes=(znT[pt][1],))
            wo = [ctx.sb(f"l2a_wo{p}", [128, 1024], BF16) for p in range(NPT)]
            gcl = ctx.sb("l2a_gcl", [128, 16], F32)
            ctx.dma("QS", gcl[0][:], W["gb_cols"][0], reads=(W["gb_cols"][1],),
                    writes=(gcl[1],))
            for p in range(NPT):
                ctx.dma("QP", wo[p][0][:], W["woT"][0][p * 128:(p + 1) * 128, :],
                        reads=(W["woT"][1],), writes=(wo[p][1],))
            boc = ctx.sb("l2a_boc", [128, 8], F32)
            bcl = ctx.sb("l2a_bcl", [128, 8], BF16)
            psf2 = ctx.ps("l2a_psf2", [128, 1], F32)
            ctx.emit("V", lambda E: E.tensor_copy(bcl[0][:], gcl[0][:, 8:16]),
                     reads=(gcl[1],), writes=(bcl[1],))
            for ob in range(NPT):
                pieces = [(wo[kc][0][:, ob * 128:(ob + 1) * 128], bcl[0][:, kc:kc + 1])
                          for kc in range(NPT)]
                reads = tuple(wo[kc][1] for kc in range(NPT)) + (bcl[1],)
                mm_group(ctx, psf2[0][:, :], pieces, reads=reads, wbuf=psf2[1])
                ctx.emit("V", lambda E, ob=ob: E.tensor_copy(
                    boc[0][:, ob:ob + 1], psf2[0][:, :]),
                    reads=(psf2[1],), writes=(boc[1],))
            for p in range(NPT):
                ctx.emit("V", lambda E, p=p: E.tensor_scalar(
                    wo[p][0][:], wo[p][0][:], gcl[0][:, p:p + 1], None, ALU.mult),
                    reads=(wo[p][1], gcl[1]), writes=(wo[p][1],))
            psm2 = [ctx.ps(f"l2a_omm{i}", [128, 512]) for i in range(4)]
            rp = ResidPt(ctx, "l2a", xin_ap, xin_bufs, xout_ap, xout_bufs, cw=width)

            def o_evict(ob, tc, ps_t, ps_b):
                dst, dst_b = rp.sl(ob, tc)
                ctx.emit("V", lambda E, ob=ob: E.scalar_tensor_tensor(
                    dst, ps_t[:, :], boc[0][:, ob:ob + 1], dst, ALU.add, ALU.add),
                    reads=(ps_b, boc[1], dst_b), writes=(dst_b,))

            fm_linear(ctx, wo, znT, NPT, range(NPT), range(ntc), psm2, o_evict,
                      ob_begin=rp.begin, ob_end=rp.end)


def l2_attn_nc(width=TT):
    def fn(ctx, consts, X, W):
        emit_l2_attn(ctx, consts, X, W, width=width)
    return build_sublayer_nc(
        fn,
        [("wrT", [1024, 1024], F32), ("wkT", [1024, 1024], F32),
         ("wvT", [1024, 1024], F32), ("woT", [1024, 1024], F32),
         ("tm_cols", [128, 24], F32), ("td_row", [1, 1024], F32),
         ("tf_row", [1, 1024], F32), ("gb_cols", [128, 16], F32)],
        [("r_sc", [width, 1024], F32), ("k_sc", [width, 1024], F32),
         ("v_sc", [width, 1024], F32), ("wkv_sc", [width, 1024], BF16)],
        width=width)


def prep_l2_attn(time_mix_r, time_mix_k, time_mix_v, W_r, W_k, W_v, W_o,
                 time_decay, time_first, ln_gamma, ln_beta):
    tm = np.stack([time_mix_r.reshape(1024), time_mix_k.reshape(1024),
                   time_mix_v.reshape(1024)], 0)
    tmc = tm.reshape(3, 8, 128).transpose(2, 0, 1).reshape(128, 24)
    gb = np.concatenate([ln_gamma.reshape(8, 128).T, ln_beta.reshape(8, 128).T], 1)
    return {
        "wrT": np.ascontiguousarray(W_r.T), "wkT": np.ascontiguousarray(W_k.T),
        "wvT": np.ascontiguousarray(W_v.T), "woT": np.ascontiguousarray(W_o.T),
        "tm_cols": np.ascontiguousarray(tmc),
        "td_row": np.ascontiguousarray(time_decay.reshape(1, 1024)),
        "tf_row": np.ascontiguousarray(time_first.reshape(1, 1024)),
        "gb_cols": np.ascontiguousarray(gb),
    }


HID = 2730


def _hid_blocks():
    bl = []
    c = 0
    while c < HID:
        w = min(128, HID - c)
        bl.append((c, w))
        c += w
    return bl


def emit_l12_ffn(ctx, consts, X, W, width=TT, tag="l12"):
    nc = ctx.nc
    xin_ap, xin_bufs = X["in"]
    xout_ap, xout_bufs = X["out"]
    ntc = width // 512
    ntt = width // 128
    heads = 8
    hstride = ntt * 64
    ones1x, ones1x_b = consts["c_ones1x"]
    blk2, blk2_b = consts["c_blk2f"]
    blk2T, blk2T_b = consts["c_blk2Tf"]
    crot, crot_b = consts["c_rot"]
    z_sc = W["z_sc"]
    zsh_sc = W["zsh_sc"]

    def load_z_hilo(zn, zl, shift_hi):
        with ctx.scope():
            stg = [ctx.sb(f"{tag}_zst{i}", [128, width], F32) for i in range(2)]
            for p in range(NPT):
                sg, sg_b = stg[p % 2]
                ctx.dma("QS", sg[:], z_sc[0][p * 128:(p + 1) * 128, :],
                        reads=(z_sc[1],), writes=(sg_b,))
                if shift_hi and p >= 4:
                    ctx.emit("V", lambda E, p=p, sg=sg: E.tensor_copy(
                        zn[p][0][:, 1:width], sg[:, 0:width - 1]),
                        reads=(sg_b,), writes=(zn[p][1],))
                    ctx.emit("V", lambda E, p=p: E.memset(zn[p][0][:, 0:1], 0.0),
                             writes=(zn[p][1],))
                    if zl is not None:
                        ctx.emit("V", lambda E, p=p, sg=sg: E.tensor_tensor(
                            zl[p][0][:, 1:width], sg[:, 0:width - 1],
                            zn[p][0][:, 1:width], ALU.subtract),
                            reads=(sg_b, zn[p][1]), writes=(zl[p][1],))
                        ctx.emit("V", lambda E, p=p: E.memset(zl[p][0][:, 0:1], 0.0),
                                 writes=(zl[p][1],))
                else:
                    ctx.emit("V", lambda E, p=p, sg=sg: E.tensor_copy(
                        zn[p][0][:], sg[:]),
                        reads=(sg_b,), writes=(zn[p][1],))
                    if zl is not None:
                        ctx.emit("V", lambda E, p=p, sg=sg: E.tensor_tensor(
                            zl[p][0][:], sg[:], zn[p][0][:], ALU.subtract),
                            reads=(sg_b, zn[p][1]), writes=(zl[p][1],))

    def inner_layer(l):
        with ctx.scope():
            hz = [ctx.sb(f"{tag}_hz{p}", [128, width], BF16) for p in range(NPT)]
            hl = [ctx.sb(f"{tag}_hl{p}", [128, width], BF16) for p in range(NPT)]
            load_z_hilo(hz, hl, shift_hi=True)
            wqk = [ctx.sb(f"{tag}_wqk{p}", [128, 1024], BF16) for p in range(NPT)]
            wql = [ctx.sb(f"{tag}_wql{p}", [128, 1024], BF16) for p in range(NPT)]
            wv_t = [ctx.sb(f"{tag}_wv{p}", [128, 512], BF16) for p in range(NPT)]
            wfs = [ctx.sb(f"{tag}_wfs{i}", [128, 1024], F32) for i in range(1)]
            for p in range(NPT):
                rsl = slice(l * 1024 + p * 128, l * 1024 + (p + 1) * 128)
                ctx.dma("QP", wqk[p][0][:], W["wqkv"][0][rsl, 0:1024],
                        reads=(W["wqkv"][1],), writes=(wqk[p][1],))
                ctx.dma("QP", wv_t[p][0][:], W["wqkv"][0][rsl, 1024:1536],
                        reads=(W["wqkv"][1],), writes=(wv_t[p][1],))
                wf, wf_b = wfs[0]
                ctx.dma("QS", wf[:], W["wqkv"][0][rsl, 0:1024],
                        reads=(W["wqkv"][1],), writes=(wf_b,))
                ctx.emit("V", lambda E, p=p, wf=wf: E.tensor_tensor(
                    wql[p][0][:], wf[:], wqk[p][0][:], ALU.subtract),
                    reads=(wf_b, wqk[p][1]), writes=(wql[p][1],))
            qksc = ctx.sb(f"{tag}_qksc", [128, 2], F32)
            ctx.dma("QS", qksc[0][:], W["qk_cols"][0][:, l * 2:(l + 1) * 2],
                    reads=(W["qk_cols"][1],), writes=(qksc[1],))
            qs8 = ctx.sb(f"{tag}_qs8", [128, 2], F32)
            ctx.emit("V", lambda E: E.tensor_scalar(qs8[0][:, 0:1], qksc[0][:, 0:1],
                                                    8.0, None, ALU.mult),
                     reads=(qksc[1],), writes=(qs8[1],))
            ctx.emit("V", lambda E: E.tensor_copy(qs8[0][:, 1:2], qksc[0][:, 1:2]),
                     reads=(qksc[1],), writes=(qs8[1],))
            rope = [ctx.sb(f"{tag}_rope{i}", [128, 512], F32) for i in range(2)]
            qf = [ctx.sb(f"{tag}_qf{i}", [128, 512], F32) for i in range(3)]
            q2 = [ctx.sb(f"{tag}_q2{i}", [128, 512], F32) for i in range(3)]
            qb2 = [ctx.sb(f"{tag}_qb{i}", [128, 512], BF16) for i in range(3)]
            qlb = [ctx.sb(f"{tag}_qlb{i}", [128, 512], BF16) for i in range(3)]
            t2 = ctx.sb(f"{tag}_t2", [128, 512], F32)
            rr = ctx.sb(f"{tag}_rr", [2, 512], F32)
            psm = [ctx.ps(f"{tag}_mm{i}", [128, 512]) for i in range(3)]
            ps2 = ctx.ps(f"{tag}_ps2", [2, 512])
            psb = ctx.ps(f"{tag}_psb", [128, 512])
            psr = ctx.ps(f"{tag}_psr", [128, 512])
            def qkev(sec, ob, tc, gi, ps_t, ps_b):
                tsl2 = slice(tc * 512, (tc + 1) * 512)
                f, f_b = qf[gi % 3]
                sq, sq_b = q2[gi % 3]
                ctx.emit("V", lambda E, f=f, ps_t=ps_t: E.tensor_copy(
                    f[:], ps_t[:, :]),
                    reads=(ps_b,), writes=(f_b,))
                ctx.emit("S", lambda E, sq=sq, f=f: E.activation(
                    sq[:], f[:], AF.Square),
                    reads=(f_b,), writes=(sq_b,))
                mm_group(ctx, ps2[0][:, :], [(blk2[:, :], sq[:])],
                         reads=(blk2_b, sq_b), wbuf=ps2[1])
                ctx.emit("S", lambda E: E.activation(rr[0][:, :], ps2[0][:, :],
                                                     AF.Sqrt),
                         reads=(ps2[1],), writes=(rr[1],))
                ctx.emit("V", lambda E: E.tensor_scalar(rr[0][:, :], rr[0][:, :],
                                                        1e-12, None, ALU.max),
                         reads=(rr[1],), writes=(rr[1],))
                ctx.emit("V", lambda E: E.reciprocal(rr[0][:, :], rr[0][:, :]),
                         reads=(rr[1],), writes=(rr[1],))
                mm_group(ctx, psb[0][:, :], [(blk2T[:, :], rr[0][:, :])],
                         reads=(blk2T_b, rr[1]), wbuf=psb[1])
                ctx.emit("V", lambda E, f=f: E.tensor_tensor(
                    f[:], f[:], psb[0][:, :], ALU.mult),
                    reads=(f_b, psb[1]), writes=(f_b,))
                ctx.emit("V", lambda E, f=f, sec=sec: E.tensor_scalar(
                    f[:], f[:], qs8[0][:, sec:sec + 1], None, ALU.mult),
                    reads=(f_b, qs8[1]), writes=(f_b,))
                ia, ib = 0, 1
                for i, tblname in ((ia, "ropeA_q" if sec == 0 else "ropeA_k"),
                                   (ib, "ropeB_q" if sec == 0 else "ropeB_k")):
                    ctx.dma("QS", rope[i][0][:], W[tblname][0][:, tsl2],
                            reads=(W[tblname][1],), writes=(rope[i][1],))
                qh_, qh_b = qb2[gi % 3]
                ql_, ql_b = qlb[gi % 3]
                ctx.emit("V", lambda E, qh_=qh_, f=f: E.tensor_copy(qh_[:], f[:]),
                         reads=(f_b,), writes=(qh_b,))
                ctx.emit("V", lambda E, ql_=ql_, f=f, qh_=qh_: E.tensor_tensor(
                    ql_[:], f[:], qh_[:], ALU.subtract),
                    reads=(f_b, qh_b), writes=(ql_b,))

                def fnrot(E, qh_=qh_, ql_=ql_):
                    E.matmul(psr[0][:, :], crot[:, :], qh_[:],
                             start=True, stop=False)
                    return E.matmul(psr[0][:, :], crot[:, :], ql_[:],
                                    start=False, stop=True)

                ctx.emit("T", fnrot, reads=(crot_b, qh_b, ql_b),
                         writes=(psr[1],))
                ctx.emit("V", lambda E, f=f, ia=ia: E.tensor_tensor(
                    f[:], f[:], rope[ia][0][:], ALU.mult),
                    reads=(f_b, rope[ia][1]), writes=(f_b,))
                ctx.emit("V", lambda E, ib=ib: E.tensor_tensor(
                    t2[0][:], psr[0][:, :], rope[ib][0][:], ALU.mult),
                    reads=(psr[1], rope[ib][1]), writes=(t2[1],))
                ctx.emit("V", lambda E, f=f: E.tensor_tensor(
                    f[:], f[:], t2[0][:], ALU.add),
                    reads=(f_b, t2[1]), writes=(f_b,))
                ctx.emit("V", lambda E, qh_=qh_, f=f: E.tensor_copy(qh_[:], f[:]),
                         reads=(f_b,), writes=(qh_b,))
                ctx.emit("V", lambda E, ql_=ql_, f=f, qh_=qh_: E.tensor_tensor(
                    ql_[:], f[:], qh_[:], ALU.subtract),
                    reads=(f_b, qh_b), writes=(ql_b,))
                dram = W["q_sc"] if sec == 0 else W["k_sc"]
                for half in range(2):
                    h = ob * 2 + half
                    ctx.dma(("QS", "QA")[half],
                            dram[0][h * 64:(h + 1) * 64, tsl2],
                            qh_[64 * half:64 * half + 64, :],
                            reads=(qh_b,), writes=(dram[1],))
                    ctx.dma(("QP", "QS")[half],
                            dram[0][512 + h * 64:512 + (h + 1) * 64, tsl2],
                            ql_[64 * half:64 * half + 64, :],
                            reads=(ql_b,), writes=(dram[1],))

            from collections import deque
            for sec in range(2):
                pend = deque()
                gi = 0
                for ob in range(4):
                    col = slice(sec * 512 + ob * 128, sec * 512 + (ob + 1) * 128)
                    for tc in range(ntc):
                        tsl = slice(tc * 512, (tc + 1) * 512)
                        ps_t, ps_b = psm[gi % 3]
                        pieces = []
                        reads = []
                        for kc in range(NPT):
                            pieces += [(wqk[kc][0][:, col], hz[kc][0][:, tsl]),
                                       (wqk[kc][0][:, col], hl[kc][0][:, tsl]),
                                       (wql[kc][0][:, col], hz[kc][0][:, tsl])]
                            reads += [wqk[kc][1], wql[kc][1], hz[kc][1], hl[kc][1]]
                        mm_group(ctx, ps_t[:, :], pieces, reads=tuple(reads), wbuf=ps_b)
                        pend.append((sec, ob, tc, gi, ps_t, ps_b))
                        gi += 1
                        if len(pend) > 2:
                            qkev(*pend.popleft())
                while pend:
                    qkev(*pend.popleft())
            vstg = [ctx.sb(f"{tag}_vs{i}", [128, 512], BF16) for i in range(3)]
            for tb in range(ntt):
                ps_t, ps_b = psm[tb % 2]
                pieces = [(hz[kc][0][:, tb * 128:(tb + 1) * 128], wv_t[kc][0][:, :])
                          for kc in range(NPT)]
                reads = tuple(hz[kc][1] for kc in range(NPT)) + \
                    tuple(wv_t[kc][1] for kc in range(NPT))
                mm_group(ctx, ps_t[:, :], pieces, reads=reads, wbuf=ps_b)
                vs, vs_b = vstg[tb % 3]
                ctx.emit("V", lambda E, vs=vs, ps_t=ps_t: E.tensor_copy(vs[:], ps_t[:, :]),
                         reads=(ps_b,), writes=(vs_b,))
                for hh in range(8):
                    ctx.dma(("QS", "QA", "QP")[hh % 3],
                            W["v_sc"][0][:, hh * hstride + tb * 64:
                                         hh * hstride + (tb + 1) * 64],
                            vs[:, hh * 64:(hh + 1) * 64],
                            reads=(vs_b,), writes=(W["v_sc"][1],))
        attn64_core(ctx, consts, f"{tag}_a{l}", W["q_sc"], W["k_sc"], W["v_sc"],
                    W["za_sc"], heads, width, causal=False, local=True)
        with ctx.scope():
            zT = [ctx.sb(f"{tag}_zT{p}", [128, width], BF16) for p in range(4)]
            for p in range(4):
                for half in range(2):
                    ctx.dma("QS", zT[p][0][64 * half:64 * half + 64, :],
                            W["za_sc"][0][(2 * p + half) * 64:(2 * p + half + 1) * 64, :],
                            reads=(W["za_sc"][1],), writes=(zT[p][1],))
            wo = [ctx.sb(f"{tag}_wo{p}", [128, 1024], BF16) for p in range(4)]
            for p in range(4):
                ctx.dma("QP", wo[p][0][:],
                        W["wo"][0][l * 512 + p * 128:l * 512 + (p + 1) * 128, :],
                        reads=(W["wo"][1],), writes=(wo[p][1],))
            psm = [ctx.ps(f"{tag}_omm{i}", [128, 512]) for i in range(4)]
            zbufs = [z_sc[1]] * NPT
            rp = ResidPt(ctx, tag + "o", z_sc[0], zbufs, z_sc[0], zbufs, cw=width)

            def o_evict(ob, tc, ps_t, ps_b):
                dst, dst_b = rp.sl(ob, tc)
                ctx.emit("V", lambda E: E.tensor_tensor(dst, ps_t[:, :], dst, ALU.add),
                         reads=(ps_b, dst_b), writes=(dst_b,))

            fm_linear(ctx, wo, zT, 4, range(NPT), range(ntc), psm, o_evict,
                      ob_begin=rp.begin, ob_end=rp.end)
        with ctx.scope():
            shstg = [ctx.sb(f"{tag}_shs{i}", [128, width], F32) for i in range(2)]
            zcol = ctx.sb(f"{tag}_zc", [128, 1], F32)
            ctx.emit("V", lambda E: E.memset(zcol[0][:], 0.0), writes=(zcol[1],))
            for p in range(NPT):
                sg, sg_b = shstg[p % 2]
                if p >= 4:
                    ctx.dma("QS", sg[:, 1:width],
                            z_sc[0][p * 128:(p + 1) * 128, 0:width - 1],
                            reads=(z_sc[1],), writes=(sg_b,))
                    ctx.dma("QS", sg[:, 0:1], zcol[0][:], reads=(zcol[1],),
                            writes=(sg_b,))
                else:
                    ctx.dma("QS", sg[:], z_sc[0][p * 128:(p + 1) * 128, :],
                            reads=(z_sc[1],), writes=(sg_b,))
                ctx.dma("QS", zsh_sc[0][p * 128:(p + 1) * 128, :], sg[:],
                        reads=(sg_b,), writes=(zsh_sc[1],))
        with ctx.scope():
            with ctx.scope():
                xn = [ctx.sb(f"{tag}_fxn{p}", [128, width], BF16) for p in range(NPT)]
                with ctx.scope():
                    sc = ln_scratch(ctx, f"{tag}_fln", width)
                    ln_produce(ctx, consts, sc,
                               lambda pt: zsh_sc[0][pt * 128:(pt + 1) * 128, :],
                               [zsh_sc[1]] * NPT, xn, xn, width=width)
                for p in range(NPT):
                    ctx.dma("QS", W["xnb_sc"][0][p * 128:(p + 1) * 128, :], xn[p][0][:],
                            reads=(xn[p][1],), writes=(W["xnb_sc"][1],))
            bh = _hid_blocks()
            nbh = len(bh)
            w1t = [ctx.sb(f"{tag}_w1{p}", [128, 5460], BF16) for p in range(NPT)]
            w2t = [ctx.sb(f"{tag}_w2{i}", [128, 1024], BF16) for i in range(nbh)]
            gcl = ctx.sb(f"{tag}_gcl", [128, 16], F32)
            ctx.dma("QS", gcl[0][:], W["lngb_cols"][0][:, l * 16:(l + 1) * 16],
                    reads=(W["lngb_cols"][1],), writes=(gcl[1],))
            b1c = ctx.sb(f"{tag}_b1c", [128, 2 * nbh], F32)
            ctx.dma("QS", b1c[0][:], W["b1_cols"][0][:, l * 2 * nbh:(l + 1) * 2 * nbh],
                    reads=(W["b1_cols"][1],), writes=(b1c[1],))
            bclb = ctx.sb(f"{tag}_bclb", [128, 8], BF16)
            ctx.emit("V", lambda E: E.tensor_copy(bclb[0][:], gcl[0][:, 8:16]),
                     reads=(gcl[1],), writes=(bclb[1],))
            b2c = ctx.sb(f"{tag}_b2c", [128, 8], F32)
            ctx.dma("QS", b2c[0][:], W["b2_cols"][0][:, l * 8:(l + 1) * 8],
                    reads=(W["b2_cols"][1],), writes=(b2c[1],))
            psf = ctx.ps(f"{tag}_psf", [128, 1], F32)
            for p in range(NPT):
                ctx.dma("QP", w1t[p][0][:],
                        W["w1"][0][l * 1024 + p * 128:l * 1024 + (p + 1) * 128, :],
                        reads=(W["w1"][1],), writes=(w1t[p][1],))
            for bi, (c0, cw) in enumerate(bh):
                ctx.dma("QP", w2t[bi][0][0:cw, :],
                        W["w2"][0][l * HID + c0:l * HID + c0 + cw, :],
                        reads=(W["w2"][1],), writes=(w2t[bi][1],))
            for bi, (c0, cw) in enumerate(bh):
                for which in range(2):
                    off = which * HID
                    pieces = [(w1t[kc][0][:, off + c0:off + c0 + cw],
                               bclb[0][:, kc:kc + 1]) for kc in range(NPT)]
                    reads = tuple(w1t[kc][1] for kc in range(NPT)) + (bclb[1],)
                    mm_group(ctx, psf[0][0:cw, :], pieces, reads=reads, wbuf=psf[1])
                    ctx.emit("V", lambda E, bi=bi, which=which, cw=cw:
                             E.tensor_tensor(
                                 b1c[0][0:cw, 2 * bi + which:2 * bi + which + 1],
                                 b1c[0][0:cw, 2 * bi + which:2 * bi + which + 1],
                                 psf[0][0:cw, :], ALU.add),
                             reads=(b1c[1], psf[1]), writes=(b1c[1],))
            for p in range(NPT):
                ctx.emit("V", lambda E, p=p: E.tensor_scalar(
                    w1t[p][0][:], w1t[p][0][:], gcl[0][:, p:p + 1], None, ALU.mult),
                    reads=(w1t[p][1], gcl[1]), writes=(w1t[p][1],))
            h2t = [ctx.sb(f"{tag}_h2{i}", [128, 512], BF16) for i in range(nbh)]
            xnt = [ctx.sb(f"{tag}_xnt{p}", [128, 512], BF16) for p in range(NPT)]
            g1 = [ctx.sb(f"{tag}_g1{i}", [128, 512], F32) for i in range(2)]
            zslp = [ctx.sb(f"{tag}_zsl{i}", [128, 512], F32) for i in range(NPT)]
            psm = [ctx.ps(f"{tag}_fmm{i}", [128, 512]) for i in range(4)]
            for tc in range(ntc):
                tsl = slice(tc * 512, (tc + 1) * 512)
                for ob in range(NPT):
                    xt, xb = zslp[ob]
                    ctx.dma("QS", xt[:], z_sc[0][ob * 128:(ob + 1) * 128, tsl],
                            reads=(zbufs2[ob],), writes=(xb,))
                for p in range(NPT):
                    ctx.dma("QS", xnt[p][0][:],
                            W["xnb_sc"][0][p * 128:(p + 1) * 128, tsl],
                            reads=(W["xnb_sc"][1],), writes=(xnt[p][1],))
                for bi, (c0, cw) in enumerate(bh):
                    psA, psA_b = psm[0 + 2 * (bi % 2)]
                    psG, psG_b = psm[1 + 2 * (bi % 2)]
                    for which, (pst, psb_) in ((0, (psA, psA_b)), (1, (psG, psG_b))):
                        off = which * HID
                        pieces = [(w1t[kc][0][:, off + c0:off + c0 + cw],
                                   xnt[kc][0][:]) for kc in range(NPT)]
                        reads = tuple(w1t[kc][1] for kc in range(NPT)) + \
                            tuple(xnt[kc][1] for kc in range(NPT))
                        mm_group(ctx, pst[0:cw, :], pieces, reads=reads, wbuf=psb_)
                    gg, gg_b = g1[bi % 2]
                    ctx.emit("S", lambda E, gg=gg, psG=psG, bi=bi, cw=cw: E.activation(
                        gg[0:cw, :], psG[0:cw, :], AF.Gelu,
                        bias=b1c[0][0:cw, 2 * bi + 1:2 * bi + 2]),
                        reads=(psG_b, b1c[1]), writes=(gg_b,))
                    ht, ht_b = h2t[bi]
                    ctx.emit("V", lambda E, ht=ht, psA=psA, gg=gg, bi=bi, cw=cw:
                             E.scalar_tensor_tensor(ht[0:cw, :], psA[0:cw, :],
                                                    b1c[0][0:cw, 2 * bi:2 * bi + 1],
                                                    gg[0:cw, :], ALU.add, ALU.mult),
                             reads=(psA_b, b1c[1], gg_b), writes=(ht_b,))
                for ob in range(NPT):
                    ps_t, ps_b = psm[ob % 2]

                    def fnw2(E, ob=ob, ps_t=ps_t):
                        ins = None
                        for bi, (c0, cw) in enumerate(bh):
                            ins = E.matmul(ps_t[:, :],
                                           w2t[bi][0][0:cw, ob * 128:(ob + 1) * 128],
                                           h2t[bi][0][0:cw, :],
                                           start=(bi == 0), stop=(bi == nbh - 1))
                        return ins

                    reads = tuple(w2t[bi][1] for bi in range(nbh)) + \
                        tuple(h2t[bi][1] for bi in range(nbh))
                    ctx.emit("T", fnw2, reads=reads, writes=(ps_b,))

                    xt, xb = zslp[ob]
                    ctx.emit("V", lambda E, xt=xt, ps_t=ps_t, ob=ob:
                             E.scalar_tensor_tensor(xt[:], ps_t[:, :],
                                                    b2c[0][:, ob:ob + 1], xt[:],
                                                    ALU.add, ALU.add),
                             reads=(ps_b, b2c[1], xb), writes=(xb,))
                    ctx.dma("QS", z_sc[0][ob * 128:(ob + 1) * 128, tsl], xt[:],
                            reads=(xb,), writes=(zbufs2[ob],))

    zbufs2 = [z_sc[1]] * NPT
    with ctx.scope():
        xn0 = [ctx.sb(f"{tag}_xn0{p}", [128, width], BF16) for p in range(NPT)]
        with ctx.scope():
            sc = ln_scratch(ctx, f"{tag}_ln0", width)
            ln_produce(ctx, consts, sc, lambda pt: xin_ap[pt * 128:(pt + 1) * 128, :],
                       xin_bufs, xn0, xn0, width=width, zout=z_sc)
    for l in range(2):
        inner_layer(l)
    with ctx.scope():
        zst = [ctx.sb(f"{tag}_zfin{i}", [128, width], F32) for i in range(2)]
        rp = ResidPt(ctx, tag + "f", xin_ap, xin_bufs, xout_ap, xout_bufs, cw=width)
        for pt in range(NPT):
            rp.begin(pt)
            zt, zt_b = zst[pt % 2]
            ctx.dma("QS", zt[:], z_sc[0][pt * 128:(pt + 1) * 128, :],
                    reads=(z_sc[1],), writes=(zt_b,))
            for tc in range(ntc):
                dst, dst_b = rp.sl(pt, tc)
                tsl = slice(tc * 512, (tc + 1) * 512)
                ctx.emit("V", lambda E, zt=zt, tsl=tsl, dst=dst: E.scalar_tensor_tensor(
                    dst, zt[:, tsl], 0.5, dst, ALU.mult, ALU.add),
                    reads=(zt_b, dst_b), writes=(dst_b,))
            rp.end(pt)


def l12_ffn_nc(width=TT, tag="l12"):
    hstride = (width // 128) * 64

    def fn(ctx, consts, X, W):
        emit_l12_ffn(ctx, consts, X, W, width=width, tag=tag)
    return build_sublayer_nc(
        fn,
        [("wqkv", [2048, 1536], F32), ("wo", [1024, 1024], F32),
         ("w1", [2048, 5460], F32), ("w2", [2 * HID, 1024], F32),
         ("qk_cols", [128, 4], F32), ("lngb_cols", [128, 32], F32),
         ("b1_cols", [128, 4 * len(_hid_blocks())], F32),
         ("b2_cols", [128, 16], F32),
         ("ropeA_q", [128, width], F32), ("ropeB_q", [128, width], F32),
         ("ropeA_k", [128, width], F32), ("ropeB_k", [128, width], F32)],
        [("z_sc", [1024, width], F32), ("zsh_sc", [1024, width], F32),
         ("xnb_sc", [1024, width], BF16),
         ("q_sc", [1024, width], BF16), ("k_sc", [1024, width], BF16),
         ("v_sc", [128, 8 * hstride], BF16), ("za_sc", [512, width], BF16)],
        width=width)


def rope_tables(width):
    DIM_HEAD, WINDOW = 64, 512
    half_idx = np.arange(0, DIM_HEAD, 2, dtype=np.float64)
    inv_freq = 1.0 / 10000.0 ** (half_idx / DIM_HEAD)
    t = np.arange(width, dtype=np.float64)
    freqs = t[:, None] * inv_freq[None, :]
    freqs = np.concatenate([freqs, freqs], -1)
    base = (half_idx + 0.4 * DIM_HEAD) / (1.4 * DIM_HEAD)
    power = (t - width // 2) / float(WINDOW // 2)
    scale = base[None, :] ** power[:, None]
    scale = np.concatenate([scale, scale], -1)
    cos, sin = np.cos(freqs), np.sin(freqs)
    Aq = (cos * scale).T; Bq = (sin * scale).T
    Ak = (cos / scale).T; Bk = (sin / scale).T
    out = {}
    for name, a in (("ropeA_q", Aq), ("ropeB_q", Bq), ("ropeA_k", Ak), ("ropeB_k", Bk)):
        out[name] = np.ascontiguousarray(np.tile(a, (2, 1)).astype(np.float32))
    return out


def prep_l12_ffn(wqkv, q_scale, k_scale, wo, ln_g, ln_b, w1, b1, w2, b2, width=TT):
    bh = _hid_blocks()
    nbh = len(bh)
    b1c = np.zeros((128, 4 * nbh), np.float32)
    for l in range(2):
        for bi, (c0, cw) in enumerate(bh):
            b1c[0:cw, l * 2 * nbh + 2 * bi] = b1[l, c0:c0 + cw]
            b1c[0:cw, l * 2 * nbh + 2 * bi + 1] = b1[l, HID + c0:HID + c0 + cw]
    lngb = np.zeros((128, 32), np.float32)
    for l in range(2):
        lngb[:, l * 16:l * 16 + 8] = ln_g[l].reshape(8, 128).T
        lngb[:, l * 16 + 8:l * 16 + 16] = ln_b[l].reshape(8, 128).T
    b2c = np.zeros((128, 16), np.float32)
    for l in range(2):
        b2c[:, l * 8:(l + 1) * 8] = b2[l].reshape(8, 128).T
    qk = np.zeros((128, 4), np.float32)
    for l in range(2):
        qk[:, l * 2] = np.tile(q_scale[l], 2)
        qk[:, l * 2 + 1] = np.tile(k_scale[l], 2)
    d = {
        "wqkv": np.ascontiguousarray(wqkv.reshape(2048, 1536)),
        "wo": np.ascontiguousarray(wo.reshape(1024, 1024)),
        "w1": np.ascontiguousarray(w1.reshape(2048, 5460)),
        "w2": np.ascontiguousarray(w2.reshape(2 * HID, 1024)),
        "qk_cols": qk, "lngb_cols": lngb, "b1_cols": b1c, "b2_cols": b2c,
    }
    d.update(rope_tables(width))
    return d


_NC_CACHE = {}


def _get_nc(name, width=TT):
    key = (name, width)
    if key not in _NC_CACHE:
        _NC_CACHE[key] = {"l0f": lambda: l0_ffn_nc(first=True, width=width),
                          "l0a": lambda: l0_attn_nc(width=width),
                          "l12": lambda: l12_ffn_nc(width=width),
                          "l1a": lambda: l1_attn_nc(width=width),
                          "l2a": lambda: l2_attn_nc(width=width),
                          "l3f": lambda: l3_ffn_nc(width=width),
                          "l3a": lambda: l3_attn_nc(width=width)}[name]()
    return _NC_CACHE[key]


def kernel(**inputs):
    i = inputs
    n_cores = 8
    consts = host_consts()
    w_l0f = {"w_upT": np.ascontiguousarray(i["L0_ffn_w_up"].T),
             "w_downT": np.ascontiguousarray(i["L0_ffn_w_down"].T)}
    w_l0a = prep_l0_attn(i["L0_attn_W_spin"], i["L0_attn_b_spin"],
                         i["L0_attn_W_global"], i["L0_attn_W_J"], i["L0_attn_beta"])
    w_l1f = prep_l12_ffn(i["L1_ffn_wqkv"], i["L1_ffn_q_scale"], i["L1_ffn_k_scale"],
                         i["L1_ffn_wo"], i["L1_ffn_ln_g"], i["L1_ffn_ln_b"],
                         i["L1_ffn_w1"], i["L1_ffn_b1"], i["L1_ffn_w2"], i["L1_ffn_b2"])
    w_l1a = prep_l1_attn(i["L1_attn_Wqkv"], i["L1_attn_bqkv"], i["L1_attn_Wo"],
                         i["L1_attn_bo"], i["L1_attn_wq"], i["L1_attn_wk"])
    w_l2f = prep_l12_ffn(i["L2_ffn_wqkv"], i["L2_ffn_q_scale"], i["L2_ffn_k_scale"],
                         i["L2_ffn_wo"], i["L2_ffn_ln_g"], i["L2_ffn_ln_b"],
                         i["L2_ffn_w1"], i["L2_ffn_b1"], i["L2_ffn_w2"], i["L2_ffn_b2"])
    w_l2a = prep_l2_attn(i["L2_attn_time_mix_r"], i["L2_attn_time_mix_k"],
                         i["L2_attn_time_mix_v"], i["L2_attn_W_r"], i["L2_attn_W_k"],
                         i["L2_attn_W_v"], i["L2_attn_W_o"], i["L2_attn_time_decay"],
                         i["L2_attn_time_first"], i["L2_attn_ln_gamma"],
                         i["L2_attn_ln_beta"])
    w_l3f = prep_l3_ffn(i["L3_ffn_Wg"], i["L3_ffn_W1"], i["L3_ffn_b1"],
                        i["L3_ffn_W2"], i["L3_ffn_b2"])
    w_l3a = prep_l3_attn(i["L3_attn_Wq"], i["L3_attn_Wk"], i["L3_attn_Wv"],
                         i["L3_attn_Wo"], i["L3_attn_q_scale"], i["L3_attn_k_scale"])

    chain = [("l0f", w_l0f), ("l0a", w_l0a), ("l0f", w_l0f),
             ("l12", w_l1f), ("l1a", w_l1a), ("l12", w_l1f),
             ("l12", w_l2f), ("l2a", w_l2a), ("l12", w_l2f),
             ("l3f", w_l3f), ("l3a", w_l3a), ("l3f", w_l3f)]

    x = inputs["L0_attn_x"].astype(np.float32, copy=False)
    xres = [np.ascontiguousarray(x[b].T) for b in range(n_cores)]
    for name, wd in chain:
        nc = _get_nc(name)
        in_maps = []
        for b in range(n_cores):
            m = dict(wd)
            m.update(consts)
            m["xres_in"] = xres[b]
            in_maps.append(m)
        res = run_bass_kernel_spmd(nc, in_maps, core_ids=list(range(n_cores)))
        xres = [r["xres_out"] for r in res.results]
    out = np.stack([np.ascontiguousarray(xr.T) for xr in xres], 0)
    return out.astype(np.float32)


def _decl_w(nc, prefix, weight_specs, internal_specs):
    W = {}
    for name, shape, dt in weight_specs:
        ap = nc.dram_tensor(prefix + name, list(shape), dt, kind="ExternalInput").ap()
        W[name] = (ap, Buf(prefix + name))
    for name, shape, dt in internal_specs:
        ap = nc.dram_tensor(prefix + name, list(shape), dt, kind="Internal").ap()
        W[name] = (ap, Buf(prefix + name))
    return W


FUSED_SPECS = {
    "l0f": ([("w_upT", [1024, 4096], F32), ("w_downT", [4096, 1024], F32)],
            [("cu", [128, 8 * 4096], BF16), ("cd", [128, 32 * 1024], BF16)]),
    "l0a": ([("wspinT", [1024, 1024], F32), ("bspin_cols", [128, 8], F32),
             ("wglobT", [1024, 1024], F32), ("wjT", [1024, 1024], F32),
             ("beta128", [128, 1], F32)], []),
    "l1a": ([("wqkvT", [1024, 3072], F32), ("bqk_cols", [128, 16], F32),
             ("bv_row", [1, 1024], F32), ("woT", [1024, 1024], F32),
             ("bo_cols", [128, 8], F32), ("wqwk128", [128, 2], F32)],
            [("q_sc", [2048, TT], BF16), ("k_sc", [2048, TT], BF16),
             ("v_sc", [128, 16 * (TT // 128) * 64], BF16), ("z_sc", [1024, TT], BF16)]),
    "l12": ([("wqkv", [2048, 1536], F32), ("wo", [1024, 1024], F32),
             ("w1", [2048, 5460], F32), ("w2", [2 * HID, 1024], F32),
             ("qk_cols", [128, 4], F32), ("lngb_cols", [128, 32], F32),
             ("b1_cols", [128, 4 * 22], F32), ("b2_cols", [128, 16], F32),
             ("ropeA_q", [128, TT], F32), ("ropeB_q", [128, TT], F32),
             ("ropeA_k", [128, TT], F32), ("ropeB_k", [128, TT], F32)],
            [("z_sc", [1024, TT], F32), ("zsh_sc", [1024, TT], F32),
             ("xnb_sc", [1024, TT], BF16),
             ("q_sc", [1024, TT], BF16), ("k_sc", [1024, TT], BF16),
             ("v_sc", [128, 8 * (TT // 128) * 64], BF16), ("za_sc", [512, TT], BF16)]),
    "l2a": ([("wrT", [1024, 1024], F32), ("wkT", [1024, 1024], F32),
             ("wvT", [1024, 1024], F32), ("woT", [1024, 1024], F32),
             ("tm_cols", [128, 24], F32), ("td_row", [1, 1024], F32),
             ("tf_row", [1, 1024], F32), ("gb_cols", [128, 16], F32)],
            [("r_sc", [TT, 1024], F32), ("k_sc", [TT, 1024], F32),
             ("v_sc", [TT, 1024], F32), ("wkv_sc", [TT, 1024], BF16)]),
    "l3f": ([("wgT", [128, 64], F32), ("w1s", [8 * 1024, 4096], F32),
             ("w2s", [128, 8 * 32 * 1024], F32), ("b1cols", [8 * 128, 32], F32),
             ("b2rows", [1, 8 * 1024], F32)],
            [("xn_sc", [1024, TT], BF16)]),
    "l3a": ([("wqkvT", [1024, 3072], F32), ("woT", [1024, 1024], F32),
             ("qk_scales", [2, 16], F32)],
            [("q_sc", [2048, TT], BF16), ("k_sc", [2048, TT], BF16),
             ("v_sc", [128, 16 * (TT // 128) * 64], BF16), ("z_sc", [1024, TT], BF16)]),
}

FUSED_EMIT = {
    "l0a": emit_l0_attn, "l1a": emit_l1_attn, "l2a": emit_l2_attn,
    "l3a": emit_l3_attn, "l3f": emit_l3_ffn,
}


def build_fused_nc(width=TT):
    nc = bass.Bass("TRN2", target_bir_lowering=False)
    with ExitStack() as stack:
        ctx = Ctx(nc, stack)
        xin = nc.dram_tensor("xres_in", [DM, width], F32, kind="ExternalInput").ap()
        xout = nc.dram_tensor("xres_out", [DM, width], F32, kind="ExternalOutput").ap()
        xa = nc.dram_tensor("xres_a", [DM, width], F32, kind="Internal").ap()
        xb = nc.dram_tensor("xres_b", [DM, width], F32, kind="Internal").ap()
        xs = [(xin, [Buf(f"xi{p}") for p in range(NPT)]),
              (xa, [Buf(f"xa{p}") for p in range(NPT)]),
              (xb, [Buf(f"xb{p}") for p in range(NPT)]),
              (xout, [Buf(f"xo{p}") for p in range(NPT)])]
        consts = declare_consts(nc, ctx, stack)
        Wd = {}
        for key, pref in (("l0f", "l0f_"), ("l0a", "l0a_"), ("f1", "f1_"),
                          ("l1a", "l1a_"), ("f2", "f2_"), ("l2a", "l2a_"),
                          ("l3f", "l3f_"), ("l3a", "l3a_")):
            spec = FUSED_SPECS["l12"] if key in ("f1", "f2") else FUSED_SPECS[key]
            Wd[key] = _decl_w(nc, pref, spec[0], spec[1])
        chain = [("l0f", 0), ("l0a", 0), ("l0f", 1),
                 ("f1", 0), ("l1a", 0), ("f1", 1),
                 ("f2", 0), ("l2a", 0), ("f2", 1),
                 ("l3f", 0), ("l3a", 0), ("l3f", 1)]
        cur = 0
        for si, (key, app) in enumerate(chain):
            src = xs[cur]
            if si == len(chain) - 1:
                dst = xs[3]
            else:
                dst = xs[1] if cur != 1 else xs[2]
            X = {"in": src, "out": dst}
            if key == "l0f":
                Wx = dict(Wd["l0f"])
                Wx["first"] = (app == 0)
                emit_l0_ffn(ctx, consts, X, Wx, width=width)
            elif key in ("f1", "f2"):
                emit_l12_ffn(ctx, consts, X, Wd[key], width=width, tag=key + str(app))
            else:
                FUSED_EMIT[key](ctx, consts, X, Wd[key], width=width)
            cur = 1 if dst is xs[1] else (2 if dst is xs[2] else 3)
        ctx.barrier()
    return nc


_FUSED_NC = [None]


def kernel(**inputs):
    i = inputs
    n_cores = 8
    base = dict(host_consts())
    w_l0f = {"w_upT": np.ascontiguousarray(i["L0_ffn_w_up"].T),
             "w_downT": np.ascontiguousarray(i["L0_ffn_w_down"].T)}
    preps = {
        "l0f_": w_l0f,
        "l0a_": prep_l0_attn(i["L0_attn_W_spin"], i["L0_attn_b_spin"],
                             i["L0_attn_W_global"], i["L0_attn_W_J"], i["L0_attn_beta"]),
        "f1_": prep_l12_ffn(i["L1_ffn_wqkv"], i["L1_ffn_q_scale"], i["L1_ffn_k_scale"],
                            i["L1_ffn_wo"], i["L1_ffn_ln_g"], i["L1_ffn_ln_b"],
                            i["L1_ffn_w1"], i["L1_ffn_b1"], i["L1_ffn_w2"], i["L1_ffn_b2"]),
        "l1a_": prep_l1_attn(i["L1_attn_Wqkv"], i["L1_attn_bqkv"], i["L1_attn_Wo"],
                             i["L1_attn_bo"], i["L1_attn_wq"], i["L1_attn_wk"]),
        "f2_": prep_l12_ffn(i["L2_ffn_wqkv"], i["L2_ffn_q_scale"], i["L2_ffn_k_scale"],
                            i["L2_ffn_wo"], i["L2_ffn_ln_g"], i["L2_ffn_ln_b"],
                            i["L2_ffn_w1"], i["L2_ffn_b1"], i["L2_ffn_w2"], i["L2_ffn_b2"]),
        "l2a_": prep_l2_attn(i["L2_attn_time_mix_r"], i["L2_attn_time_mix_k"],
                             i["L2_attn_time_mix_v"], i["L2_attn_W_r"], i["L2_attn_W_k"],
                             i["L2_attn_W_v"], i["L2_attn_W_o"], i["L2_attn_time_decay"],
                             i["L2_attn_time_first"], i["L2_attn_ln_gamma"],
                             i["L2_attn_ln_beta"]),
        "l3f_": prep_l3_ffn(i["L3_ffn_Wg"], i["L3_ffn_W1"], i["L3_ffn_b1"],
                            i["L3_ffn_W2"], i["L3_ffn_b2"]),
        "l3a_": prep_l3_attn(i["L3_attn_Wq"], i["L3_attn_Wk"], i["L3_attn_Wv"],
                             i["L3_attn_Wo"], i["L3_attn_q_scale"], i["L3_attn_k_scale"]),
    }
    for pref, d in preps.items():
        for k, v in d.items():
            if k == "first":
                continue
            base[pref + k] = v
    if _FUSED_NC[0] is None:
        _FUSED_NC[0] = build_fused_nc()
    nc = _FUSED_NC[0]
    x = inputs["L0_attn_x"].astype(np.float32, copy=False)
    in_maps = []
    for b in range(n_cores):
        m = dict(base)
        m["xres_in"] = np.ascontiguousarray(x[b].T)
        in_maps.append(m)
    res = run_bass_kernel_spmd(nc, in_maps, core_ids=list(range(n_cores)))
    out = np.stack([np.ascontiguousarray(r["xres_out"].T) for r in res.results], 0)
    return out.astype(np.float32)


# version 22 (final)
# speedup vs baseline: 1.0193x; 1.0193x over previous
import numpy as np
import ml_dtypes
from contextlib import ExitStack

import concourse.bass as bass
import concourse.mybir as mybir
from concourse.bass_utils import run_bass_kernel_spmd
from concourse.alu_op_type import AluOpType as ALU

F32 = mybir.dt.float32
BF16 = mybir.dt.bfloat16
AF = mybir.ActivationFunctionType
AX = mybir.AxisListType.X
NEG = -3.0e38

TT = 4096
DM = 1024
NPT = DM // 128
NTC = TT // 512
BF = ml_dtypes.bfloat16

FVALS = [0.0, 1.0]

N_LAUNCH = [0]


class Buf:
    __slots__ = ("name", "w", "r")

    def __init__(self, name=""):
        self.name = name
        self.w = None
        self.r = {}

    def __repr__(self):
        return f"Buf({self.name})"


DMAQ = ("QS", "QP", "QA")
SEQ = {"T": "T", "V": "V", "S": "S", "P": "P", "QS": "QS", "QP": "P", "QA": "S"}
ALL_E = ("T", "V", "S", "P", "QS", "QP", "QA")


NROT = 8


class Ctx:
    def __init__(self, nc, stack):
        self.nc = nc
        self.stack = stack
        self.sem = {e: stack.enter_context(nc.semaphore("sem_" + e))
                    for e in ("T", "V", "S", "P")}
        for q in DMAQ:
            for k in range(NROT):
                self.sem[f"{q}{k}"] = stack.enter_context(nc.semaphore(f"sem_{q}{k}"))
        self.cnt = {e: 0 for e in ("T", "V", "S", "P")}
        self.qcnt = {q: 0 for q in DMAQ}
        self.eng = {"T": nc.tensor, "V": nc.vector, "S": nc.scalar, "P": nc.gpsimd,
                    "QS": nc.sync, "QP": nc.gpsimd, "QA": nc.scalar}
        self.waited = {s: {} for s in ("T", "V", "S", "P", "QS")}

    @staticmethod
    def _dep(deps, ec):
        e, c = ec
        if deps.get(e, 0) < c:
            deps[e] = c

    def emit(self, e, fn, reads=(), writes=()):
        seq = SEQ[e]
        is_dma = e in DMAQ
        deps = {}
        for b in reads:
            if b.w:
                self._dep(deps, b.w)
        for b in writes:
            if b.w:
                self._dep(deps, b.w)
            for pe, pc in b.r.items():
                self._dep(deps, (pe, pc))
        if is_dma:
            i = self.qcnt[e]
            tok = f"{e}{i % NROT}"
            if i >= NROT:
                self._dep(deps, (tok, i // NROT))
        w = self.waited[seq]
        eng = self.eng[e]
        for pe in sorted(deps):
            pc = deps[pe]
            if pe == "T" and e == "T":
                continue
            if w.get(pe, 0) >= pc:
                continue
            eng.wait_ge(self.sem[pe], pc * (1 if pe in self.cnt else 16))
            w[pe] = pc
        ins = fn(eng)
        if is_dma:
            self.qcnt[e] = i + 1
            ins.then_inc(self.sem[tok], 16)
            me = (tok, i // NROT + 1)
        else:
            self.cnt[e] += 1
            ins.then_inc(self.sem[e], 1)
            me = (e, self.cnt[e])
        for b in reads:
            if b.r.get(me[0], 0) < me[1]:
                b.r[me[0]] = me[1]
        for b in writes:
            b.w = me
            b.r = {}
        return ins

    def _tokens(self):
        toks = dict(self.cnt)
        for q in DMAQ:
            i = self.qcnt[q]
            for k in range(NROT):
                v = (i - k + NROT - 1) // NROT if i > k else 0
                v = (i - 1 - k) // NROT + 1 if i > k else 0
                toks[f"{q}{k}"] = v
        return toks

    def barrier(self):
        cur = self._tokens()
        for seq in ("T", "V", "S", "P", "QS"):
            w = self.waited[seq]
            eng = self.eng[seq]
            for pe, pc in cur.items():
                if pc and w.get(pe, 0) < pc:
                    eng.wait_ge(self.sem[pe], pc * (1 if pe in self.cnt else 16))
                    w[pe] = pc

    from contextlib import contextmanager as _cm

    @_cm
    def scope(self):
        old = self.stack
        st = ExitStack()
        self.stack = st
        try:
            yield st
        finally:
            st.close()
            self.stack = old
            self.barrier()

    def dma(self, q, dst_ap, src_ap, reads=(), writes=()):
        return self.emit(q, lambda E: E.dma_start(dst_ap, src_ap), reads, writes)

    def _uniq(self, name):
        self._names = getattr(self, "_names", {})
        n = self._names.get(name, 0)
        self._names[name] = n + 1
        return name if n == 0 else f"{name}_u{n}"

    def sb(self, name, shape, dtype):
        name = self._uniq(name)
        t = self.stack.enter_context(self.nc.sbuf_tensor(name, shape, dtype))
        return t, Buf(name)

    def ps(self, name, shape, dtype=F32):
        name = self._uniq(name)
        t = self.stack.enter_context(self.nc.psum_tensor(name, shape, dtype))
        return t, Buf(name)


def host_consts():
    c = {}
    c["c_ident"] = np.eye(128, dtype=BF)
    c["c_identf"] = np.eye(128, dtype=np.float32)
    c["c_ones1x"] = np.ones((1, 128), dtype=BF)
    c["c_ones1xf"] = np.ones((1, 128), dtype=np.float32)
    c["c_onesk"] = np.ones((128, 1), dtype=BF)
    c["c_oneskf"] = np.ones((128, 1), dtype=np.float32)
    c["c_onesb"] = np.ones((128, 128), dtype=BF)
    c["c_onesbf"] = np.ones((128, 128), dtype=np.float32)
    c["c_one11"] = np.ones((1, 1), dtype=BF)
    c["c_one11f"] = np.ones((1, 1), dtype=np.float32)
    b2 = np.zeros((128, 2), dtype=BF)
    b2[0:64, 0] = 1
    b2[64:128, 1] = 1
    c["c_blk2"] = b2
    c["c_blk2T"] = np.ascontiguousarray(b2.T)
    c["c_blk2f"] = b2.astype(np.float32)
    c["c_blk2Tf"] = np.ascontiguousarray(b2.T).astype(np.float32)
    ii = np.arange(128)
    c["c_maskd"] = np.where(ii[None, :] > ii[:, None], NEG, 0.0).astype(np.float32)
    mT = np.zeros((128, 4 * 512), dtype=np.float32)
    for r in range(4):
        j = ii[:, None]
        i = np.arange(512)[None, :]
        mT[:, r * 512:(r + 1) * 512] = np.where(i < 128 * r + j, NEG, 0.0)
    c["c_maskdT"] = mT
    c["c_masklow"] = np.where(ii[None, :] < ii[:, None], NEG, 0.0).astype(np.float32)
    mb = np.zeros((128, 4 * 512), dtype=np.float32)
    for r in range(4):
        j = ii[:, None]
        i = np.arange(512)[None, :]
        mb[:, r * 512:(r + 1) * 512] = np.where(i > 128 * r + j, NEG, 0.0)
    c["c_maskbT"] = mb
    rot = np.zeros((128, 128), dtype=BF)
    for blk in range(2):
        o = blk * 64
        for f in range(32):
            rot[o + f + 32, o + f] = -1
            rot[o + f, o + f + 32] = 1
    c["c_rot"] = rot
    c["c_iota"] = np.arange(128, dtype=np.float32).reshape(1, 128)
    c["c_ustrict"] = np.triu(np.ones((128, 128), dtype=BF), 1)
    c["c_fvals"] = np.tile(np.array([FVALS], dtype=np.float32), (128, 1))
    c["c_fvalsb"] = np.tile(np.array([FVALS], dtype=BF), (128, 1))
    return c


DRAM_CONSTS = ("c_maskdT", "c_maskbT", "c_maskd", "c_masklow")


def declare_consts(nc, ctx, stack):
    out = {}
    for name, arr in host_consts().items():
        dt = BF16 if arr.dtype == BF else F32
        d = nc.dram_tensor(name, list(arr.shape), dt, kind="ExternalInput").ap()
        b = Buf(name)
        if name in DRAM_CONSTS:
            out[name] = (d, b)
            continue
        t = stack.enter_context(nc.sbuf_tensor("sb_" + name, list(arr.shape), dt))
        ctx.dma("QS", t[:], d, writes=(b,))
        out[name] = (t, b)
    fv = out["c_fvals"][0]
    fvb = out["c_fvalsb"][0]
    for i, v in enumerate(FVALS):
        nc.const_aps.register(F32, float(v), fv[:, i:i + 1])
        nc.const_aps.register(BF16, float(v), fvb[:, i:i + 1])
    ctx.barrier()
    return out


def mm_group(ctx, out_ap, pieces, reads, wbuf):
    n = len(pieces)

    def fn(E):
        ins = None
        for i, (lhs, rhs) in enumerate(pieces):
            ins = E.matmul(out_ap, lhs, rhs, start=(i == 0), stop=(i == n - 1))
        return ins

    return ctx.emit("T", fn, reads=reads, writes=(wbuf,))


def row_to_col(ctx, consts, dst_ap, src_row_ap, reads, wbuf):
    one11, one11_b = consts["c_one11"]
    return ctx.emit("T", lambda E: E.matmul(dst_ap, src_row_ap, one11[:, :], start=True, stop=True),
                    reads=tuple(reads) + (one11_b,), writes=(wbuf,))


def ln_produce(ctx, consts, sc, x_dram_ap_fn, x_dram_bufs, xb_tiles, xn_tiles,
               eps=1e-5, n_pt=NPT, width=TT, xlo_tiles=None, zout=None):
    nc = ctx.nc
    ntc = width // 512
    (stg, stg_b), (stg2, stg2_b) = sc["stg"]
    x2s = sc["x2"]
    (rsum, rsum_b) = sc["rsum"]
    (rsq, rsq_b) = sc["rsq"]
    (rtmp, rtmp_b) = sc["rtmp"]
    (psum_s, psum_s_b) = sc["ps_a"]
    (psum_q, psum_q_b) = sc["ps_b"]
    (ps_bc, ps_bc_b) = sc["ps_c"]
    onesk, onesk_b = consts["c_onesk"]
    ones1xf, ones1xf_b = consts["c_ones1xf"]

    ctx.emit("V", lambda E: E.memset(rsum[:], 0.0), writes=(rsum_b,))
    ctx.emit("V", lambda E: E.memset(rsq[:], 0.0), writes=(rsq_b,))
    sq_jobs = []
    for pt in range(n_pt):
        sg, sg_b = (stg, stg_b) if pt % 2 == 0 else (stg2, stg2_b)
        xb_t, xb_b = xb_tiles[pt]
        ctx.dma("QS", sg[:], x_dram_ap_fn(pt), reads=(x_dram_bufs[pt],), writes=(sg_b,))
        ctx.emit("V", lambda E, sg=sg, xb_t=xb_t: E.tensor_copy(xb_t[:], sg[:]),
                 reads=(sg_b,), writes=(xb_b,))

        def do_mm(pt2, tc, x2, x2_b, xb_t=None, xb_b=None):
            t, b = xb_tiles[pt2]
            mm_group(ctx, psum_s[:], [(onesk[:, :], t[:, tc * 512:(tc + 1) * 512])],
                     reads=(onesk_b, b), wbuf=psum_s_b)
            mm_group(ctx, psum_q[:], [(onesk[:, :], x2[:, :])],
                     reads=(onesk_b, x2_b), wbuf=psum_q_b)
            s_tc = rsum[:, tc * 512:(tc + 1) * 512]
            q_tc = rsq[:, tc * 512:(tc + 1) * 512]
            ctx.emit("V", lambda E, s_tc=s_tc: E.tensor_tensor(s_tc, s_tc, psum_s[:], ALU.add),
                     reads=(psum_s_b, rsum_b), writes=(rsum_b,))
            ctx.emit("V", lambda E, q_tc=q_tc: E.tensor_tensor(q_tc, q_tc, psum_q[:], ALU.add),
                     reads=(psum_q_b, rsq_b), writes=(rsq_b,))

        for tc in range(ntc):
            x2, x2_b = x2s[(pt * ntc + tc) % len(x2s)]
            ctx.emit("S", lambda E, xb_t=xb_t, x2=x2, tc=tc:
                     E.activation(x2[:], xb_t[:, tc * 512:(tc + 1) * 512], AF.Square),
                     reads=(xb_b,), writes=(x2_b,))
            sq_jobs.append((pt, tc, x2, x2_b))
            if len(sq_jobs) > 2:
                do_mm(*sq_jobs.pop(0))
    while sq_jobs:
        do_mm(*sq_jobs.pop(0))
    dinv = 1.0 / (n_pt * 128)
    ctx.emit("V", lambda E: E.tensor_scalar(rsum[:], rsum[:], dinv, None, ALU.mult),
             reads=(rsum_b,), writes=(rsum_b,))
    ctx.emit("V", lambda E: E.tensor_scalar(rsq[:], rsq[:], dinv, None, ALU.mult),
             reads=(rsq_b,), writes=(rsq_b,))
    for tcc in range(ntc):
        sl = slice(tcc * 512, (tcc + 1) * 512)
        ctx.emit("V", lambda E, sl=sl: E.scalar_tensor_tensor(
            rtmp[:], rsum[:, sl], -1.0, rsum[:, sl], ALU.mult, ALU.mult),
            reads=(rsum_b,), writes=(rtmp_b,))
        ctx.emit("V", lambda E, sl=sl: E.tensor_tensor(rsq[:, sl], rsq[:, sl], rtmp[:], ALU.add),
                 reads=(rsq_b, rtmp_b), writes=(rsq_b,))
    ctx.emit("V", lambda E: E.tensor_scalar(rsq[:], rsq[:], float(eps), None, ALU.add),
             reads=(rsq_b,), writes=(rsq_b,))
    ctx.emit("S", lambda E: E.activation(rsq[:], rsq[:], AF.Ln),
             reads=(rsq_b,), writes=(rsq_b,))
    ctx.emit("S", lambda E: E.activation(rsq[:], rsq[:], AF.Exp, scale=-0.5),
             reads=(rsq_b,), writes=(rsq_b,))
    ctx.emit("V", lambda E: E.scalar_tensor_tensor(rsum[:], rsum[:], -1.0, rsq[:], ALU.mult, ALU.mult),
             reads=(rsum_b, rsq_b), writes=(rsum_b,))
    fstg = sc.get("fstg")
    (ps_bc2, ps_bc2_b) = sc["ps_d"]
    for tc in range(ntc):
        sl = slice(tc * 512, (tc + 1) * 512)
        mm_group(ctx, ps_bc[:], [(ones1xf[:, :], rsq[:, sl])],
                 reads=(ones1xf_b, rsq_b), wbuf=ps_bc_b)
        mm_group(ctx, ps_bc2[:], [(ones1xf[:, :], rsum[:, sl])],
                 reads=(ones1xf_b, rsum_b), wbuf=ps_bc2_b)
        for pt in range(n_pt):
            xb_t, xb_b = xb_tiles[pt]
            xn_t, xn_b = xn_tiles[pt]
            ft, ft_b = fstg[pt % 2]
            ctx.emit("V", lambda E, xb_t=xb_t, ft=ft, sl=sl:
                     E.tensor_tensor(ft[:], xb_t[:, sl], ps_bc[:], ALU.mult),
                     reads=(xb_b, ps_bc_b), writes=(ft_b,))
            ctx.emit("V", lambda E, ft=ft: E.tensor_tensor(ft[:], ft[:], ps_bc2[:], ALU.add),
                     reads=(ft_b, ps_bc2_b), writes=(ft_b,))
            if zout is not None:
                ctx.dma("QS", zout[0][pt * 128:(pt + 1) * 128, sl], ft[:],
                        reads=(ft_b,), writes=(zout[1],))
            if xlo_tiles is None:
                ctx.emit("V", lambda E, xn_t=xn_t, ft=ft, sl=sl:
                         E.tensor_copy(xn_t[:, sl], ft[:]),
                         reads=(ft_b,), writes=(xn_b,))
            else:
                xl_t, xl_b = xlo_tiles[pt]
                ctx.emit("V", lambda E, xn_t=xn_t, ft=ft, sl=sl:
                         E.tensor_copy(xn_t[:, sl], ft[:]),
                         reads=(ft_b,), writes=(xn_b,))
                ctx.emit("V", lambda E, xn_t=xn_t, xl_t=xl_t, ft=ft, sl=sl:
                         E.tensor_tensor(xl_t[:, sl], ft[:], xn_t[:, sl], ALU.subtract),
                         reads=(ft_b, xn_b), writes=(xl_b,))


def ln_scratch(ctx, tag, width=TT):
    sc = {}
    sc["stg"] = [ctx.sb(f"{tag}_stg{i}", [128, width], F32) for i in range(2)]
    sc["x2"] = [ctx.sb(f"{tag}_x2{i}", [128, 512], BF16) for i in range(4)]
    sc["rsum"] = ctx.sb(f"{tag}_rsum", [1, width], F32)
    sc["rsq"] = ctx.sb(f"{tag}_rsq", [1, width], F32)
    sc["rtmp"] = ctx.sb(f"{tag}_rtmp", [1, 512], F32)
    sc["fstg"] = [ctx.sb(f"{tag}_fs{i}", [128, 512], F32) for i in range(2)]
    sc["ps_a"] = ctx.ps(f"{tag}_psa", [1, 512], F32)
    sc["ps_d"] = ctx.ps(f"{tag}_psd", [128, 512], F32)
    sc["ps_b"] = ctx.ps(f"{tag}_psb", [1, 512], F32)
    sc["ps_c"] = ctx.ps(f"{tag}_psc", [128, 512], F32)
    return sc


def linear_fm(ctx, get_w, get_x, evict, n_k, obs, tcs, ps_pairs, tcw=512):
    i = 0
    for ob in obs:
        if ob_begin is not None:
            ob_begin(ob)
        for tc in tcs:
            ps_t, ps_b = ps_pairs[i % len(ps_pairs)]
            i += 1
            pieces = []
            reads = []
            for kc in range(n_k):
                wap, wb = get_w(kc, ob)
                xap, xb = get_x(kc, tc)
                pieces.append((wap, xap))
                reads += [wb, xb]
            fw = pieces[0][0].shape[-1] if hasattr(pieces[0][0], "shape") else 128
            mm_group(ctx, ps_t[0:fw, 0:tcw], pieces, reads=tuple(reads), wbuf=ps_b)
            evict(ob, tc, ps_t, ps_b)
        if ob_end is not None:
            ob_end(ob)


def bcast_row_f32(ctx, consts, ps_pair, dst_ap, dst_buf, row_ap, row_bufs):
    ones1xf, ones1xf_b = consts["c_ones1xf"]
    ps_t, ps_b = ps_pair
    n = 512
    mm_group(ctx, ps_t[:, 0:n], [(ones1xf[:, :], row_ap)],
             reads=(ones1xf_b,) + tuple(row_bufs), wbuf=ps_b)
    ctx.emit("V", lambda E: E.tensor_copy(dst_ap, ps_t[:, 0:n]),
             reads=(ps_b,), writes=(dst_buf,))


def rowsq_stats(ctx, consts, src_tiles, ps_row_pair, sq_stage, tc, n_pt=NPT):
    onesk, onesk_b = consts["c_onesk"]
    ps_t, ps_b = ps_row_pair
    sl = slice(tc * 512, (tc + 1) * 512)

    for pt in range(n_pt):
        st_t, st_b = sq_stage[pt % 2]
        t, b = src_tiles[pt]
        ctx.emit("S", lambda E, t=t, st_t=st_t: E.activation(st_t[:], t[:, sl], AF.Square),
                 reads=(b,), writes=(st_b,))

        def fn(E, st_t=st_t, pt=pt):
            return E.matmul(ps_t[:, :], onesk[:, :], st_t[:, :],
                            start=(pt == 0), stop=(pt == n_pt - 1))

        ctx.emit("T", fn, reads=(onesk_b, st_b), writes=(ps_b,))
    return ps_t, ps_b


def ternarize_weight(ctx, consts, tag, w_ap, w_buf, n_pt, cols,
                     wt_tiles, cache_dram_ap, cache_dram_buf):
    nc = ctx.nc
    oneskf, oneskf_b = consts["c_oneskf"]
    ones1xf, ones1xf_b = consts["c_ones1xf"]
    with ctx.scope():
        own = wt_tiles is None
        ccw = min(cols, 1024)
        ncc = cols // ccw
        stg = [ctx.sb(f"{tag}_stg{i}", [128, cols], F32) for i in range(2)]
        red = ctx.sb(f"{tag}_red", [128, 1], F32)
        acc = ctx.sb(f"{tag}_acc", [128, 1], F32)
        arow = ctx.sb(f"{tag}_arow", [1, 1], F32)
        thr = ctx.sb(f"{tag}_thr", [128, 1], F32)
        sgn = ctx.sb(f"{tag}_sgn", [128, ccw], F32)
        msk = ctx.sb(f"{tag}_msk", [128, ccw], F32)
        wt_local = None if not own else [ctx.sb(f"{tag}_wt{i}", [128, cols], BF16)
                                         for i in range(2)]
        ps_a = ctx.ps(f"{tag}_psa", [1, 1], F32)
        ps_f = ctx.ps(f"{tag}_psf", [128, 1], F32)

        ctx.emit("V", lambda E: E.memset(acc[0][:], 0.0), writes=(acc[1],))
        for rt in range(n_pt):
            sg, sg_b = stg[rt % 2]
            ctx.dma("QS", sg[:], w_ap[rt * 128:(rt + 1) * 128, :],
                    reads=(w_buf,), writes=(sg_b,))
            ctx.emit("V", lambda E, sg=sg: E.reduce_sum(red[0][:], sg[:], axis=AX,
                                                        apply_absolute_value=True),
                     reads=(sg_b,), writes=(red[1],))
            ctx.emit("V", lambda E: E.tensor_tensor(acc[0][:], acc[0][:], red[0][:], ALU.add),
                     reads=(acc[1], red[1]), writes=(acc[1],))
        mm_group(ctx, ps_a[0][:, :], [(oneskf[:, :], acc[0][:, :])],
                 reads=(oneskf_b, acc[1]), wbuf=ps_a[1])
        ctx.emit("V", lambda E: E.tensor_scalar(arow[0][:, :], ps_a[0][:, :],
                                                0.5 / (n_pt * 128 * cols), None, ALU.mult),
                 reads=(ps_a[1],), writes=(arow[1],))
        mm_group(ctx, ps_f[0][:, :], [(ones1xf[:, :], arow[0][:, :])],
                 reads=(ones1xf_b, arow[1]), wbuf=ps_f[1])
        ctx.emit("V", lambda E: E.tensor_copy(thr[0][:], ps_f[0][:, :]),
                 reads=(ps_f[1],), writes=(thr[1],))

        for rt in range(n_pt):
            sg, sg_b = stg[rt % 2]
            ctx.dma("QS", sg[:], w_ap[rt * 128:(rt + 1) * 128, :],
                    reads=(w_buf,), writes=(sg_b,))
            wt, wt_b = wt_tiles[rt] if not own else wt_local[rt % 2]
            for cc in range(ncc):
                cs = slice(cc * ccw, (cc + 1) * ccw)
                ctx.emit("S", lambda E, sg=sg, cs=cs: E.sign(sgn[0][:], sg[:, cs]),
                         reads=(sg_b,), writes=(sgn[1],))
                ctx.emit("S", lambda E, sg=sg, cs=cs: E.activation(msk[0][:], sg[:, cs], AF.Abs),
                         reads=(sg_b,), writes=(msk[1],))
                ctx.emit("V", lambda E: E.tensor_scalar(msk[0][:], msk[0][:], thr[0][:],
                                                        None, ALU.is_gt),
                         reads=(msk[1], thr[1]), writes=(msk[1],))
                ctx.emit("V", lambda E, wt=wt, cs=cs: E.tensor_tensor(
                    wt[:, cs], sgn[0][:], msk[0][:], ALU.mult),
                    reads=(sgn[1], msk[1]), writes=(wt_b,))
            ctx.dma("QS", cache_dram_ap[:, rt * cols:(rt + 1) * cols], wt[:],
                    reads=(wt_b,), writes=(cache_dram_buf,))


class ResidPt:

    def __init__(self, ctx, tag, xin_ap, xin_bufs, xout_ap, xout_bufs, c0=0,
                 cw=TT, nbuf=2):
        self.ctx = ctx
        self.xin_ap, self.xin_bufs = xin_ap, xin_bufs
        self.xout_ap, self.xout_bufs = xout_ap, xout_bufs
        self.c0, self.cw = c0, cw
        self.tiles = [ctx.sb(f"{tag}_rp{i}", [128, cw], F32) for i in range(nbuf)]
        self.cur = {}

    def begin(self, ob):
        t, b = self.tiles[ob % len(self.tiles)]
        self.cur[ob] = (t, b)
        self.ctx.dma("QS", t[:],
                     self.xin_ap[ob * 128:(ob + 1) * 128, self.c0:self.c0 + self.cw],
                     reads=(self.xin_bufs[ob],), writes=(b,))

    def sl(self, ob, tc):
        t, b = self.cur[ob]
        return t[:, tc * 512 - self.c0:(tc + 1) * 512 - self.c0], b

    def end(self, ob):
        t, b = self.cur[ob]
        self.ctx.dma("QS",
                     self.xout_ap[ob * 128:(ob + 1) * 128, self.c0:self.c0 + self.cw],
                     t[:], reads=(b,), writes=(self.xout_bufs[ob],))


def resid_evict(ctx, xin_ap, xin_bufs, xout_ap, xout_bufs, xsl_pool, ob, tc,
                prep, scale=0.5, width=512):
    xt, xb = xsl_pool[(ob * 8 + tc) % len(xsl_pool)]
    sl = slice(tc * width, (tc + 1) * width)
    ctx.dma("QS", xt[:, 0:width], xin_ap[ob * 128:(ob + 1) * 128, sl],
            reads=(xin_bufs[ob],), writes=(xb,))
    prep(xt[:, 0:width], xb)
    ctx.dma("QS", xout_ap[ob * 128:(ob + 1) * 128, sl], xt[:, 0:width],
            reads=(xb,), writes=(xout_bufs[ob],))


def emit_l0_ffn(ctx, consts, X, W, width=TT):
    nc = ctx.nc
    xin_ap, xin_bufs = X["in"]
    xout_ap, xout_bufs = X["out"]
    DF = 4096
    ntc = width // 512
    onesk, onesk_b = consts["c_onesk"]
    with ctx.scope():
        xn = [ctx.sb(f"l0f_xn{p}", [128, width], BF16) for p in range(NPT)]
        c1b = ctx.sb("l0f_c1b", [128, width], F32)
        with ctx.scope():
            sc = ln_scratch(ctx, "l0f_ln", width)
            ln_produce(ctx, consts, sc, lambda pt: xin_ap[pt * 128:(pt + 1) * 128, :],
                       xin_bufs, xn, xn, width=width)
        with ctx.scope():
            sqst = [ctx.sb(f"l0f_sq{i}", [128, 512], BF16) for i in range(2)]
            psr = ctx.ps("l0f_psr", [1, 512], F32)
            psc = ctx.ps("l0f_psc", [128, 512], F32)
            rrow = ctx.sb("l0f_rms", [1, width], F32)
            for tc in range(ntc):
                rowsq_stats(ctx, consts, xn, psr, sqst, tc)
                rt, rb = rrow
                sl = slice(tc * 512, (tc + 1) * 512)
                ctx.emit("S", lambda E, sl=sl: E.activation(rt[:, sl], psr[0][:, :],
                                                            AF.Sqrt, scale=1.0 / DM),
                         reads=(psr[1],), writes=(rb,))
                ctx.emit("V", lambda E, sl=sl: E.tensor_scalar(rt[:, sl], rt[:, sl],
                                                               1e-8, None, ALU.add),
                         reads=(rb,), writes=(rb,))
                ctx.emit("V", lambda E, sl=sl: E.reciprocal(rt[:, sl], rt[:, sl]),
                         reads=(rb,), writes=(rb,))
                bcast_row_f32(ctx, consts, psc, c1b[0][:, sl], c1b[1], rt[:, sl], (rb,))
        if W["first"]:
            ternarize_weight(ctx, consts, "l0f_td", W["w_downT"][0], W["w_downT"][1],
                             32, 1024, None, W["cd"][0], W["cd"][1])
        wu = [ctx.sb(f"l0f_wu{p}", [128, DF], BF16) for p in range(NPT)]
        if W["first"]:
            ternarize_weight(ctx, consts, "l0f_tu", W["w_upT"][0], W["w_upT"][1],
                             NPT, DF, wu, W["cu"][0], W["cu"][1])
        else:
            for p in range(NPT):
                ctx.dma("QP", wu[p][0][:], W["cu"][0][:, p * DF:(p + 1) * DF],
                        reads=(W["cu"][1],), writes=(wu[p][1],))
        with ctx.scope():
            hb = [ctx.sb(f"l0f_h{i}", [128, 512], BF16) for i in range(32)]
            t1 = [ctx.sb(f"l0f_t1{i}", [128, 512], F32) for i in range(2)]
            h2 = [ctx.sb(f"l0f_h2{i}", [128, 512], BF16) for i in range(3)]
            wd = [ctx.sb(f"l0f_wd{i}", [128, 4096], BF16) for i in range(1)]
            xsl = [ctx.sb(f"l0f_xsl{i}", [128, 512], F32) for i in range(4)]
            c2b = ctx.sb("l0f_c2b", [128, 512], F32)
            c2r = ctx.sb("l0f_c2r", [1, 512], F32)
            psm = [ctx.ps(f"l0f_mm{i}", [128, 512]) for i in range(2)]
            psd = [ctx.ps(f"l0f_dn{i}", [128, 512]) for i in range(4)]
            psq = ctx.ps("l0f_psq", [1, 512])
            psb2 = ctx.ps("l0f_psb2", [128, 512])
            for tc in range(ntc):
                sl = slice(tc * 512, (tc + 1) * 512)
                sqq = []

                def do_sq(fb, st_t, st_b):
                    ctx.emit("T", lambda E, st_t=st_t, fb=fb: E.matmul(
                        psq[0][:, :], onesk[:, :], st_t[:],
                        start=(fb == 0), stop=(fb == 31)),
                        reads=(onesk_b, st_b), writes=(psq[1],))

                for fb in range(32):
                    ps_t, ps_b = psm[fb % 2]
                    pieces = [(wu[kc][0][:, fb * 128:(fb + 1) * 128], xn[kc][0][:, sl])
                              for kc in range(NPT)]
                    reads = tuple(wu[kc][1] for kc in range(NPT)) + \
                        tuple(xn[kc][1] for kc in range(NPT))
                    mm_group(ctx, ps_t[:, :], pieces, reads=reads, wbuf=ps_b)
                    tt, tb = t1[fb % 2]
                    ctx.emit("V", lambda E, tt=tt, ps_t=ps_t, sl=sl:
                             E.tensor_tensor(tt[:], ps_t[:, :], c1b[0][:, sl], ALU.mult),
                             reads=(ps_b, c1b[1]), writes=(tb,))
                    ht, hbuf = hb[fb]
                    ctx.emit("S", lambda E, ht=ht, tt=tt: E.activation(ht[:], tt[:], AF.Gelu),
                             reads=(tb,), writes=(hbuf,))
                    st_t, st_b = h2[fb % len(h2)]
                    ctx.emit("S", lambda E, st_t=st_t, ht=ht:
                             E.activation(st_t[:], ht[:], AF.Square),
                             reads=(hbuf,), writes=(st_b,))
                    sqq.append((fb, st_t, st_b))
                    if len(sqq) > 2:
                        do_sq(*sqq.pop(0))
                while sqq:
                    do_sq(*sqq.pop(0))
                ctx.emit("S", lambda E: E.activation(c2r[0][:, :], psq[0][:, :],
                                                     AF.Sqrt, scale=1.0 / DF),
                         reads=(psq[1],), writes=(c2r[1],))
                ctx.emit("V", lambda E: E.tensor_scalar(c2r[0][:, :], c2r[0][:, :],
                                                        1e-8, None, ALU.add),
                         reads=(c2r[1],), writes=(c2r[1],))
                ctx.emit("V", lambda E: E.reciprocal(c2r[0][:, :], c2r[0][:, :]),
                         reads=(c2r[1],), writes=(c2r[1],))
                bcast_row_f32(ctx, consts, psb2, c2b[0][:, :], c2b[1],
                              c2r[0][:, :], (c2r[1],))
                for half in range(2):
                    obs = list(range(half * 4, half * 4 + 4))
                    for j, ob in enumerate(obs):
                        xt, xb = xsl[j % len(xsl)] if False else xsl[j]
                        ctx.dma("QS", xt[:], xin_ap[ob * 128:(ob + 1) * 128,
                                                    tc * 512:(tc + 1) * 512],
                                reads=(xin_bufs[ob],), writes=(xb,))
                    for g in range(8):
                        wdt, wdb = wd[0]
                        ctx.dma("QS", wdt[:], W["cd"][0][:, g * 4096:(g + 1) * 4096],
                                reads=(W["cd"][1],), writes=(wdb,))
                        for k4 in range(4):
                            kc = g * 4 + k4
                            for j, ob in enumerate(obs):
                                ctx.emit("T", lambda E, wdt=wdt, ob=ob, j=j, kc=kc, k4=k4:
                                         E.matmul(
                                    psd[j][0][:, :],
                                    wdt[:, k4 * 1024 + ob * 128:k4 * 1024 + (ob + 1) * 128],
                                    hb[kc][0][:, :], start=(kc == 0), stop=(kc == 31)),
                                    reads=(wdb, hb[kc][1]), writes=(psd[j][1],))
                    for j, ob in enumerate(obs):
                        xt, xb = xsl[j]
                        tt, tb = t1[j % 2]
                        ctx.emit("V", lambda E, tt=tt, j=j: E.tensor_tensor(
                            tt[:], psd[j][0][:, :], c2b[0][:, :], ALU.mult),
                            reads=(psd[j][1], c2b[1]), writes=(tb,))
                        ctx.emit("V", lambda E, xt=xt, tt=tt:
                                 E.scalar_tensor_tensor(xt[:], tt[:], 0.5, xt[:],
                                                        ALU.mult, ALU.add),
                                 reads=(tb, xb), writes=(xb,))
                        ctx.dma("QS", xout_ap[ob * 128:(ob + 1) * 128,
                                              tc * 512:(tc + 1) * 512], xt[:],
                                reads=(xb,), writes=(xout_bufs[ob],))


def build_sublayer_nc(emit_fn, weight_specs, internal_specs, width=TT,
                      extra_outputs=()):
    nc = bass.Bass("TRN2", target_bir_lowering=False)
    with ExitStack() as stack:
        ctx = Ctx(nc, stack)
        xin = nc.dram_tensor("xres_in", [DM, width], F32, kind="ExternalInput").ap()
        xout = nc.dram_tensor("xres_out", [DM, width], F32, kind="ExternalOutput").ap()
        X = {"in": (xin, [Buf(f"xin{p}") for p in range(NPT)]),
             "out": (xout, [Buf(f"xout{p}") for p in range(NPT)])}
        W = {}
        for name, shape, dt in weight_specs:
            ap = nc.dram_tensor(name, list(shape), dt, kind="ExternalInput").ap()
            W[name] = (ap, Buf(name))
        for name, shape, dt in internal_specs:
            ap = nc.dram_tensor(name, list(shape), dt, kind="Internal").ap()
            W[name] = (ap, Buf(name))
        for name, shape, dt in extra_outputs:
            ap = nc.dram_tensor(name, list(shape), dt, kind="ExternalOutput").ap()
            W[name] = (ap, Buf(name))
        consts = declare_consts(nc, ctx, stack)
        emit_fn(ctx, consts, X, W)
        ctx.barrier()
    return nc


def run_sublayer(nc, xres_in, weights, n_core=1):
    ins = dict(weights)
    ins["xres_in"] = np.ascontiguousarray(xres_in)
    for k, v in host_consts().items():
        ins[k] = v
    N_LAUNCH[0] += 1
    res = run_bass_kernel_spmd(nc, [ins] * n_core, core_ids=list(range(n_core)))
    return [r["xres_out"] for r in res.results]


def l0_ffn_nc(first=True, width=TT):
    def fn(ctx, consts, X, W):
        Wd = {"w_upT": W["w_upT"], "w_downT": W["w_downT"], "cu": W["l0f_cu"],
              "cd": W["l0f_cd"], "first": first}
        emit_l0_ffn(ctx, consts, X, Wd, width=width)
    return build_sublayer_nc(
        fn,
        [("w_upT", [1024, 4096], F32), ("w_downT", [4096, 1024], F32)],
        [("l0f_cu", [128, 8 * 4096], BF16), ("l0f_cd", [128, 32 * 1024], BF16)],
        width=width)


def attn64_core(ctx, consts, tag, q_dram, k_dram, v_dram, z_dram, heads, width,
                causal, local=False):
    nc = ctx.nc
    ntt = width // 128
    nqb = width // 512
    hstride = ntt * 64
    h65 = ntt * 65
    ones1x, ones1x_b = consts["c_ones1x"]
    onesbf, onesbf_b = consts["c_onesbf"]
    ident, ident_b = consts["c_ident"]
    identf, identf_b = consts["c_identf"]
    with ctx.scope():
        q2t = [ctx.sb(f"{tag}_q2t{i}", [128, width], BF16) for i in range(2)]
        q2l = [ctx.sb(f"{tag}_q2l{i}", [128, width], BF16) for i in range(2)]
        k2t = [ctx.sb(f"{tag}_k2t{i}", [128, width], BF16) for i in range(2)]
        k2l = [ctx.sb(f"{tag}_k2l{i}", [128, width], BF16) for i in range(2)]
        vt_p = [ctx.sb(f"{tag}_vt{i}", [128, h65], BF16) for i in range(4)]
        ptile = [ctx.sb(f"{tag}_pt{i}", [128, 512], BF16) for i in range(4)]
        mrow = ctx.sb(f"{tag}_mrow", [1, 1024], BF16)
        mcf = ctx.sb(f"{tag}_mcf", [128, 8], F32)
        mcb = ctx.sb(f"{tag}_mcb", [128, 8], F32)
        tmpc = ctx.sb(f"{tag}_tmpc", [128, 2], F32)
        rdt = ctx.sb(f"{tag}_rdt", [65, 512], F32)
        rdsb = ctx.sb(f"{tag}_rdsb", [64, 512], F32)
        zstg = [ctx.sb(f"{tag}_zstg{i}", [64, 512], BF16) for i in range(2)]
        ps_s = ctx.ps(f"{tag}_pss", [128, 2048], F32)
        ps_s_h = [Buf(f"{tag}_pssh{i}") for i in range(4)]
        ps_st = [ctx.ps(f"{tag}_psst{i}", [128, 512], F32) for i in range(2)]
        ps_pv = ctx.ps(f"{tag}_pspv", [65, 512], F32)
        ps_pv2 = ctx.ps(f"{tag}_pspv2", [65, 512], F32)
        ps_rd = ps_st[1]
        mdiag = mTs = mlow = mbTs = None
        if causal or local:
            mdiag = ctx.sb(f"{tag}_mdg", [128, 128], F32)
            mTs = [ctx.sb(f"{tag}_mT{r}", [128, 512], F32) for r in range(4)]
            cm, cm_b = consts["c_maskd"]
            ctx.dma("QS", mdiag[0][:], cm, reads=(cm_b,), writes=(mdiag[1],))
            cmT, cmT_b = consts["c_maskdT"]
            for r in range(4):
                ctx.dma("QS", mTs[r][0][:], cmT[:, r * 512:(r + 1) * 512],
                        reads=(cmT_b,), writes=(mTs[r][1],))
        if local:
            mlow = ctx.sb(f"{tag}_mlw", [128, 128], F32)
            cl, cl_b = consts["c_masklow"]
            ctx.dma("QS", mlow[0][:], cl, reads=(cl_b,), writes=(mlow[1],))
            mbTs = [ctx.sb(f"{tag}_mbT{r}", [128, 512], F32) for r in range(4)]
            cb, cb_b = consts["c_maskbT"]
            for r in range(4):
                ctx.dma("QS", mbTs[r][0][:], cb[:, r * 512:(r + 1) * 512],
                        reads=(cb_b,), writes=(mbTs[r][1],))
        lo0 = heads * 64

        for hp in range(heads // 2):
            qt_t, qt_b = q2t[hp % 2]
            ql_t, ql_b = q2l[hp % 2]
            kt_t, kt_b = k2t[hp % 2]
            kl_t, kl_b = k2l[hp % 2]
            vts = [vt_p[(2 * hp + u) % 4] for u in range(2)]
            for u in range(2):
                h = 2 * hp + u
                pb = slice(64 * u, 64 * u + 64)
                ctx.dma("QS", qt_t[pb, :], q_dram[0][h * 64:(h + 1) * 64, :],
                        reads=(q_dram[1],), writes=(qt_b,))
                ctx.dma("QS", ql_t[pb, :], q_dram[0][lo0 + h * 64:lo0 + (h + 1) * 64, :],
                        reads=(q_dram[1],), writes=(ql_b,))
                ctx.dma("QS", kt_t[pb, :], k_dram[0][h * 64:(h + 1) * 64, :],
                        reads=(k_dram[1],), writes=(kt_b,))
                ctx.dma("QS", kl_t[pb, :], k_dram[0][lo0 + h * 64:lo0 + (h + 1) * 64, :],
                        reads=(k_dram[1],), writes=(kl_b,))
                vt, vt_b = vts[u]
                ctx.emit("V", lambda E, vt=vt: E.memset(
                    vt[:].rearrange("p (c s) -> p c s", s=65)[:, :, 64:65], 1.0),
                    writes=(vt_b,))
                ctx.dma("QP", vt[:].rearrange("p (c s) -> p c s", s=65)[:, :, 0:64],
                        v_dram[0][:, h * hstride:(h + 1) * hstride].rearrange(
                            "p (c n) -> p c n", n=64),
                        reads=(v_dram[1],), writes=(vt_b,))
            for qb in range(nqb):
                for qt4 in range(4):
                    qt = qb * 4 + qt4
                    if local:
                        kj_lo = max(0, qt - 4) * 128
                        nk = 128 * (qt + 1)
                    else:
                        kj_lo = 0
                        nk = 128 * (qt + 1) if causal else width
                    first = True
                    for ri, k0 in enumerate(range(kj_lo, nk, 512)):
                        kw = min(512, nk - k0)
                        bk = ri % 2
                        pssA = ps_s[0][:, bk * 512:bk * 512 + 512]
                        pssB = ps_s[0][:, 1024 + bk * 512:1024 + bk * 512 + 512]
                        wbA, wbB = ps_s_h[bk], ps_s_h[2 + bk]

                        def fn(E, k0=k0, kw=kw, qt=qt, pssA=pssA, pssB=pssB):
                            E.matmul(pssA[:, 0:kw], qt_t[0:64, 128 * qt:128 * qt + 128],
                                     kt_t[0:64, k0:k0 + kw], start=True, stop=True)
                            return E.matmul(pssB[:, 0:kw],
                                            qt_t[64:128, 128 * qt:128 * qt + 128],
                                            kt_t[64:128, k0:k0 + kw],
                                            start=True, stop=True)

                        ctx.emit("T", fn, reads=(qt_b, kt_b), writes=(wbA, wbB))
                        for u, (pss, wb) in enumerate(((pssA, wbA), (pssB, wbB))):
                            if (causal or local) and k0 + kw == nk:
                                ctx.emit("V", lambda E, kw=kw, pss=pss: E.tensor_tensor(
                                    pss[:, kw - 128:kw], pss[:, kw - 128:kw],
                                    mdiag[0][:], ALU.add),
                                    reads=(wb, mdiag[1]), writes=(wb,))
                            if local and k0 == kj_lo and qt >= 4:
                                ctx.emit("V", lambda E, pss=pss: E.tensor_tensor(
                                    pss[:, 0:128], pss[:, 0:128],
                                    mlow[0][:], ALU.add),
                                    reads=(wb, mlow[1]), writes=(wb,))
                            ci = u * 4 + qt4
                            if first and u == 1:
                                first = False
                            if ri == 0:
                                ctx.emit("V", lambda E, kw=kw, ci=ci, pss=pss:
                                         E.reduce_max(
                                    mcf[0][:, ci:ci + 1], pss[:, 0:kw], axis=AX),
                                    reads=(wb,), writes=(mcf[1],))
                            else:
                                ctx.emit("V", lambda E, kw=kw, u=u, pss=pss:
                                         E.reduce_max(
                                    tmpc[0][:, u:u + 1], pss[:, 0:kw], axis=AX),
                                    reads=(wb,), writes=(tmpc[1],))
                                ctx.emit("V", lambda E, ci=ci, u=u: E.tensor_tensor(
                                    mcf[0][:, ci:ci + 1], mcf[0][:, ci:ci + 1],
                                    tmpc[0][:, u:u + 1], ALU.max),
                                    reads=(mcf[1], tmpc[1]), writes=(mcf[1],))
                ctx.emit("V", lambda E: E.tensor_scalar(mcb[0][:], mcf[0][:], -1.0,
                                                        None, ALU.mult),
                         reads=(mcf[1],), writes=(mcb[1],))
                for u in range(2):
                    for qt4 in range(4):
                        ci = u * 4 + qt4
                        ctx.emit("T", lambda E, ci=ci: E.transpose(
                            ps_s[0][0:1, 1920:2048], mcb[0][:, ci:ci + 1], identf[:, :]),
                            reads=(mcb[1], identf_b), writes=(ps_s_h[3],))
                        ctx.emit("V", lambda E, u=u, qt4=qt4: E.tensor_copy(
                            mrow[0][0:1, u * 512 + qt4 * 128:u * 512 + (qt4 + 1) * 128],
                            ps_s[0][0:1, 1920:2048]),
                            reads=(ps_s_h[3],), writes=(mrow[1],))
                if local:
                    kc0 = max(0, 4 * qb - 4)
                    kc1 = 4 * qb + 4
                else:
                    kc0 = 0
                    kc1 = 4 * (qb + 1) if causal else ntt
                kcs = list(range(kc0, kc1))
                qsl = slice(512 * qb, 512 * qb + 512)
                n2 = 2 * len(kcs)
                st_slots = [(ps_st[0][0], ps_st[0][1]), (ps_st[1][0], ps_st[1][1]),
                            (None, None), (None, None)]

                def st_slot(i2):
                    j = i2 % 4
                    if j < 2:
                        return ps_st[j][0][:, :], ps_st[j][1]
                    return ps_s[0][:, (j - 2) * 512:(j - 1) * 512], ps_s_h[j - 2]

                def emit_st(i2):
                    kc, u = kcs[i2 // 2], i2 % 2
                    pst, pst_b = st_slot(i2)
                    ksl = slice(128 * kc, 128 * kc + 128)
                    ph = slice(64 * u, 64 * u + 64)
                    pieces = [(ones1x[:, :], mrow[0][0:1, u * 512:u * 512 + 512]),
                              (kt_t[ph, ksl], qt_t[ph, qsl]),
                              (kt_t[ph, ksl], ql_t[ph, qsl]),
                              (kl_t[ph, ksl], qt_t[ph, qsl])]
                    mm_group(ctx, pst, pieces,
                             reads=(ones1x_b, mrow[1], qt_b, kt_b, ql_b, kl_b),
                             wbuf=pst_b)
                    if local:
                        rr = kc - (4 * qb - 4)
                        mt = mbTs[rr] if rr < 4 else mTs[rr - 4]
                        ctx.emit("V", lambda E, pst=pst, mt=mt: E.tensor_tensor(
                            pst, pst, mt[0][:], ALU.add),
                            reads=(pst_b, mt[1]), writes=(pst_b,))
                    elif causal and kc >= 4 * qb:
                        r = kc - 4 * qb
                        ctx.emit("V", lambda E, pst=pst, r=r: E.tensor_tensor(
                            pst, pst, mTs[r][0][:], ALU.add),
                            reads=(pst_b, mTs[r][1]), writes=(pst_b,))

                def emit_exp_pv(i2):
                    kc, u = kcs[i2 // 2], i2 % 2
                    ki = i2 // 2
                    pst, pst_b = st_slot(i2)
                    pt, pt_b = ptile[i2 % 4]
                    ctx.emit("S", lambda E, pt=pt, pst=pst: E.activation(
                        pt[:], pst, AF.Exp),
                        reads=(pst_b,), writes=(pt_b,))
                    vt, vt_b = vts[u]
                    pvdst = ps_pv[0][:, :] if u == 0 else ps_pv2[0][:, :]
                    pvb = ps_pv[1] if u == 0 else ps_pv2[1]
                    ctx.emit("T", lambda E, pt=pt, kc=kc, ki=ki, vt=vt, pvdst=pvdst:
                             E.matmul(pvdst, vt[:, kc * 65:(kc + 1) * 65], pt[:],
                                      start=(ki == 0), stop=(ki == len(kcs) - 1)),
                             reads=(pt_b, vt_b), writes=(pvb,))

                for i2 in range(n2):
                    emit_st(i2)
                    if i2 >= 2:
                        emit_exp_pv(i2 - 2)
                for i2 in range(max(0, n2 - 2), n2):
                    emit_exp_pv(i2)
                for u in range(2):
                    h = 2 * hp + u
                    pv = ps_pv if u == 0 else ps_pv2
                    ctx.emit("V", lambda E, pv=pv: E.reciprocal(
                        rdt[0][64:65, :], pv[0][64:65, :]),
                        reads=(pv[1],), writes=(rdt[1],))
                    ctx.emit("T", lambda E: E.matmul(
                        ps_rd[0][0:64, :], onesbf[64:65, 0:64], rdt[0][64:65, :],
                        start=True, stop=True),
                        reads=(rdt[1], onesbf_b), writes=(ps_rd[1],))
                    ctx.emit("V", lambda E: E.tensor_copy(rdsb[0][:], ps_rd[0][0:64, :]),
                             reads=(ps_rd[1],), writes=(rdsb[1],))
                    zs, zs_b = zstg[u]
                    ctx.emit("V", lambda E, zs=zs, pv=pv: E.tensor_tensor(
                        zs[:], pv[0][0:64, :], rdsb[0][:], ALU.mult),
                        reads=(pv[1], rdsb[1]), writes=(zs_b,))
                    ctx.dma("QS", z_dram[0][h * 64:(h + 1) * 64, qb * 512:(qb + 1) * 512],
                            zs[:], reads=(zs_b,), writes=(z_dram[1],))


def fm_linear(ctx, w_tiles, x_tiles, n_k, obs, tcs, ps_pairs, evict,
              extra_pieces=None, ob_begin=None, ob_end=None):
    i = 0
    for ob in obs:
        if ob_begin is not None:
            ob_begin(ob)
        for tc in tcs:
            ps_t, ps_b = ps_pairs[i % len(ps_pairs)]
            i += 1
            pieces = [(w_tiles[kc][0][:, ob * 128:(ob + 1) * 128],
                       x_tiles[kc][0][:, tc * 512:(tc + 1) * 512])
                      for kc in range(n_k)]
            reads = tuple(w_tiles[kc][1] for kc in range(n_k)) + \
                tuple(x_tiles[kc][1] for kc in range(n_k))
            if extra_pieces is not None:
                ep, eb = extra_pieces(tc)
                pieces += ep
                reads += tuple(eb)
            mm_group(ctx, ps_t[:, :], pieces, reads=reads, wbuf=ps_b)
            evict(ob, tc, ps_t, ps_b)
        if ob_end is not None:
            ob_end(ob)


def emit_l1_attn(ctx, consts, X, W, width=TT):
    nc = ctx.nc
    xin_ap, xin_bufs = X["in"]
    xout_ap, xout_bufs = X["out"]
    ntc = width // 512
    ntt = width // 128
    heads = 16
    hstride = ntt * 64
    ones1x, ones1x_b = consts["c_ones1x"]
    blk2, blk2_b = consts["c_blk2f"]
    blk2T, blk2T_b = consts["c_blk2Tf"]
    with ctx.scope():
      with ctx.scope():
        xn = [ctx.sb(f"l1a_xn{p}", [128, width], BF16) for p in range(NPT)]
        xl = [ctx.sb(f"l1a_xl{p}", [128, width], BF16) for p in range(NPT)]
        with ctx.scope():
            sc = ln_scratch(ctx, "l1a_ln", width)
            ln_produce(ctx, consts, sc, lambda pt: xin_ap[pt * 128:(pt + 1) * 128, :],
                       xin_bufs, xn, xn, width=width, xlo_tiles=xl)
        with ctx.scope():
            wqk = [ctx.sb(f"l1a_wqk{p}", [128, 1024], BF16) for p in range(NPT)]
            wql = [ctx.sb(f"l1a_wql{p}", [128, 1024], BF16) for p in range(NPT)]
            wfs = [ctx.sb(f"l1a_wfs{i}", [128, 1024], F32) for i in range(2)]

            def load_sec_w(sec):
                for p in range(NPT):
                    csl = slice(sec * 1024, (sec + 1) * 1024)
                    ctx.dma("QP", wqk[p][0][:], W["wqkvT"][0][p * 128:(p + 1) * 128, csl],
                            reads=(W["wqkvT"][1],), writes=(wqk[p][1],))
                    wf, wf_b = wfs[p % 2]
                    ctx.dma("QS", wf[:], W["wqkvT"][0][p * 128:(p + 1) * 128, csl],
                            reads=(W["wqkvT"][1],), writes=(wf_b,))
                    ctx.emit("V", lambda E, p=p, wf=wf: E.tensor_tensor(
                        wql[p][0][:], wf[:], wqk[p][0][:], ALU.subtract),
                        reads=(wf_b, wqk[p][1]), writes=(wql[p][1],))
            bqk = ctx.sb("l1a_bqk", [128, 16], F32)
            ctx.dma("QS", bqk[0][:], W["bqk_cols"][0], reads=(W["bqk_cols"][1],),
                    writes=(bqk[1],))
            wqwk = ctx.sb("l1a_wqwk", [128, 2], F32)
            ctx.dma("QS", wqwk[0][:], W["wqwk128"][0], reads=(W["wqwk128"][1],),
                    writes=(wqwk[1],))
            scol = ctx.sb("l1a_scol", [128, 1], F32)
            ctx.emit("V", lambda E: E.tensor_tensor(scol[0][:], wqwk[0][:, 0:1],
                                                    wqwk[0][:, 1:2], ALU.mult),
                     reads=(wqwk[1],), writes=(scol[1],))
            qf = [ctx.sb(f"l1a_qf{i}", [128, 512], F32) for i in range(3)]
            qlo_st = [ctx.sb(f"l1a_qlo{i}", [128, 512], BF16) for i in range(3)]
            q2 = [ctx.sb(f"l1a_q2{i}", [128, 512], F32) for i in range(3)]
            qo = [ctx.sb(f"l1a_qo{i}", [128, 512], BF16) for i in range(3)]
            rr = ctx.sb("l1a_rr", [2, 512], F32)
            psm = [ctx.ps(f"l1a_mm{i}", [128, 512]) for i in range(3)]
            ps2 = ctx.ps("l1a_ps2", [2, 512])
            psb = ctx.ps("l1a_psb", [128, 512])

            def qk_evict(sec):
                def evict(ob, tc, gi, ps_t, ps_b):
                    f, f_b = qf[gi % 3]
                    sq, sq_b = q2[gi % 3]
                    bcol = bqk[0][:, sec * 8 + ob:sec * 8 + ob + 1]
                    ctx.emit("V", lambda E: E.tensor_scalar(f[:], ps_t[:, :], bcol,
                                                            None, ALU.add),
                             reads=(ps_b, bqk[1]), writes=(f_b,))
                    ctx.emit("S", lambda E: E.activation(sq[:], f[:], AF.Square),
                             reads=(f_b,), writes=(sq_b,))
                    mm_group(ctx, ps2[0][:, :], [(blk2[:, :], sq[:])],
                             reads=(blk2_b, sq_b), wbuf=ps2[1])
                    ctx.emit("V", lambda E: E.tensor_scalar(rr[0][:, :], ps2[0][:, :],
                                                            1.0 / 64, 1e-6, ALU.mult, ALU.add),
                             reads=(ps2[1],), writes=(rr[1],))
                    ctx.emit("S", lambda E: E.activation(rr[0][:, :], rr[0][:, :], AF.Ln),
                             reads=(rr[1],), writes=(rr[1],))
                    ctx.emit("S", lambda E: E.activation(rr[0][:, :], rr[0][:, :],
                                                         AF.Exp, scale=-0.5),
                             reads=(rr[1],), writes=(rr[1],))
                    mm_group(ctx, psb[0][:, :], [(blk2T[:, :], rr[0][:, :])],
                             reads=(blk2T_b, rr[1]), wbuf=psb[1])
                    ctx.emit("V", lambda E: E.tensor_tensor(f[:], f[:], psb[0][:, :],
                                                            ALU.mult),
                             reads=(f_b, psb[1]), writes=(f_b,))
                    zs, zs_b = qo[gi % 3]
                    zl, zl_b = qlo_st[gi % 3]
                    if sec == 0:
                        ctx.emit("V", lambda E: E.tensor_scalar(
                            f[:], f[:], scol[0][:, 0:1], None, ALU.mult),
                            reads=(f_b, scol[1]), writes=(f_b,))
                    ctx.emit("V", lambda E: E.tensor_copy(zs[:], f[:]),
                             reads=(f_b,), writes=(zs_b,))
                    ctx.emit("V", lambda E: E.tensor_tensor(zl[:], f[:], zs[:], ALU.subtract),
                             reads=(f_b, zs_b), writes=(zl_b,))
                    dram = W["q_sc"] if sec == 0 else W["k_sc"]
                    for half in range(2):
                        h = ob * 2 + half
                        ctx.dma("QS", dram[0][h * 64:(h + 1) * 64, tc * 512:(tc + 1) * 512],
                                zs[64 * half:64 * half + 64, :], reads=(zs_b,),
                                writes=(dram[1],))
                        ctx.dma("QS", dram[0][1024 + h * 64:1024 + (h + 1) * 64,
                                              tc * 512:(tc + 1) * 512],
                                zl[64 * half:64 * half + 64, :], reads=(zl_b,),
                                writes=(dram[1],))
                return evict

            for sec in range(2):
                load_sec_w(sec)
                ev = qk_evict(sec)
                from collections import deque
                pend = deque()
                gi = 0
                for ob in range(8):
                    col = slice(ob * 128, ob * 128 + 128)
                    for tc in range(ntc):
                        ps_t, ps_b = psm[gi % 3]
                        tsl = slice(tc * 512, (tc + 1) * 512)
                        pieces = []
                        reads = []
                        for kc in range(NPT):
                            pieces += [(wqk[kc][0][:, col], xn[kc][0][:, tsl]),
                                       (wqk[kc][0][:, col], xl[kc][0][:, tsl]),
                                       (wql[kc][0][:, col], xn[kc][0][:, tsl])]
                            reads += [wqk[kc][1], wql[kc][1], xn[kc][1], xl[kc][1]]
                        mm_group(ctx, ps_t[:, :], pieces, reads=tuple(reads), wbuf=ps_b)
                        pend.append((ob, tc, gi, ps_t, ps_b))
                        gi += 1
                        if len(pend) > 2:
                            ev(*pend.popleft())
                while pend:
                    ev(*pend.popleft())
        with ctx.scope():
            wv = [ctx.sb(f"l1a_wv{p}", [128, 1024], BF16) for p in range(NPT)]
            for p in range(NPT):
                ctx.dma("QP", wv[p][0][:], W["wqkvT"][0][p * 128:(p + 1) * 128, 2048:3072],
                        reads=(W["wqkvT"][1],), writes=(wv[p][1],))
            bvr = ctx.sb("l1a_bvr", [1, 1024], BF16)
            ctx.dma("QP", bvr[0][:], W["bv_row"][0], reads=(W["bv_row"][1],),
                    writes=(bvr[1],))
            vstg = [ctx.sb(f"l1a_vs{i}", [128, 512], BF16) for i in range(3)]
            psm = [ctx.ps(f"l1a_vmm{i}", [128, 512]) for i in range(2)]
            for tb in range(ntt):
                for fb in range(2):
                    ps_t, ps_b = psm[fb % 2]
                    pieces = [(xn[kc][0][:, tb * 128:(tb + 1) * 128],
                               wv[kc][0][:, fb * 512:(fb + 1) * 512])
                              for kc in range(NPT)]
                    pieces.append((ones1x[:, :], bvr[0][0:1, fb * 512:(fb + 1) * 512]))
                    reads = tuple(xn[kc][1] for kc in range(NPT)) + \
                        tuple(wv[kc][1] for kc in range(NPT)) + (ones1x_b, bvr[1])
                    mm_group(ctx, ps_t[:, :], pieces, reads=reads, wbuf=ps_b)
                    vs, vs_b = vstg[(tb * 2 + fb) % 3]
                    ctx.emit("V", lambda E, vs=vs, ps_t=ps_t: E.tensor_copy(vs[:], ps_t[:, :]),
                             reads=(ps_b,), writes=(vs_b,))
                    for hh in range(8):
                        h = fb * 8 + hh
                        ctx.dma("QS", W["v_sc"][0][:, h * hstride + tb * 64:
                                                   h * hstride + (tb + 1) * 64],
                                vs[:, hh * 64:(hh + 1) * 64],
                                reads=(vs_b,), writes=(W["v_sc"][1],))
      attn64_core(ctx, consts, "l1a", W["q_sc"], W["k_sc"], W["v_sc"], W["z_sc"],
                  heads, width, causal=False)
      with ctx.scope():
          zT = [ctx.sb(f"l1a_zT{p}", [128, width], BF16) for p in range(NPT)]
          for p in range(NPT):
              for half in range(2):
                  ctx.dma("QS", zT[p][0][64 * half:64 * half + 64, :],
                          W["z_sc"][0][(2 * p + half) * 64:(2 * p + half + 1) * 64, :],
                          reads=(W["z_sc"][1],), writes=(zT[p][1],))
          wo = [ctx.sb(f"l1a_wo{p}", [128, 1024], BF16) for p in range(NPT)]
          for p in range(NPT):
              ctx.dma("QP", wo[p][0][:], W["woT"][0][p * 128:(p + 1) * 128, :],
                      reads=(W["woT"][1],), writes=(wo[p][1],))
          bo = ctx.sb("l1a_bo", [128, 8], F32)
          ctx.dma("QS", bo[0][:], W["bo_cols"][0], reads=(W["bo_cols"][1],),
                  writes=(bo[1],))
          psm = [ctx.ps(f"l1a_omm{i}", [128, 512]) for i in range(4)]
          rp = ResidPt(ctx, "l1a", xin_ap, xin_bufs, xout_ap, xout_bufs, cw=width)

          def o_evict(ob, tc, ps_t, ps_b):
              dst, dst_b = rp.sl(ob, tc)
              ctx.emit("V", lambda E: E.scalar_tensor_tensor(
                  dst, ps_t[:, :], bo[0][:, ob:ob + 1], dst, ALU.add, ALU.add),
                  reads=(ps_b, bo[1], dst_b), writes=(dst_b,))

          fm_linear(ctx, wo, zT, NPT, range(NPT), range(ntc), psm, o_evict,
                    ob_begin=rp.begin, ob_end=rp.end)


def l1_attn_nc(width=TT):
    hstride = (width // 128) * 64

    def fn(ctx, consts, X, W):
        emit_l1_attn(ctx, consts, X, W, width=width)
    return build_sublayer_nc(
        fn,
        [("wqkvT", [1024, 3072], F32), ("bqk_cols", [128, 16], F32),
         ("bv_row", [1, 1024], F32), ("woT", [1024, 1024], F32),
         ("bo_cols", [128, 8], F32), ("wqwk128", [128, 2], F32)],
        [("q_sc", [2048, width], BF16), ("k_sc", [2048, width], BF16),
         ("v_sc", [128, 16 * hstride], BF16), ("z_sc", [1024, width], BF16)],
        width=width)


def prep_l1_attn(Wqkv, bqkv, Wo, bo, wq, wk):
    bqk = bqkv[0:2048].reshape(16, 128).T
    return {
        "wqkvT": np.ascontiguousarray(Wqkv.T),
        "bqk_cols": np.ascontiguousarray(bqk),
        "bv_row": np.ascontiguousarray(bqkv[2048:3072].reshape(1, 1024)),
        "woT": np.ascontiguousarray(Wo.T),
        "bo_cols": np.ascontiguousarray(bo.reshape(8, 128).T),
        "wqwk128": np.ascontiguousarray(
            np.stack([np.tile(wq, 2), np.tile(wk, 2)], axis=1)),
    }


def emit_l3_attn(ctx, consts, X, W, width=TT):
    nc = ctx.nc
    xin_ap, xin_bufs = X["in"]
    xout_ap, xout_bufs = X["out"]
    ntc = width // 512
    ntt = width // 128
    heads = 16
    hstride = ntt * 64
    ones1x, ones1x_b = consts["c_ones1x"]
    blk2, blk2_b = consts["c_blk2f"]
    blk2T, blk2T_b = consts["c_blk2Tf"]
    with ctx.scope():
      with ctx.scope():
        xn = [ctx.sb(f"l3a_xn{p}", [128, width], BF16) for p in range(NPT)]
        xl = [ctx.sb(f"l3a_xl{p}", [128, width], BF16) for p in range(NPT)]
        with ctx.scope():
            sc = ln_scratch(ctx, "l3a_ln", width)
            ln_produce(ctx, consts, sc, lambda pt: xin_ap[pt * 128:(pt + 1) * 128, :],
                       xin_bufs, xn, xn, width=width, xlo_tiles=xl)
        with ctx.scope():
            wqk = [ctx.sb(f"l3a_wqk{p}", [128, 1024], BF16) for p in range(NPT)]
            wql = [ctx.sb(f"l3a_wql{p}", [128, 1024], BF16) for p in range(NPT)]
            wfs = [ctx.sb(f"l3a_wfs{i}", [128, 1024], F32) for i in range(2)]

            def load_sec_w(sec):
                for p in range(NPT):
                    csl = slice(sec * 1024, (sec + 1) * 1024)
                    ctx.dma("QP", wqk[p][0][:], W["wqkvT"][0][p * 128:(p + 1) * 128, csl],
                            reads=(W["wqkvT"][1],), writes=(wqk[p][1],))
                    wf, wf_b = wfs[p % 2]
                    ctx.dma("QS", wf[:], W["wqkvT"][0][p * 128:(p + 1) * 128, csl],
                            reads=(W["wqkvT"][1],), writes=(wf_b,))
                    ctx.emit("V", lambda E, p=p, wf=wf: E.tensor_tensor(
                        wql[p][0][:], wf[:], wqk[p][0][:], ALU.subtract),
                        reads=(wf_b, wqk[p][1]), writes=(wql[p][1],))

            qsc = ctx.sb("l3a_qsc", [2, 16], F32)
            ctx.dma("QS", qsc[0][:], W["qk_scales"][0], reads=(W["qk_scales"][1],),
                    writes=(qsc[1],))
            qf = [ctx.sb(f"l3a_qf{i}", [128, 512], F32) for i in range(3)]
            qlo_st = [ctx.sb(f"l3a_qlo{i}", [128, 512], BF16) for i in range(3)]
            q2 = [ctx.sb(f"l3a_q2{i}", [128, 512], F32) for i in range(3)]
            qo = [ctx.sb(f"l3a_qo{i}", [128, 512], BF16) for i in range(3)]
            rr = ctx.sb("l3a_rr", [2, 512], F32)
            rr2 = ctx.sb("l3a_rr2", [2, 512], F32)
            psm = [ctx.ps(f"l3a_mm{i}", [128, 512]) for i in range(3)]
            ps2 = ctx.ps("l3a_ps2", [2, 512])
            psb = ctx.ps("l3a_psb", [128, 512])

            def ev(sec, ob, tc, gi, ps_t, ps_b):
                f, f_b = qf[gi % 3]
                sq, sq_b = q2[gi % 3]
                ctx.emit("S", lambda E: E.activation(f[:], ps_t[:, :], AF.Silu),
                         reads=(ps_b,), writes=(f_b,))
                ctx.emit("S", lambda E: E.activation(sq[:], f[:], AF.Square),
                         reads=(f_b,), writes=(sq_b,))
                mm_group(ctx, ps2[0][:, :], [(blk2[:, :], sq[:])],
                         reads=(blk2_b, sq_b), wbuf=ps2[1])
                ctx.emit("S", lambda E: E.activation(rr[0][:, :], ps2[0][:, :], AF.Sqrt),
                         reads=(ps2[1],), writes=(rr[1],))
                ctx.emit("V", lambda E: E.tensor_scalar(rr2[0][:, :], rr[0][:, :],
                                                        1e-6, None, ALU.add),
                         reads=(rr[1],), writes=(rr2[1],))
                ctx.emit("V", lambda E: E.reciprocal(rr2[0][:, :], rr2[0][:, :]),
                         reads=(rr2[1],), writes=(rr2[1],))
                ctx.emit("V", lambda E: E.tensor_tensor(rr[0][:, :], rr[0][:, :],
                                                        rr2[0][:, :], ALU.mult),
                         reads=(rr[1], rr2[1]), writes=(rr[1],))
                ctx.emit("V", lambda E, sec=sec, ob=ob: E.tensor_scalar(
                    rr[0][:, :], rr[0][:, :], qsc[0][:, sec * 8 + ob:sec * 8 + ob + 1],
                    None, ALU.mult),
                    reads=(rr[1], qsc[1]), writes=(rr[1],))
                mm_group(ctx, psb[0][:, :], [(blk2T[:, :], rr[0][:, :])],
                         reads=(blk2T_b, rr[1]), wbuf=psb[1])
                ctx.emit("V", lambda E: E.tensor_tensor(f[:], f[:], psb[0][:, :], ALU.mult),
                         reads=(f_b, psb[1]), writes=(f_b,))
                zs, zs_b = qo[gi % 3]
                zl, zl_b = qlo_st[gi % 3]
                ctx.emit("V", lambda E: E.tensor_copy(zs[:], f[:]),
                         reads=(f_b,), writes=(zs_b,))
                ctx.emit("V", lambda E: E.tensor_tensor(zl[:], f[:], zs[:], ALU.subtract),
                         reads=(f_b, zs_b), writes=(zl_b,))
                dram = W["q_sc"] if sec == 0 else W["k_sc"]
                for half in range(2):
                    h = ob * 2 + half
                    ctx.dma("QS", dram[0][h * 64:(h + 1) * 64, tc * 512:(tc + 1) * 512],
                            zs[64 * half:64 * half + 64, :], reads=(zs_b,),
                            writes=(dram[1],))
                    ctx.dma("QS", dram[0][1024 + h * 64:1024 + (h + 1) * 64,
                                          tc * 512:(tc + 1) * 512],
                            zl[64 * half:64 * half + 64, :], reads=(zl_b,),
                            writes=(dram[1],))

            for sec in range(2):
                load_sec_w(sec)
                from collections import deque
                pend = deque()
                gi = 0
                for ob in range(8):
                    col = slice(ob * 128, ob * 128 + 128)
                    for tc in range(ntc):
                        ps_t, ps_b = psm[gi % 3]
                        tsl = slice(tc * 512, (tc + 1) * 512)
                        pieces = []
                        reads = []
                        for kc in range(NPT):
                            pieces += [(wqk[kc][0][:, col], xn[kc][0][:, tsl]),
                                       (wqk[kc][0][:, col], xl[kc][0][:, tsl]),
                                       (wql[kc][0][:, col], xn[kc][0][:, tsl])]
                            reads += [wqk[kc][1], wql[kc][1], xn[kc][1], xl[kc][1]]
                        mm_group(ctx, ps_t[:, :], pieces, reads=tuple(reads), wbuf=ps_b)
                        pend.append((ob, tc, gi, ps_t, ps_b))
                        gi += 1
                        if len(pend) > 2:
                            ev(sec, *pend.popleft())
                while pend:
                    ev(sec, *pend.popleft())
        with ctx.scope():
            wv = [ctx.sb(f"l3a_wv{p}", [128, 1024], BF16) for p in range(NPT)]
            for p in range(NPT):
                ctx.dma("QP", wv[p][0][:], W["wqkvT"][0][p * 128:(p + 1) * 128, 2048:3072],
                        reads=(W["wqkvT"][1],), writes=(wv[p][1],))
            vstg = [ctx.sb(f"l3a_vs{i}", [128, 512], BF16) for i in range(3)]
            psm = [ctx.ps(f"l3a_vmm{i}", [128, 512]) for i in range(2)]
            for tb in range(ntt):
                for fb in range(2):
                    ps_t, ps_b = psm[fb % 2]
                    pieces = [(xn[kc][0][:, tb * 128:(tb + 1) * 128],
                               wv[kc][0][:, fb * 512:(fb + 1) * 512])
                              for kc in range(NPT)]
                    reads = tuple(xn[kc][1] for kc in range(NPT)) + \
                        tuple(wv[kc][1] for kc in range(NPT))
                    mm_group(ctx, ps_t[:, :], pieces, reads=reads, wbuf=ps_b)
                    vs, vs_b = vstg[(tb * 2 + fb) % 3]
                    ctx.emit("S", lambda E, vs=vs, ps_t=ps_t: E.activation(
                        vs[:], ps_t[:, :], AF.Silu),
                        reads=(ps_b,), writes=(vs_b,))
                    for hh in range(8):
                        h = fb * 8 + hh
                        ctx.dma("QS", W["v_sc"][0][:, h * hstride + tb * 64:
                                                   h * hstride + (tb + 1) * 64],
                                vs[:, hh * 64:(hh + 1) * 64],
                                reads=(vs_b,), writes=(W["v_sc"][1],))
      attn64_core(ctx, consts, "l3a", W["q_sc"], W["k_sc"], W["v_sc"], W["z_sc"],
                  heads, width, causal=True)
      with ctx.scope():
          zT = [ctx.sb(f"l3a_zT{p}", [128, width], BF16) for p in range(NPT)]
          for p in range(NPT):
              for half in range(2):
                  ctx.dma("QS", zT[p][0][64 * half:64 * half + 64, :],
                          W["z_sc"][0][(2 * p + half) * 64:(2 * p + half + 1) * 64, :],
                          reads=(W["z_sc"][1],), writes=(zT[p][1],))
          wo = [ctx.sb(f"l3a_wo{p}", [128, 1024], BF16) for p in range(NPT)]
          for p in range(NPT):
              ctx.dma("QP", wo[p][0][:], W["woT"][0][p * 128:(p + 1) * 128, :],
                      reads=(W["woT"][1],), writes=(wo[p][1],))
          psm = [ctx.ps(f"l3a_omm{i}", [128, 512]) for i in range(4)]
          rp = ResidPt(ctx, "l3a", xin_ap, xin_bufs, xout_ap, xout_bufs, cw=width)

          def o_evict(ob, tc, ps_t, ps_b):
              dst, dst_b = rp.sl(ob, tc)
              ctx.emit("V", lambda E: E.tensor_tensor(dst, ps_t[:, :], dst, ALU.add),
                       reads=(ps_b, dst_b), writes=(dst_b,))

          fm_linear(ctx, wo, zT, NPT, range(NPT), range(ntc), psm, o_evict,
                    ob_begin=rp.begin, ob_end=rp.end)


def l3_attn_nc(width=TT):
    hstride = (width // 128) * 64

    def fn(ctx, consts, X, W):
        emit_l3_attn(ctx, consts, X, W, width=width)
    return build_sublayer_nc(
        fn,
        [("wqkvT", [1024, 3072], F32), ("woT", [1024, 1024], F32),
         ("qk_scales", [2, 16], F32)],
        [("q_sc", [2048, width], BF16), ("k_sc", [2048, width], BF16),
         ("v_sc", [128, 16 * hstride], BF16), ("z_sc", [1024, width], BF16)],
        width=width)


def prep_l3_attn(Wq, Wk, Wv, Wo, q_scale, k_scale):
    wqkvT = np.concatenate([Wq.T, Wk.T, Wv.T], axis=1)
    qs = np.stack([q_scale.reshape(8, 2).T, k_scale.reshape(8, 2).T], axis=0)
    return {
        "wqkvT": np.ascontiguousarray(wqkvT),
        "woT": np.ascontiguousarray(Wo.T),
        "qk_scales": np.ascontiguousarray(qs.transpose(1, 0, 2).reshape(2, 16)),
    }


def emit_l0_attn(ctx, consts, X, W, width=TT):
    nc = ctx.nc
    xin_ap, xin_bufs = X["in"]
    xout_ap, xout_bufs = X["out"]
    ntc = width // 512
    ntt = width // 128
    ones1x, ones1x_b = consts["c_ones1x"]
    ones1xf, ones1xf_b = consts["c_ones1xf"]
    onesk, onesk_b = consts["c_onesk"]
    ident, ident_b = consts["c_ident"]
    with ctx.scope():
        sT = [ctx.sb(f"l0a_sT{p}", [128, width], BF16) for p in range(NPT)]
        gib = ctx.sb("l0a_gib", [128, 8], F32)
        btc = ctx.sb("l0a_btc", [128, 1], F32)
        ctx.dma("QS", btc[0][:], W["beta128"][0], reads=(W["beta128"][1],),
                writes=(btc[1],))
        with ctx.scope():
            xn = [ctx.sb(f"l0a_xn{p}", [128, width], BF16) for p in range(NPT)]
            with ctx.scope():
                sc = ln_scratch(ctx, "l0a_ln", width)
                ln_produce(ctx, consts, sc, lambda pt: xin_ap[pt * 128:(pt + 1) * 128, :],
                           xin_bufs, xn, xn, width=width)
            with ctx.scope():
                ws = [ctx.sb(f"l0a_ws{p}", [128, 1024], BF16) for p in range(NPT)]
                for p in range(NPT):
                    ctx.dma("QP", ws[p][0][:], W["wspinT"][0][p * 128:(p + 1) * 128, :],
                            reads=(W["wspinT"][1],), writes=(ws[p][1],))
                bsc = ctx.sb("l0a_bsc", [128, 8], F32)
                ctx.dma("QS", bsc[0][:], W["bspin_cols"][0], reads=(W["bspin_cols"][1],),
                        writes=(bsc[1],))
                psm = [ctx.ps(f"l0a_mm{i}", [128, 512]) for i in range(2)]

                def s_evict(ob, tc, ps_t, ps_b):
                    ctx.emit("V", lambda E, ob=ob, tc=tc: E.tensor_scalar(
                        sT[ob][0][:, tc * 512:(tc + 1) * 512], ps_t[:, :],
                        bsc[0][:, ob:ob + 1], None, ALU.add),
                        reads=(ps_b, bsc[1]), writes=(sT[ob][1],))

                fm_linear(ctx, ws, xn, NPT, range(NPT), range(ntc), psm, s_evict)
        stm = [ctx.sb(f"l0a_stm{t}", [128, 1024], BF16) for t in range(ntt)]
        with ctx.scope():
            psc = [ctx.ps(f"l0a_tc{i}", [128, 128], BF16) for i in range(2)]
            for pt in range(NPT):
                for tb in range(ntt):
                    pc, pc_b = psc[tb % 2]
                    ctx.emit("T", lambda E, pc=pc, pt=pt, tb=tb: E.transpose(
                        pc[:, :], sT[pt][0][:, tb * 128:(tb + 1) * 128], ident[:, :]),
                        reads=(sT[pt][1], ident_b), writes=(pc_b,))
                    ctx.emit("V", lambda E, pc=pc, pt=pt, tb=tb: E.tensor_copy(
                        stm[tb][0][:, pt * 128:(pt + 1) * 128], pc[:, :]),
                        reads=(pc_b,), writes=(stm[tb][1],))
        with ctx.scope():
            wg = [ctx.sb(f"l0a_wg{p}", [128, 1024], BF16) for p in range(NPT)]
            for p in range(NPT):
                ctx.dma("QP", wg[p][0][:], W["wglobT"][0][p * 128:(p + 1) * 128, :],
                        reads=(W["wglobT"][1],), writes=(wg[p][1],))
            mfc = ctx.sb("l0a_mfc", [128, 8], BF16)
            mff = ctx.sb("l0a_mff", [128, 1], F32)
            psg = ctx.ps("l0a_psg", [128, 1], F32)
            for p in range(NPT):
                ctx.emit("V", lambda E, p=p: E.reduce_sum(mff[0][:], sT[p][0][:], axis=AX),
                         reads=(sT[p][1],), writes=(mff[1],))
                ctx.emit("V", lambda E, p=p: E.tensor_scalar(
                    mfc[0][:, p:p + 1], mff[0][:], 1.0 / width, None, ALU.mult),
                    reads=(mff[1],), writes=(mfc[1],))
            for ob in range(NPT):
                pieces = [(wg[kc][0][:, ob * 128:(ob + 1) * 128], mfc[0][:, kc:kc + 1])
                          for kc in range(NPT)]
                reads = tuple(wg[kc][1] for kc in range(NPT)) + (mfc[1],)
                mm_group(ctx, psg[0][:, :], pieces, reads=reads, wbuf=psg[1])
                ctx.emit("V", lambda E, ob=ob: E.tensor_tensor(
                    gib[0][:, ob:ob + 1], psg[0][:, :], btc[0][:], ALU.mult),
                    reads=(psg[1], btc[1]), writes=(gib[1],))
        with ctx.scope():
            wj = [ctx.sb(f"l0a_wj{p}", [128, 1024], BF16) for p in range(NPT)]
            for p in range(NPT):
                ctx.dma("QP", wj[p][0][:], W["wjT"][0][p * 128:(p + 1) * 128, :],
                        reads=(W["wjT"][1],), writes=(wj[p][1],))
            efT = [ctx.sb(f"l0a_ef{p}", [128, 512], BF16) for p in range(NPT)]
            ptile = [ctx.sb(f"l0a_pt{i}", [128, 512], BF16) for i in range(ntt)]
            mrow = ctx.sb("l0a_mrow", [1, 512], BF16)
            mcf = ctx.sb("l0a_mcf", [128, 4], F32)
            mcb = ctx.sb("l0a_mcb", [128, 4], BF16)
            tmpc = ctx.sb("l0a_tmpc", [128, 1], F32)
            rdt = ctx.sb("l0a_rdt", [1, 512], F32)
            rdsb = ctx.sb("l0a_rdsb", [128, 512], F32)
            t2s = [ctx.sb(f"l0a_t2{i}", [128, 512], F32) for i in range(2)]
            xsl = [ctx.sb(f"l0a_xsl{i}", [128, 512], F32) for i in range(3)]
            ps_s = ctx.ps("l0a_pss", [128, 2048], F32)
            ps_s_h = [ps_s[1], ps_s[1]]
            ps_st = [ctx.ps(f"l0a_psst{i}", [128, 512], F32) for i in range(2)]
            ps_pv = ctx.ps("l0a_pspv", [128, 512], F32)
            ps_m = ctx.ps("l0a_psm", [1, 128], BF16)

            for qb in range(ntc):
                for ob in range(NPT):
                    pst, pst_b = ps_st[ob % 2]
                    pieces = [(wj[kc][0][:, ob * 128:(ob + 1) * 128],
                               sT[kc][0][:, qb * 512:(qb + 1) * 512])
                              for kc in range(NPT)]
                    reads = tuple(wj[kc][1] for kc in range(NPT)) + \
                        tuple(sT[kc][1] for kc in range(NPT))
                    mm_group(ctx, pst[:, :], pieces, reads=reads, wbuf=pst_b)
                    ctx.emit("V", lambda E, ob=ob, pst=pst: E.tensor_scalar(
                        efT[ob][0][:], pst[:, :], 1.0 / 32.0, None, ALU.mult),
                        reads=(pst_b,), writes=(efT[ob][1],))
                for qt4 in range(4):
                    qt = qb * 4 + qt4
                    first = True
                    for k0 in range(0, width, 2048):
                        kw = min(2048, width - k0)

                        def fn(E, k0=k0, kw=kw, qt4=qt4):
                            ins = None
                            for j in range(0, kw, 512):
                                for kc in range(NPT):
                                    ins = E.matmul(
                                        ps_s[0][:, j:j + 512],
                                        efT[kc][0][:, 128 * qt4:128 * qt4 + 128],
                                        sT[kc][0][:, k0 + j:k0 + j + 512],
                                        start=(kc == 0), stop=(kc == NPT - 1))
                            return ins

                        ctx.emit("T", fn,
                                 reads=tuple(efT[kc][1] for kc in range(NPT)) +
                                 tuple(sT[kc][1] for kc in range(NPT)),
                                 writes=(ps_s[1],))
                        for j in range(0, kw, 512):
                            if first:
                                ctx.emit("V", lambda E, j=j, qt4=qt4: E.reduce_max(
                                    mcf[0][:, qt4:qt4 + 1], ps_s[0][:, j:j + 512], axis=AX),
                                    reads=(ps_s[1],), writes=(mcf[1],))
                                first = False
                            else:
                                ctx.emit("V", lambda E, j=j: E.reduce_max(
                                    tmpc[0][:], ps_s[0][:, j:j + 512], axis=AX),
                                    reads=(ps_s[1],), writes=(tmpc[1],))
                                ctx.emit("V", lambda E, qt4=qt4: E.tensor_tensor(
                                    mcf[0][:, qt4:qt4 + 1], mcf[0][:, qt4:qt4 + 1],
                                    tmpc[0][:], ALU.max),
                                    reads=(mcf[1], tmpc[1]), writes=(mcf[1],))
                ctx.emit("V", lambda E: E.tensor_scalar(mcb[0][:], mcf[0][:], -1.0,
                                                        None, ALU.mult),
                         reads=(mcf[1],), writes=(mcb[1],))
                for qt4 in range(4):
                    ctx.emit("T", lambda E, qt4=qt4: E.transpose(
                        ps_m[0][:, :], mcb[0][:, qt4:qt4 + 1], ident[:, :]),
                        reads=(mcb[1], ident_b), writes=(ps_m[1],))
                    ctx.emit("V", lambda E, qt4=qt4: E.tensor_copy(
                        mrow[0][0:1, qt4 * 128:(qt4 + 1) * 128], ps_m[0][0:1, :]),
                        reads=(ps_m[1],), writes=(mrow[1],))
                for mc in range(ntt):
                    pst, pst_b = ps_st[mc % 2]
                    pieces = [(ones1x[:, :], mrow[0][0:1, :])]
                    pieces += [(sT[kc][0][:, 128 * mc:128 * mc + 128],
                                efT[kc][0][:, 0:512])
                               for kc in range(NPT)]
                    reads = (ones1x_b, mrow[1]) + \
                        tuple(sT[kc][1] for kc in range(NPT)) + \
                        tuple(efT[kc][1] for kc in range(NPT))
                    mm_group(ctx, pst[:, :], pieces, reads=reads, wbuf=pst_b)
                    pt, pt_b = ptile[mc]
                    ctx.emit("S", lambda E, pt=pt, pst=pst: E.activation(
                        pt[:], pst[:, :], AF.Exp),
                        reads=(pst_b,), writes=(pt_b,))
                    ctx.emit("T", lambda E, pt=pt, mc=mc: E.matmul(
                        ps_s[0][0:1, 0:512], onesk[:, :], pt[:],
                        start=(mc == 0), stop=(mc == ntt - 1)),
                        reads=(pt_b, onesk_b), writes=(ps_s[1],))
                ctx.emit("V", lambda E: E.reciprocal(rdt[0][0:1, :], ps_s[0][0:1, 0:512]),
                         reads=(ps_s_h[0],), writes=(rdt[1],))
                ctx.emit("T", lambda E: E.matmul(ps_st[0][0][:, :], ones1xf[:, :],
                                                 rdt[0][0:1, :], start=True, stop=True),
                         reads=(rdt[1], ones1xf_b), writes=(ps_st[0][1],))
                ctx.emit("V", lambda E: E.tensor_copy(rdsb[0][:], ps_st[0][0][:, :]),
                         reads=(ps_st[0][1],), writes=(rdsb[1],))
                for db in range(NPT):
                    def fnpv(E, db=db):
                        ins = None
                        for mc in range(ntt):
                            ins = E.matmul(ps_pv[0][:, :],
                                           stm[mc][0][:, db * 128:(db + 1) * 128],
                                           ptile[mc][0][:],
                                           start=(mc == 0), stop=(mc == ntt - 1))
                        return ins

                    ctx.emit("T", fnpv,
                             reads=tuple(ptile[mc][1] for mc in range(ntt)) +
                             tuple(stm[mc][1] for mc in range(ntt)),
                             writes=(ps_pv[1],))
                    t2, t2_b = t2s[db % 2]
                    ctx.emit("V", lambda E, t2=t2: E.tensor_tensor(
                        t2[:], ps_pv[0][:, :], rdsb[0][:], ALU.mult),
                        reads=(ps_pv[1], rdsb[1]), writes=(t2_b,))
                    ctx.emit("V", lambda E, t2=t2, db=db, qb=qb: E.tensor_tensor(
                        t2[:], t2[:], sT[db][0][:, qb * 512:(qb + 1) * 512], ALU.add),
                        reads=(t2_b, sT[db][1]), writes=(t2_b,))

                    def prep(dst_ap, dst_b, t2=t2, t2_b=t2_b, db=db):
                        ts, ts_b = t2s[(db + 1) % 2]
                        ctx.emit("S", lambda E: E.activation(
                            ts[:], t2[:], AF.Tanh, bias=gib[0][:, db:db + 1],
                            scale=btc[0][:, 0:1]),
                            reads=(t2_b, gib[1], btc[1]), writes=(ts_b,))
                        ctx.emit("V", lambda E: E.tensor_tensor(dst_ap, ts[:], dst_ap,
                                                                ALU.add),
                                 reads=(ts_b, dst_b), writes=(dst_b,))

                    resid_evict(ctx, xin_ap, xin_bufs, xout_ap, xout_bufs,
                                xsl, db, qb, prep)


def l0_attn_nc(width=TT):
    def fn(ctx, consts, X, W):
        emit_l0_attn(ctx, consts, X, W, width=width)
    return build_sublayer_nc(
        fn,
        [("wspinT", [1024, 1024], F32), ("bspin_cols", [128, 8], F32),
         ("wglobT", [1024, 1024], F32),
         ("wjT", [1024, 1024], F32), ("beta128", [128, 1], F32)],
        [],
        width=width)


def prep_l0_attn(W_spin, b_spin, W_global, W_J, beta):
    return {
        "wspinT": np.ascontiguousarray(W_spin.T),
        "bspin_cols": np.ascontiguousarray(b_spin.reshape(8, 128).T),
        "wglobT": np.ascontiguousarray(W_global.T),
        "wjT": np.ascontiguousarray(W_J.T),
        "beta128": np.ascontiguousarray(np.broadcast_to(beta.reshape(1, 1), (128, 1))),
    }


def emit_l3_ffn(ctx, consts, X, W, width=TT):
    nc = ctx.nc
    xin_ap, xin_bufs = X["in"]
    xout_ap, xout_bufs = X["out"]
    ntc = width // 512
    ntt = width // 128
    NE = 8
    ones1x, ones1x_b = consts["c_ones1x"]
    ident, ident_b = consts["c_ident"]
    identf, identf_b = consts["c_identf"]
    with ctx.scope():
        gates = ctx.sb("moe_g", [128, NE * ntt], F32)
        with ctx.scope():
            xn = [ctx.sb(f"moe_xn{p}", [128, width], BF16) for p in range(NPT)]
            xl = [ctx.sb(f"moe_xl{p}", [128, width], BF16) for p in range(NPT)]
            with ctx.scope():
                sc = ln_scratch(ctx, "moe_ln", width)
                ln_produce(ctx, consts, sc, lambda pt: xin_ap[pt * 128:(pt + 1) * 128, :],
                           xin_bufs, xn, xn, width=width, xlo_tiles=xl)
            for p in range(NPT):
                ctx.dma("QS", W["xn_sc"][0][p * 128:(p + 1) * 128, :], xn[p][0][:],
                        reads=(xn[p][1],), writes=(W["xn_sc"][1],))
            wg = ctx.sb("moe_wg", [128, NPT * 8], BF16)
            wgl = ctx.sb("moe_wgl", [128, NPT * 8], BF16)
            wgf = ctx.sb("moe_wgf", [128, NPT * 8], F32)
            ctx.dma("QP", wg[0][:], W["wgT"][0], reads=(W["wgT"][1],), writes=(wg[1],))
            ctx.dma("QS", wgf[0][:], W["wgT"][0], reads=(W["wgT"][1],), writes=(wgf[1],))
            ctx.emit("V", lambda E: E.tensor_tensor(wgl[0][:], wgf[0][:], wg[0][:],
                                                    ALU.subtract),
                     reads=(wgf[1], wg[1]), writes=(wgl[1],))
            lg = ctx.sb("moe_lg", [128, 8], F32)
            lgg = ctx.sb("moe_lgg", [128, 8], F32)
            m1 = ctx.sb("moe_m1", [128, 1], F32)
            m2 = ctx.sb("moe_m2", [128, 1], F32)
            w1c = ctx.sb("moe_w1c", [128, 1], F32)
            msk1 = ctx.sb("moe_msk1", [128, 8], F32)
            msk2 = ctx.sb("moe_msk2", [128, 8], F32)
            psg = ctx.ps("moe_psg", [128, 8], F32)
            for tb in range(ntt):
                tsl = slice(tb * 128, (tb + 1) * 128)
                pieces = []
                reads = []
                for kc in range(NPT):
                    ksl = slice(kc * 8, (kc + 1) * 8)
                    pieces += [(xn[kc][0][:, tsl], wg[0][:, ksl]),
                               (xl[kc][0][:, tsl], wg[0][:, ksl]),
                               (xn[kc][0][:, tsl], wgl[0][:, ksl])]
                    reads += [xn[kc][1], xl[kc][1], wg[1], wgl[1]]
                mm_group(ctx, psg[0][:, :], pieces, reads=tuple(reads), wbuf=psg[1])
                ctx.emit("V", lambda E: E.tensor_copy(lg[0][:], psg[0][:, :]),
                         reads=(psg[1],), writes=(lg[1],))
                ctx.emit("V", lambda E: E.reduce_max(m1[0][:], lg[0][:], axis=AX),
                         reads=(lg[1],), writes=(m1[1],))
                ctx.emit("V", lambda E: E.tensor_scalar(msk1[0][:], lg[0][:], m1[0][:],
                                                        None, ALU.is_ge),
                         reads=(lg[1], m1[1]), writes=(msk1[1],))
                ctx.emit("V", lambda E: E.scalar_tensor_tensor(lgg[0][:], msk1[0][:],
                                                               -1e30, lg[0][:],
                                                               ALU.mult, ALU.add),
                         reads=(msk1[1], lg[1]), writes=(lgg[1],))
                ctx.emit("V", lambda E: E.reduce_max(m2[0][:], lgg[0][:], axis=AX),
                         reads=(lgg[1],), writes=(m2[1],))
                ctx.emit("V", lambda E: E.tensor_scalar(msk2[0][:], lgg[0][:], m2[0][:],
                                                        None, ALU.is_ge),
                         reads=(lgg[1], m2[1]), writes=(msk2[1],))
                ctx.emit("V", lambda E: E.tensor_tensor(w1c[0][:], m1[0][:], m2[0][:],
                                                        ALU.subtract),
                         reads=(m1[1], m2[1]), writes=(w1c[1],))
                ctx.emit("S", lambda E: E.activation(w1c[0][:], w1c[0][:], AF.Sigmoid),
                         reads=(w1c[1],), writes=(w1c[1],))
                ctx.emit("V", lambda E: E.tensor_scalar(msk1[0][:], msk1[0][:],
                                                        w1c[0][:], None, ALU.mult),
                         reads=(msk1[1], w1c[1]), writes=(msk1[1],))
                ctx.emit("V", lambda E: E.tensor_scalar(w1c[0][:], w1c[0][:], -1.0, 1.0,
                                                        ALU.mult, ALU.add),
                         reads=(w1c[1],), writes=(w1c[1],))
                ctx.emit("V", lambda E: E.tensor_scalar(msk2[0][:], msk2[0][:],
                                                        w1c[0][:], None, ALU.mult),
                         reads=(msk2[1], w1c[1]), writes=(msk2[1],))
                ctx.emit("V", lambda E: E.tensor_tensor(msk1[0][:], msk1[0][:],
                                                        msk2[0][:], ALU.add),
                         reads=(msk1[1], msk2[1]), writes=(msk1[1],))
                for e in range(NE):
                    ctx.emit("V", lambda E, e=e, tb=tb: E.tensor_copy(
                        gates[0][:, e * ntt + tb:e * ntt + tb + 1],
                        msk1[0][:, e:e + 1]),
                        reads=(msk1[1],), writes=(gates[1],))
        with ctx.scope():
            ntc4 = min(4, ntc)
            nhalf = ntc // ntc4
            w1t = [ctx.sb(f"moe_w1{p}", [128, 4096], BF16) for p in range(NPT)]
            w2t = [ctx.sb(f"moe_w2{i}", [128, 4096], BF16) for i in range(2)]
            xnt = [ctx.sb(f"moe_xnt{p}", [128, 512], BF16) for p in range(NPT)]
            hT = [ctx.sb(f"moe_h{i}", [128, 512], BF16) for i in range(32)]
            b1c = ctx.sb("moe_b1c", [128, 32], F32)
            b2r = ctx.sb("moe_b2r", [1, 1024], BF16)
            yacc = [ctx.sb(f"moe_y{i}", [128, 1024], F32) for i in range(4 * ntc4)]
            psb = [ctx.ps(f"moe_ps{i}", [128, 512], F32) for i in range(8)]
            rp_moe = [None]
            for half in range(nhalf):
                for e in range(NE):
                    for p in range(NPT):
                        ctx.dma("QP", w1t[p][0][:],
                                W["w1s"][0][e * 1024 + p * 128:e * 1024 + (p + 1) * 128, :],
                                reads=(W["w1s"][1],), writes=(w1t[p][1],))
                    ctx.dma("QS", b1c[0][:], W["b1cols"][0][e * 128:(e + 1) * 128, :],
                            reads=(W["b1cols"][1],), writes=(b1c[1],))
                    ctx.dma("QP", b2r[0][:], W["b2rows"][0][0:1, e * 1024:(e + 1) * 1024],
                            reads=(W["b2rows"][1],), writes=(b2r[1],))
                    for tc4 in range(ntc4):
                        tc = half * ntc4 + tc4
                        tsl = slice(tc * 512, (tc + 1) * 512)
                        for p in range(NPT):
                            ctx.dma("QS", xnt[p][0][:],
                                    W["xn_sc"][0][p * 128:(p + 1) * 128, tsl],
                                    reads=(W["xn_sc"][1],), writes=(xnt[p][1],))
                        for fb in range(32):
                            ps_t, ps_b = psb[fb % 2]
                            pieces = [(w1t[kc][0][:, fb * 128:(fb + 1) * 128],
                                       xnt[kc][0][:]) for kc in range(NPT)]
                            reads = tuple(w1t[kc][1] for kc in range(NPT)) + \
                                tuple(xnt[kc][1] for kc in range(NPT))
                            mm_group(ctx, ps_t[:, :], pieces, reads=reads, wbuf=ps_b)
                            ht, ht_b = hT[fb]
                            ctx.emit("S", lambda E, ht=ht, ps_t=ps_t, fb=fb: E.activation(
                                ht[:], ps_t[:, :], AF.Gelu, bias=b1c[0][:, fb:fb + 1]),
                                reads=(ps_b, b1c[1]), writes=(ht_b,))
                        for g in range(8):
                            wt, wt_b = w2t[g % 2]
                            ctx.dma("QP", wt[:],
                                    W["w2s"][0][:, e * 32768 + g * 4096:
                                                e * 32768 + (g + 1) * 4096],
                                    reads=(W["w2s"][1],), writes=(wt_b,))
                            for k4 in range(4):
                                fc = g * 4 + k4
                                for ob2 in range(2):
                                    for tb4 in range(4):
                                        j = ob2 * 4 + tb4

                                        def fnw2(E, fc=fc, k4=k4, ob2=ob2, tb4=tb4,
                                                 j=j, wt=wt):
                                            ins = E.matmul(
                                                psb[j][0][:, :],
                                                hT[fc][0][:, tb4 * 128:(tb4 + 1) * 128],
                                                wt[:, k4 * 1024 + ob2 * 512:
                                                   k4 * 1024 + (ob2 + 1) * 512],
                                                start=(fc == 0), stop=False)
                                            if fc == 31:
                                                ins = E.matmul(
                                                    psb[j][0][:, :], ones1x[:, :],
                                                    b2r[0][0:1, ob2 * 512:(ob2 + 1) * 512],
                                                    start=False, stop=True)
                                            return ins

                                        ctx.emit("T", fnw2,
                                                 reads=(hT[fc][1], wt_b, ones1x_b, b2r[1]),
                                                 writes=(psb[j][1],))
                        for ob2 in range(2):
                            for tb4 in range(4):
                                j = ob2 * 4 + tb4
                                tb = tc * 4 + tb4
                                yt, yt_b = yacc[(tc4 * 4 + tb4)]
                                gcol = gates[0][:, e * ntt + tb:e * ntt + tb + 1]
                                osl = slice(ob2 * 512, (ob2 + 1) * 512)
                                if e == 0:
                                    ctx.emit("V", lambda E, yt=yt, j=j, osl=osl, gcol=gcol:
                                             E.tensor_scalar(yt[:, osl], psb[j][0][:, :],
                                                             gcol, None, ALU.mult),
                                             reads=(psb[j][1], gates[1]), writes=(yt_b,))
                                else:
                                    ctx.emit("V", lambda E, yt=yt, j=j, osl=osl, gcol=gcol:
                                             E.scalar_tensor_tensor(yt[:, osl],
                                                                    psb[j][0][:, :], gcol,
                                                                    yt[:, osl],
                                                                    ALU.mult, ALU.add),
                                             reads=(psb[j][1], gates[1], yt_b),
                                             writes=(yt_b,))

                if rp_moe[0] is None:
                    rp_moe[0] = ResidPt(ctx, "moerp", xin_ap, xin_bufs, xout_ap,
                                        xout_bufs, c0=0, cw=ntc4 * 512)
                rp = rp_moe[0]
                rp.c0 = half * ntc4 * 512
                for pt in range(NPT):
                    rp.begin(pt)
                    for tc4 in range(ntc4):
                        tc = half * ntc4 + tc4
                        dst_ap, dst_b = rp.sl(pt, tc)
                        for tb4 in range(4):
                            yt, yt_b = yacc[tc4 * 4 + tb4]
                            pj = psb[6 + tb4 % 2]
                            ctx.emit("T", lambda E, yt=yt, pt=pt, pj=pj: E.transpose(
                                pj[0][0:128, 0:128],
                                yt[:, pt * 128:(pt + 1) * 128], identf[:, :]),
                                reads=(yt_b, identf_b), writes=(pj[1],))
                            ctx.emit("V", lambda E, tb4=tb4, pj=pj, dst_ap=dst_ap:
                                     E.scalar_tensor_tensor(
                                dst_ap[:, tb4 * 128:(tb4 + 1) * 128],
                                pj[0][0:128, 0:128], 0.5,
                                dst_ap[:, tb4 * 128:(tb4 + 1) * 128],
                                ALU.mult, ALU.add),
                                reads=(pj[1], dst_b), writes=(dst_b,))
                    rp.end(pt)


def l3_ffn_nc(width=TT):
    def fn(ctx, consts, X, W):
        emit_l3_ffn(ctx, consts, X, W, width=width)
    return build_sublayer_nc(
        fn,
        [("wgT", [128, 64], F32), ("w1s", [8 * 1024, 4096], F32),
         ("w2s", [128, 8 * 32 * 1024], F32), ("b1cols", [8 * 128, 32], F32),
         ("b2rows", [1, 8 * 1024], F32)],
        [("xn_sc", [1024, width], BF16)],
        width=width)


def prep_l3_ffn(Wg, W1, b1, W2, b2):
    wgT = Wg.T.reshape(8, 128, 8).transpose(1, 0, 2).reshape(128, 64)
    return {
        "wgT": np.ascontiguousarray(wgT),
        "w1s": np.ascontiguousarray(W1.reshape(8 * 1024, 4096)),
        "w2s": np.ascontiguousarray(
            W2.reshape(8, 32, 128, 1024).transpose(2, 0, 1, 3).reshape(128, 8 * 32 * 1024)),
        "b1cols": np.ascontiguousarray(
            b1.reshape(8, 32, 128).transpose(0, 2, 1).reshape(8 * 128, 32)),
        "b2rows": np.ascontiguousarray(b2.reshape(1, 8 * 1024)),
    }


def emit_l2_attn(ctx, consts, X, W, width=TT):
    nc = ctx.nc
    xin_ap, xin_bufs = X["in"]
    xout_ap, xout_bufs = X["out"]
    ntc = width // 512
    ntt = width // 128
    ones1x, ones1x_b = consts["c_ones1x"]
    ones1xf, ones1xf_b = consts["c_ones1xf"]
    onesk, onesk_b = consts["c_onesk"]
    ident, ident_b = consts["c_ident"]
    iota, iota_b = consts["c_iota"]
    ustrict, ustrict_b = consts["c_ustrict"]
    with ctx.scope():
        wrow = ctx.sb("l2a_wrow", [1, 1024], F32)
        urow = ctx.sb("l2a_urow", [1, 1024], F32)
        e127 = ctx.sb("l2a_e127", [1, 1024], F32)
        e128 = ctx.sb("l2a_e128", [1, 1024], F32)
        ssum = ctx.sb("l2a_ssum", [1, 1024], F32)
        ssq = ctx.sb("l2a_ssq", [1, 1024], F32)
        ctx.dma("QS", urow[0][:], W["tf_row"][0], reads=(W["tf_row"][1],),
                writes=(urow[1],))
        ctx.dma("QS", wrow[0][:], W["td_row"][0], reads=(W["td_row"][1],),
                writes=(wrow[1],))
        ctx.emit("S", lambda E: E.activation(wrow[0][:], wrow[0][:], AF.Exp),
                 reads=(wrow[1],), writes=(wrow[1],))
        ctx.emit("V", lambda E: E.tensor_scalar(wrow[0][:], wrow[0][:], -1.0,
                                                None, ALU.mult),
                 reads=(wrow[1],), writes=(wrow[1],))
        ctx.emit("S", lambda E: E.activation(e127[0][:], wrow[0][:], AF.Exp, scale=127.0),
                 reads=(wrow[1],), writes=(e127[1],))
        ctx.emit("S", lambda E: E.activation(e128[0][:], wrow[0][:], AF.Exp, scale=128.0),
                 reads=(wrow[1],), writes=(e128[1],))
        with ctx.scope():
            xn = [ctx.sb(f"l2a_xn{p}", [128, width], BF16) for p in range(NPT)]
            with ctx.scope():
                sc = ln_scratch(ctx, "l2a_ln", width)
                ln_produce(ctx, consts, sc, lambda pt: xin_ap[pt * 128:(pt + 1) * 128, :],
                           xin_bufs, xn, xn, width=width)
            tmc = ctx.sb("l2a_tmc", [128, 24], F32)
            ctx.dma("QS", tmc[0][:], W["tm_cols"][0], reads=(W["tm_cols"][1],),
                    writes=(tmc[1],))
            xm = [ctx.sb(f"l2a_xm{p}", [128, width], BF16) for p in range(NPT)]
            sh = ctx.sb("l2a_sh", [128, width], BF16)
            dd = ctx.sb("l2a_dd", [128, width], BF16)
            wt = [ctx.sb(f"l2a_wt{p}", [128, 1024], BF16) for p in range(NPT)]
            vstg = [ctx.sb(f"l2a_vs{i}", [128, 512], F32) for i in range(3)]
            psm = [ctx.ps(f"l2a_mm{i}", [128, 512]) for i in range(2)]
            for m, (wname, dram) in enumerate(
                    (("wrT", "r_sc"), ("wkT", "k_sc"), ("wvT", "v_sc"))):
                for p in range(NPT):
                    ctx.dma("QP", wt[p][0][:], W[wname][0][p * 128:(p + 1) * 128, :],
                            reads=(W[wname][1],), writes=(wt[p][1],))
                    ctx.emit("V", lambda E, p=p: E.tensor_copy(
                        sh[0][:, 1:width], xn[p][0][:, 0:width - 1]),
                        reads=(xn[p][1],), writes=(sh[1],))
                    ctx.emit("V", lambda E: E.memset(sh[0][:, 0:1], 0.0),
                             reads=(), writes=(sh[1],))
                    ctx.emit("V", lambda E, p=p: E.tensor_tensor(
                        dd[0][:], xn[p][0][:], sh[0][:], ALU.subtract),
                        reads=(xn[p][1], sh[1]), writes=(dd[1],))
                    ctx.emit("V", lambda E, p=p, m=m: E.scalar_tensor_tensor(
                        xm[p][0][:], dd[0][:], tmc[0][:, m * 8 + p:m * 8 + p + 1],
                        sh[0][:], ALU.mult, ALU.add),
                        reads=(dd[1], tmc[1], sh[1]), writes=(xm[p][1],))
                for tb in range(ntt):
                    for fb in range(2):
                        ps_t, ps_b = psm[fb % 2]
                        pieces = [(xm[kc][0][:, tb * 128:(tb + 1) * 128],
                                   wt[kc][0][:, fb * 512:(fb + 1) * 512])
                                  for kc in range(NPT)]
                        reads = tuple(xm[kc][1] for kc in range(NPT)) + \
                            tuple(wt[kc][1] for kc in range(NPT))
                        mm_group(ctx, ps_t[:, :], pieces, reads=reads, wbuf=ps_b)
                        vs, vs_b = vstg[(tb * 2 + fb) % 3]
                        if m == 0:
                            ctx.emit("S", lambda E, vs=vs, ps_t=ps_t: E.activation(
                                vs[:], ps_t[:, :], AF.Sigmoid),
                                reads=(ps_b,), writes=(vs_b,))
                        else:
                            ctx.emit("V", lambda E, vs=vs, ps_t=ps_t: E.tensor_copy(
                                vs[:], ps_t[:, :]),
                                reads=(ps_b,), writes=(vs_b,))
                        ctx.dma("QS", W[dram][0][tb * 128:(tb + 1) * 128,
                                                 fb * 512:(fb + 1) * 512],
                                vs[:], reads=(vs_b,), writes=(W[dram][1],))
        with ctx.scope():
            IW = ctx.sb("l2a_IW", [128, 1024], F32)
            E1 = ctx.sb("l2a_E1", [128, 1024], F32)
            E2 = ctx.sb("l2a_E2", [128, 1024], F32)
            UB = ctx.sb("l2a_UB", [128, 1024], F32)
            EWB = ctx.sb("l2a_EWB", [128, 1024], F32)
            with ctx.scope():
                negw = ctx.sb("l2a_negw", [1, 1024], F32)
                psw = [ctx.ps(f"l2a_psw{i}", [128, 512]) for i in range(2)]
                ctx.emit("V", lambda E: E.tensor_scalar(negw[0][:], wrow[0][:], -1.0,
                                                        None, ALU.mult),
                         reads=(wrow[1],), writes=(negw[1],))
                for hf in range(2):
                    csl = slice(hf * 512, (hf + 1) * 512)
                    ps_t, ps_b = psw[hf]
                    mm_group(ctx, ps_t[:, :], [(iota[:, :], negw[0][0:1, csl])],
                             reads=(iota_b, negw[1]), wbuf=ps_b)
                    ctx.emit("V", lambda E, ps_t=ps_t, csl=csl: E.tensor_copy(
                        IW[0][:, csl], ps_t[:, :]),
                        reads=(ps_b,), writes=(IW[1],))
                    mm_group(ctx, ps_t[:, :], [(ones1xf[:, :], wrow[0][0:1, csl])],
                             reads=(ones1xf_b, wrow[1]), wbuf=ps_b)
                    ctx.emit("V", lambda E, ps_t=ps_t, csl=csl: E.tensor_tensor(
                        E1[0][:, csl], ps_t[:, :], IW[0][:, csl], ALU.add),
                        reads=(ps_b, IW[1]), writes=(E1[1],))
                    ctx.emit("S", lambda E, csl=csl: E.activation(
                        E1[0][:, csl], E1[0][:, csl], AF.Exp, scale=-1.0),
                        reads=(E1[1],), writes=(E1[1],))
                    ctx.emit("S", lambda E, csl=csl: E.activation(
                        E2[0][:, csl], IW[0][:, csl], AF.Exp, scale=-1.0),
                        reads=(IW[1],), writes=(E2[1],))
                    mm_group(ctx, ps_t[:, :], [(ones1xf[:, :], urow[0][0:1, csl])],
                             reads=(ones1xf_b, urow[1]), wbuf=ps_b)
                    ctx.emit("V", lambda E, ps_t=ps_t, csl=csl: E.tensor_copy(
                        UB[0][:, csl], ps_t[:, :]),
                        reads=(ps_b,), writes=(UB[1],))
                    mm_group(ctx, ps_t[:, :], [(ones1xf[:, :], wrow[0][0:1, csl])],
                             reads=(ones1xf_b, wrow[1]), wbuf=ps_b)
                    ctx.emit("V", lambda E, ps_t=ps_t, csl=csl: E.tensor_copy(
                        EWB[0][:, csl], ps_t[:, :]),
                        reads=(ps_b,), writes=(EWB[1],))
                    ctx.emit("S", lambda E, csl=csl: E.activation(
                        EWB[0][:, csl], EWB[0][:, csl], AF.Exp),
                        reads=(EWB[1],), writes=(EWB[1],))
            Nr = ctx.sb("l2a_Nr", [1, 1024], F32)
            Dr = ctx.sb("l2a_Dr", [1, 1024], F32)
            t1r = ctx.sb("l2a_t1r", [1, 1024], F32)
            ctx.emit("V", lambda E: E.memset(Nr[0][:], 0.0), writes=(Nr[1],))
            ctx.emit("V", lambda E: E.memset(Dr[0][:], 0.0), writes=(Dr[1],))
            kt = [ctx.sb(f"l2a_kt{i}", [128, 1024], F32) for i in range(2)]
            vt = [ctx.sb(f"l2a_vt{i}", [128, 1024], F32) for i in range(2)]
            rt = [ctx.sb(f"l2a_rt{i}", [128, 1024], F32) for i in range(2)]
            ek = ctx.sb("l2a_ek", [128, 1024], F32)
            yh = ctx.sb("l2a_yh", [128, 1024], BF16)
            yl = ctx.sb("l2a_yl", [128, 1024], BF16)
            zh = ctx.sb("l2a_zh", [128, 1024], BF16)
            zl = ctx.sb("l2a_zl", [128, 1024], BF16)
            yf = ctx.sb("l2a_yf", [128, 1024], F32)
            nt = ctx.sb("l2a_nt", [128, 1024], F32)
            dt_ = ctx.sb("l2a_dt", [128, 1024], F32)
            eu = ctx.sb("l2a_eu", [128, 1024], F32)
            ekr = ctx.sb("l2a_ekr", [128, 1024], F32)
            ta = ctx.sb("l2a_ta", [128, 1024], F32)
            tb_ = ctx.sb("l2a_tb", [128, 1024], F32)
            crel = ctx.sb("l2a_crel", [128, 1024], F32)
            t2r = ctx.sb("l2a_t2r", [1, 1024], F32)
            wkv = ctx.sb("l2a_wkv", [128, 1024], F32)
            wsq = ctx.sb("l2a_wsq", [128, 1024], BF16)
            wkb = ctx.sb("l2a_wkb", [128, 1024], BF16)
            ps_py = ctx.ps("l2a_ppy", [128, 1024], F32)
            ps_pz = ctx.ps("l2a_ppz", [128, 1024], F32)
            ps_sy = ctx.ps("l2a_psy", [1, 1024], F32)
            ps_sz = ctx.ps("l2a_psz", [1, 1024], F32)
            ctx.emit("V", lambda E: E.memset(ssum[0][:], 0.0), writes=(ssum[1],))
            ctx.emit("V", lambda E: E.memset(ssq[0][:], 0.0), writes=(ssq[1],))
            for cch in range(ntt):
                ka, ka_b = kt[cch % 2]
                va, va_b = vt[cch % 2]
                ra, ra_b = rt[cch % 2]
                rsl = slice(cch * 128, (cch + 1) * 128)
                ctx.dma("QS", ka[:], W["k_sc"][0][rsl, :], reads=(W["k_sc"][1],),
                        writes=(ka_b,))
                ctx.dma("QS", va[:], W["v_sc"][0][rsl, :], reads=(W["v_sc"][1],),
                        writes=(va_b,))
                ctx.dma("QS", ra[:], W["r_sc"][0][rsl, :], reads=(W["r_sc"][1],),
                        writes=(ra_b,))
                ctx.emit("V", lambda E, ka=ka: E.tensor_tensor(ek[0][:], ka[:],
                                                               IW[0][:], ALU.add),
                         reads=(ka_b, IW[1]), writes=(ek[1],))
                ctx.emit("S", lambda E: E.activation(ek[0][:], ek[0][:], AF.Exp),
                         reads=(ek[1],), writes=(ek[1],))
                ctx.emit("V", lambda E: E.tensor_copy(zh[0][:], ek[0][:]),
                         reads=(ek[1],), writes=(zh[1],))
                ctx.emit("V", lambda E: E.tensor_tensor(zl[0][:], ek[0][:], zh[0][:],
                                                        ALU.subtract),
                         reads=(ek[1], zh[1]), writes=(zl[1],))
                for hf in range(2):
                    csl = slice(hf * 512, (hf + 1) * 512)

                    def fnz(E, csl=csl):
                        E.matmul(ps_pz[0][:, csl], ustrict[:, :], zh[0][:, csl],
                                 start=True, stop=False)
                        E.matmul(ps_pz[0][:, csl], ustrict[:, :], zl[0][:, csl],
                                 start=False, stop=True)
                        E.matmul(ps_sz[0][0:1, csl], onesk[:, :], zh[0][:, csl],
                                 start=True, stop=False)
                        return E.matmul(ps_sz[0][0:1, csl], onesk[:, :], zl[0][:, csl],
                                        start=False, stop=True)

                    ctx.emit("T", fnz, reads=(ustrict_b, onesk_b, zh[1], zl[1]),
                             writes=(ps_pz[1], ps_sz[1]))
                ctx.emit("V", lambda E: E.tensor_tensor(dt_[0][:], ps_pz[0][:, :],
                                                        E1[0][:], ALU.mult),
                         reads=(ps_pz[1], E1[1]), writes=(dt_[1],))
                for hf in range(2):
                    csl = slice(hf * 512, (hf + 1) * 512)
                    mm_group(ctx, ps_pz[0][:, csl], [(ones1xf[:, :], Dr[0][0:1, csl])],
                             reads=(ones1xf_b, Dr[1]), wbuf=ps_pz[1])
                ctx.emit("V", lambda E: E.tensor_tensor(yf[0][:], ps_pz[0][:, :],
                                                        E2[0][:], ALU.mult),
                         reads=(ps_pz[1], E2[1]), writes=(yf[1],))
                ctx.emit("V", lambda E: E.tensor_tensor(dt_[0][:], dt_[0][:], yf[0][:],
                                                        ALU.add),
                         reads=(dt_[1], yf[1]), writes=(dt_[1],))
                ctx.emit("V", lambda E: E.tensor_tensor(t1r[0][:], ps_sz[0][0:1, :],
                                                        e127[0][:], ALU.mult),
                         reads=(ps_sz[1], e127[1]), writes=(t1r[1],))
                ctx.emit("V", lambda E: E.tensor_tensor(Dr[0][:], Dr[0][:], e128[0][:],
                                                        ALU.mult),
                         reads=(Dr[1], e128[1]), writes=(Dr[1],))
                ctx.emit("V", lambda E: E.tensor_tensor(Dr[0][:], Dr[0][:], t1r[0][:],
                                                        ALU.add),
                         reads=(Dr[1], t1r[1]), writes=(Dr[1],))
                ctx.emit("V", lambda E: E.tensor_tensor(ekr[0][:], ek[0][:], E2[0][:],
                                                        ALU.mult),
                         reads=(ek[1], E2[1]), writes=(ekr[1],))
                ctx.emit("V", lambda E: E.tensor_tensor(ta[0][:], dt_[0][:], EWB[0][:],
                                                        ALU.mult),
                         reads=(dt_[1], EWB[1]), writes=(ta[1],))
                ctx.emit("V", lambda E: E.tensor_tensor(tb_[0][:], ta[0][:], ekr[0][:],
                                                        ALU.add),
                         reads=(ta[1], ekr[1]), writes=(tb_[1],))
                ctx.emit("V", lambda E: E.tensor_tensor(ta[0][:], ta[0][:], ekr[0][:],
                                                        ALU.max),
                         reads=(ta[1], ekr[1]), writes=(ta[1],))
                ctx.emit("V", lambda E: E.reciprocal(ta[0][:], ta[0][:]),
                         reads=(ta[1],), writes=(ta[1],))
                ctx.emit("V", lambda E: E.tensor_tensor(ta[0][:], ta[0][:], tb_[0][:],
                                                        ALU.mult),
                         reads=(ta[1], tb_[1]), writes=(ta[1],))
                ctx.emit("S", lambda E: E.activation(ta[0][:], ta[0][:], AF.Ln),
                         reads=(ta[1],), writes=(ta[1],))
                ctx.emit("V", lambda E: E.tensor_copy(yh[0][:], ta[0][:]),
                         reads=(ta[1],), writes=(yh[1],))
                ctx.emit("V", lambda E: E.tensor_tensor(yl[0][:], ta[0][:], yh[0][:],
                                                        ALU.subtract),
                         reads=(ta[1], yh[1]), writes=(yl[1],))
                for hf in range(2):
                    csl = slice(hf * 512, (hf + 1) * 512)

                    def fnl(E, csl=csl):
                        E.matmul(ps_py[0][:, csl], ustrict[:, :], yh[0][:, csl],
                                 start=True, stop=False)
                        E.matmul(ps_py[0][:, csl], ustrict[:, :], yl[0][:, csl],
                                 start=False, stop=True)
                        E.matmul(ps_sy[0][0:1, csl], onesk[:, :], yh[0][:, csl],
                                 start=True, stop=False)
                        return E.matmul(ps_sy[0][0:1, csl], onesk[:, :], yl[0][:, csl],
                                        start=False, stop=True)

                    ctx.emit("T", fnl, reads=(ustrict_b, onesk_b, yh[1], yl[1]),
                             writes=(ps_py[1], ps_sy[1]))
                ctx.emit("S", lambda E: E.activation(crel[0][:], ps_py[0][:, :], AF.Exp),
                         reads=(ps_py[1],), writes=(crel[1],))
                ctx.emit("S", lambda E: E.activation(t2r[0][:], ps_sy[0][0:1, :], AF.Exp),
                         reads=(ps_sy[1],), writes=(t2r[1],))
                ctx.emit("V", lambda E, va=va: E.tensor_tensor(yf[0][:], ek[0][:],
                                                               va[:], ALU.mult),
                         reads=(ek[1], va_b), writes=(yf[1],))
                ctx.emit("V", lambda E: E.reciprocal(ta[0][:], crel[0][:]),
                         reads=(crel[1],), writes=(ta[1],))
                ctx.emit("V", lambda E: E.tensor_tensor(yf[0][:], yf[0][:], ta[0][:],
                                                        ALU.mult),
                         reads=(yf[1], ta[1]), writes=(yf[1],))
                ctx.emit("V", lambda E: E.tensor_copy(yh[0][:], yf[0][:]),
                         reads=(yf[1],), writes=(yh[1],))
                ctx.emit("V", lambda E: E.tensor_tensor(yl[0][:], yf[0][:], yh[0][:],
                                                        ALU.subtract),
                         reads=(yf[1], yh[1]), writes=(yl[1],))
                for hf in range(2):
                    csl = slice(hf * 512, (hf + 1) * 512)

                    def fny(E, csl=csl):
                        E.matmul(ps_py[0][:, csl], ustrict[:, :], yh[0][:, csl],
                                 start=True, stop=False)
                        E.matmul(ps_py[0][:, csl], ustrict[:, :], yl[0][:, csl],
                                 start=False, stop=True)
                        E.matmul(ps_sy[0][0:1, csl], onesk[:, :], yh[0][:, csl],
                                 start=True, stop=False)
                        return E.matmul(ps_sy[0][0:1, csl], onesk[:, :], yl[0][:, csl],
                                        start=False, stop=True)

                    ctx.emit("T", fny, reads=(ustrict_b, onesk_b, yh[1], yl[1]),
                             writes=(ps_py[1], ps_sy[1]))
                ctx.emit("V", lambda E: E.tensor_tensor(nt[0][:], ps_py[0][:, :],
                                                        E1[0][:], ALU.mult),
                         reads=(ps_py[1], E1[1]), writes=(nt[1],))
                for hf in range(2):
                    csl = slice(hf * 512, (hf + 1) * 512)
                    mm_group(ctx, ps_py[0][:, csl], [(ones1xf[:, :], Nr[0][0:1, csl])],
                             reads=(ones1xf_b, Nr[1]), wbuf=ps_py[1])
                ctx.emit("V", lambda E: E.tensor_tensor(yf[0][:], ps_py[0][:, :],
                                                        E2[0][:], ALU.mult),
                         reads=(ps_py[1], E2[1]), writes=(yf[1],))
                ctx.emit("V", lambda E: E.tensor_tensor(nt[0][:], nt[0][:], yf[0][:],
                                                        ALU.add),
                         reads=(nt[1], yf[1]), writes=(nt[1],))
                ctx.emit("V", lambda E: E.tensor_tensor(nt[0][:], nt[0][:], crel[0][:],
                                                        ALU.mult),
                         reads=(nt[1], crel[1]), writes=(nt[1],))
                ctx.emit("V", lambda E: E.tensor_tensor(t1r[0][:], ps_sy[0][0:1, :],
                                                        e127[0][:], ALU.mult),
                         reads=(ps_sy[1], e127[1]), writes=(t1r[1],))
                ctx.emit("V", lambda E: E.tensor_tensor(Nr[0][:], Nr[0][:], e128[0][:],
                                                        ALU.mult),
                         reads=(Nr[1], e128[1]), writes=(Nr[1],))
                ctx.emit("V", lambda E: E.tensor_tensor(Nr[0][:], Nr[0][:], t1r[0][:],
                                                        ALU.add),
                         reads=(Nr[1], t1r[1]), writes=(Nr[1],))
                ctx.emit("V", lambda E: E.tensor_tensor(Nr[0][:], Nr[0][:], t2r[0][:],
                                                        ALU.mult),
                         reads=(Nr[1], t2r[1]), writes=(Nr[1],))
                ctx.emit("V", lambda E, ka=ka: E.tensor_tensor(eu[0][:], ka[:],
                                                               UB[0][:], ALU.add),
                         reads=(ka_b, UB[1]), writes=(eu[1],))
                ctx.emit("S", lambda E: E.activation(eu[0][:], eu[0][:], AF.Exp),
                         reads=(eu[1],), writes=(eu[1],))
                ctx.emit("V", lambda E, va=va: E.tensor_tensor(yf[0][:], eu[0][:],
                                                               va[:], ALU.mult),
                         reads=(eu[1], va_b), writes=(yf[1],))
                ctx.emit("V", lambda E: E.tensor_tensor(nt[0][:], nt[0][:], yf[0][:],
                                                        ALU.add),
                         reads=(nt[1], yf[1]), writes=(nt[1],))
                ctx.emit("V", lambda E: E.tensor_tensor(dt_[0][:], dt_[0][:], eu[0][:],
                                                        ALU.add),
                         reads=(dt_[1], eu[1]), writes=(dt_[1],))
                ctx.emit("V", lambda E: E.reciprocal(dt_[0][:], dt_[0][:]),
                         reads=(dt_[1],), writes=(dt_[1],))
                ctx.emit("V", lambda E: E.tensor_tensor(wkv[0][:], nt[0][:], dt_[0][:],
                                                        ALU.mult),
                         reads=(nt[1], dt_[1]), writes=(wkv[1],))
                ctx.emit("V", lambda E, ra=ra: E.tensor_tensor(wkv[0][:], wkv[0][:],
                                                               ra[:], ALU.mult),
                         reads=(wkv[1], ra_b), writes=(wkv[1],))
                ctx.emit("V", lambda E: E.tensor_copy(wkb[0][:], wkv[0][:]),
                         reads=(wkv[1],), writes=(wkb[1],))
                ctx.dma("QS", W["wkv_sc"][0][rsl, :], wkb[0][:],
                        reads=(wkb[1],), writes=(W["wkv_sc"][1],))
                ctx.emit("S", lambda E: E.activation(wsq[0][:], wkv[0][:], AF.Square),
                         reads=(wkv[1],), writes=(wsq[1],))
                for hf in range(2):
                    csl = slice(hf * 512, (hf + 1) * 512)

                    def fnst(E, csl=csl):
                        E.matmul(ps_sy[0][0:1, csl], onesk[:, :], wkb[0][:, csl],
                                 start=True, stop=True)
                        return E.matmul(ps_sz[0][0:1, csl], onesk[:, :], wsq[0][:, csl],
                                        start=True, stop=True)

                    ctx.emit("T", fnst, reads=(onesk_b, wkb[1], wsq[1]),
                             writes=(ps_sy[1], ps_sz[1]))
                    ctx.emit("V", lambda E, csl=csl: E.tensor_tensor(
                        ssum[0][0:1, csl], ssum[0][0:1, csl], ps_sy[0][0:1, csl], ALU.add),
                        reads=(ssum[1], ps_sy[1]), writes=(ssum[1],))
                    ctx.emit("V", lambda E, csl=csl: E.tensor_tensor(
                        ssq[0][0:1, csl], ssq[0][0:1, csl], ps_sz[0][0:1, csl], ALU.add),
                        reads=(ssq[1], ps_sz[1]), writes=(ssq[1],))
        with ctx.scope():
            hsum = ctx.sb("l2a_hsum", [1, 16], F32)
            hsq = ctx.sb("l2a_hsq", [1, 16], F32)
            for h in range(16):
                ctx.emit("V", lambda E, h=h: E.reduce_sum(
                    hsum[0][0:1, h:h + 1], ssum[0][0:1, h * 64:(h + 1) * 64], axis=AX),
                    reads=(ssum[1],), writes=(hsum[1],))
                ctx.emit("V", lambda E, h=h: E.reduce_sum(
                    hsq[0][0:1, h:h + 1], ssq[0][0:1, h * 64:(h + 1) * 64], axis=AX),
                    reads=(ssq[1],), writes=(hsq[1],))
            nelem = float(width * 64)
            ctx.emit("V", lambda E: E.tensor_scalar(hsum[0][:], hsum[0][:], 1.0 / nelem,
                                                    None, ALU.mult),
                     reads=(hsum[1],), writes=(hsum[1],))
            ctx.emit("V", lambda E: E.tensor_scalar(hsq[0][:], hsq[0][:], 1.0 / nelem,
                                                    None, ALU.mult),
                     reads=(hsq[1],), writes=(hsq[1],))
            ctx.emit("V", lambda E: E.scalar_tensor_tensor(t1r[0][0:1, 0:16], hsum[0][:],
                                                           -1.0, hsum[0][:],
                                                           ALU.mult, ALU.mult),
                     reads=(hsum[1],), writes=(t1r[1],))
            ctx.emit("V", lambda E: E.tensor_tensor(hsq[0][:], hsq[0][:],
                                                    t1r[0][0:1, 0:16], ALU.add),
                     reads=(hsq[1], t1r[1]), writes=(hsq[1],))
            ctx.emit("V", lambda E: E.tensor_scalar(hsq[0][:], hsq[0][:], 1e-5,
                                                    None, ALU.add),
                     reads=(hsq[1],), writes=(hsq[1],))
            ctx.emit("S", lambda E: E.activation(hsq[0][:], hsq[0][:], AF.Ln),
                     reads=(hsq[1],), writes=(hsq[1],))
            ctx.emit("S", lambda E: E.activation(hsq[0][:], hsq[0][:], AF.Exp, scale=-0.5),
                     reads=(hsq[1],), writes=(hsq[1],))
            ctx.emit("V", lambda E: E.scalar_tensor_tensor(hsum[0][:], hsum[0][:], -1.0,
                                                           hsq[0][:], ALU.mult, ALU.mult),
                     reads=(hsum[1], hsq[1]), writes=(hsum[1],))
            acol = ctx.sb("l2a_acol", [128, 32], F32)
            psf = ctx.ps("l2a_psf", [128, 1], F32)
            for which, row in ((0, hsq), (16, hsum)):
                for h in range(16):
                    mm_group(ctx, psf[0][:, :],
                             [(ones1xf[:, :], row[0][0:1, h:h + 1])],
                             reads=(row[1], ones1xf_b), wbuf=psf[1])
                    ctx.emit("V", lambda E, which=which, h=h: E.tensor_copy(
                        acol[0][:, which + h:which + h + 1], psf[0][:, :]),
                        reads=(psf[1],), writes=(acol[1],))
            znT = [ctx.sb(f"l2a_zn{p}", [128, width], BF16) for p in range(NPT)]
            psc = [ctx.ps(f"l2a_ptc{i}", [128, 128], BF16) for i in range(2)]
            nkt = [ctx.sb(f"l2a_nk{i}", [128, 1024], BF16) for i in range(2)]
            nvt = [ctx.sb(f"l2a_nv{i}", [128, 1024], F32) for i in range(2)]
            wkb = ctx.sb("l2a_wkb2", [128, 1024], BF16)
            for cch in range(ntt):
                wa, wa_b = nkt[cch % 2]
                ctx.dma("QS", wa[:], W["wkv_sc"][0][cch * 128:(cch + 1) * 128, :],
                        reads=(W["wkv_sc"][1],), writes=(wa_b,))
                nb, nb_b = nvt[cch % 2]
                for h in range(16):
                    hsl = slice(h * 64, (h + 1) * 64)
                    ctx.emit("V", lambda E, wa=wa, nb=nb, hsl=hsl, h=h: E.tensor_scalar(
                        nb[:, hsl], wa[:, hsl], acol[0][:, h:h + 1],
                        acol[0][:, 16 + h:16 + h + 1], ALU.mult, ALU.add),
                        reads=(wa_b, acol[1]), writes=(nb_b,))
                ctx.emit("V", lambda E, nb=nb, wa=wa: E.tensor_copy(
                    wkb[0][:], nb[:]),
                    reads=(nb_b,), writes=(wkb[1],))
                for pt in range(NPT):
                    pc, pc_b = psc[pt % 2]
                    ctx.emit("T", lambda E, pc=pc, pt=pt: E.transpose(
                        pc[:, :], wkb[0][:, pt * 128:(pt + 1) * 128], ident[:, :]),
                        reads=(wkb[1], ident_b), writes=(pc_b,))
                    ctx.emit("V", lambda E, pc=pc, pt=pt, cch=cch: E.tensor_copy(
                        znT[pt][0][:, cch * 128:(cch + 1) * 128], pc[:, :]),
                        reads=(pc_b,), writes=(znT[pt][1],))
            wo = [ctx.sb(f"l2a_wo{p}", [128, 1024], BF16) for p in range(NPT)]
            gcl = ctx.sb("l2a_gcl", [128, 16], F32)
            ctx.dma("QS", gcl[0][:], W["gb_cols"][0], reads=(W["gb_cols"][1],),
                    writes=(gcl[1],))
            for p in range(NPT):
                ctx.dma("QP", wo[p][0][:], W["woT"][0][p * 128:(p + 1) * 128, :],
                        reads=(W["woT"][1],), writes=(wo[p][1],))
            boc = ctx.sb("l2a_boc", [128, 8], F32)
            bcl = ctx.sb("l2a_bcl", [128, 8], BF16)
            psf2 = ctx.ps("l2a_psf2", [128, 1], F32)
            ctx.emit("V", lambda E: E.tensor_copy(bcl[0][:], gcl[0][:, 8:16]),
                     reads=(gcl[1],), writes=(bcl[1],))
            for ob in range(NPT):
                pieces = [(wo[kc][0][:, ob * 128:(ob + 1) * 128], bcl[0][:, kc:kc + 1])
                          for kc in range(NPT)]
                reads = tuple(wo[kc][1] for kc in range(NPT)) + (bcl[1],)
                mm_group(ctx, psf2[0][:, :], pieces, reads=reads, wbuf=psf2[1])
                ctx.emit("V", lambda E, ob=ob: E.tensor_copy(
                    boc[0][:, ob:ob + 1], psf2[0][:, :]),
                    reads=(psf2[1],), writes=(boc[1],))
            for p in range(NPT):
                ctx.emit("V", lambda E, p=p: E.tensor_scalar(
                    wo[p][0][:], wo[p][0][:], gcl[0][:, p:p + 1], None, ALU.mult),
                    reads=(wo[p][1], gcl[1]), writes=(wo[p][1],))
            psm2 = [ctx.ps(f"l2a_omm{i}", [128, 512]) for i in range(4)]
            rp = ResidPt(ctx, "l2a", xin_ap, xin_bufs, xout_ap, xout_bufs, cw=width)

            def o_evict(ob, tc, ps_t, ps_b):
                dst, dst_b = rp.sl(ob, tc)
                ctx.emit("V", lambda E, ob=ob: E.scalar_tensor_tensor(
                    dst, ps_t[:, :], boc[0][:, ob:ob + 1], dst, ALU.add, ALU.add),
                    reads=(ps_b, boc[1], dst_b), writes=(dst_b,))

            fm_linear(ctx, wo, znT, NPT, range(NPT), range(ntc), psm2, o_evict,
                      ob_begin=rp.begin, ob_end=rp.end)


def l2_attn_nc(width=TT):
    def fn(ctx, consts, X, W):
        emit_l2_attn(ctx, consts, X, W, width=width)
    return build_sublayer_nc(
        fn,
        [("wrT", [1024, 1024], F32), ("wkT", [1024, 1024], F32),
         ("wvT", [1024, 1024], F32), ("woT", [1024, 1024], F32),
         ("tm_cols", [128, 24], F32), ("td_row", [1, 1024], F32),
         ("tf_row", [1, 1024], F32), ("gb_cols", [128, 16], F32)],
        [("r_sc", [width, 1024], F32), ("k_sc", [width, 1024], F32),
         ("v_sc", [width, 1024], F32), ("wkv_sc", [width, 1024], BF16)],
        width=width)


def prep_l2_attn(time_mix_r, time_mix_k, time_mix_v, W_r, W_k, W_v, W_o,
                 time_decay, time_first, ln_gamma, ln_beta):
    tm = np.stack([time_mix_r.reshape(1024), time_mix_k.reshape(1024),
                   time_mix_v.reshape(1024)], 0)
    tmc = tm.reshape(3, 8, 128).transpose(2, 0, 1).reshape(128, 24)
    gb = np.concatenate([ln_gamma.reshape(8, 128).T, ln_beta.reshape(8, 128).T], 1)
    return {
        "wrT": np.ascontiguousarray(W_r.T), "wkT": np.ascontiguousarray(W_k.T),
        "wvT": np.ascontiguousarray(W_v.T), "woT": np.ascontiguousarray(W_o.T),
        "tm_cols": np.ascontiguousarray(tmc),
        "td_row": np.ascontiguousarray(time_decay.reshape(1, 1024)),
        "tf_row": np.ascontiguousarray(time_first.reshape(1, 1024)),
        "gb_cols": np.ascontiguousarray(gb),
    }


HID = 2730


def _hid_blocks():
    bl = []
    c = 0
    while c < HID:
        w = min(128, HID - c)
        bl.append((c, w))
        c += w
    return bl


def emit_l12_ffn(ctx, consts, X, W, width=TT, tag="l12"):
    nc = ctx.nc
    xin_ap, xin_bufs = X["in"]
    xout_ap, xout_bufs = X["out"]
    ntc = width // 512
    ntt = width // 128
    heads = 8
    hstride = ntt * 64
    ones1x, ones1x_b = consts["c_ones1x"]
    blk2, blk2_b = consts["c_blk2f"]
    blk2T, blk2T_b = consts["c_blk2Tf"]
    crot, crot_b = consts["c_rot"]
    z_sc = W["z_sc"]
    zsh_sc = W["zsh_sc"]

    def load_z_hilo(zn, zl, shift_hi):
        with ctx.scope():
            stg = [ctx.sb(f"{tag}_zst{i}", [128, width], F32) for i in range(2)]
            for p in range(NPT):
                sg, sg_b = stg[p % 2]
                ctx.dma("QS", sg[:], z_sc[0][p * 128:(p + 1) * 128, :],
                        reads=(z_sc[1],), writes=(sg_b,))
                if shift_hi and p >= 4:
                    ctx.emit("V", lambda E, p=p, sg=sg: E.tensor_copy(
                        zn[p][0][:, 1:width], sg[:, 0:width - 1]),
                        reads=(sg_b,), writes=(zn[p][1],))
                    ctx.emit("V", lambda E, p=p: E.memset(zn[p][0][:, 0:1], 0.0),
                             writes=(zn[p][1],))
                    if zl is not None:
                        ctx.emit("V", lambda E, p=p, sg=sg: E.tensor_tensor(
                            zl[p][0][:, 1:width], sg[:, 0:width - 1],
                            zn[p][0][:, 1:width], ALU.subtract),
                            reads=(sg_b, zn[p][1]), writes=(zl[p][1],))
                        ctx.emit("V", lambda E, p=p: E.memset(zl[p][0][:, 0:1], 0.0),
                                 writes=(zl[p][1],))
                else:
                    ctx.emit("V", lambda E, p=p, sg=sg: E.tensor_copy(
                        zn[p][0][:], sg[:]),
                        reads=(sg_b,), writes=(zn[p][1],))
                    if zl is not None:
                        ctx.emit("V", lambda E, p=p, sg=sg: E.tensor_tensor(
                            zl[p][0][:], sg[:], zn[p][0][:], ALU.subtract),
                            reads=(sg_b, zn[p][1]), writes=(zl[p][1],))

    def inner_layer(l):
        with ctx.scope():
            hz = [ctx.sb(f"{tag}_hz{p}", [128, width], BF16) for p in range(NPT)]
            hl = [ctx.sb(f"{tag}_hl{p}", [128, width], BF16) for p in range(NPT)]
            load_z_hilo(hz, hl, shift_hi=True)
            wqk = [ctx.sb(f"{tag}_wqk{p}", [128, 1024], BF16) for p in range(NPT)]
            wql = [ctx.sb(f"{tag}_wql{p}", [128, 1024], BF16) for p in range(NPT)]
            wv_t = [ctx.sb(f"{tag}_wv{p}", [128, 512], BF16) for p in range(NPT)]
            wfs = [ctx.sb(f"{tag}_wfs{i}", [128, 1024], F32) for i in range(1)]
            for p in range(NPT):
                rsl = slice(l * 1024 + p * 128, l * 1024 + (p + 1) * 128)
                ctx.dma("QP", wqk[p][0][:], W["wqkv"][0][rsl, 0:1024],
                        reads=(W["wqkv"][1],), writes=(wqk[p][1],))
                ctx.dma("QP", wv_t[p][0][:], W["wqkv"][0][rsl, 1024:1536],
                        reads=(W["wqkv"][1],), writes=(wv_t[p][1],))
                wf, wf_b = wfs[0]
                ctx.dma("QS", wf[:], W["wqkv"][0][rsl, 0:1024],
                        reads=(W["wqkv"][1],), writes=(wf_b,))
                ctx.emit("V", lambda E, p=p, wf=wf: E.tensor_tensor(
                    wql[p][0][:], wf[:], wqk[p][0][:], ALU.subtract),
                    reads=(wf_b, wqk[p][1]), writes=(wql[p][1],))
            qksc = ctx.sb(f"{tag}_qksc", [128, 2], F32)
            ctx.dma("QS", qksc[0][:], W["qk_cols"][0][:, l * 2:(l + 1) * 2],
                    reads=(W["qk_cols"][1],), writes=(qksc[1],))
            qs8 = ctx.sb(f"{tag}_qs8", [128, 2], F32)
            ctx.emit("V", lambda E: E.tensor_scalar(qs8[0][:, 0:1], qksc[0][:, 0:1],
                                                    8.0, None, ALU.mult),
                     reads=(qksc[1],), writes=(qs8[1],))
            ctx.emit("V", lambda E: E.tensor_copy(qs8[0][:, 1:2], qksc[0][:, 1:2]),
                     reads=(qksc[1],), writes=(qs8[1],))
            rope = [ctx.sb(f"{tag}_rope{i}", [128, 512], F32) for i in range(2)]
            qf = [ctx.sb(f"{tag}_qf{i}", [128, 512], F32) for i in range(3)]
            q2 = [ctx.sb(f"{tag}_q2{i}", [128, 512], F32) for i in range(3)]
            qb2 = [ctx.sb(f"{tag}_qb{i}", [128, 512], BF16) for i in range(3)]
            qlb = [ctx.sb(f"{tag}_qlb{i}", [128, 512], BF16) for i in range(3)]
            t2 = ctx.sb(f"{tag}_t2", [128, 512], F32)
            rr = ctx.sb(f"{tag}_rr", [2, 512], F32)
            psm = [ctx.ps(f"{tag}_mm{i}", [128, 512]) for i in range(3)]
            ps2 = ctx.ps(f"{tag}_ps2", [2, 512])
            psb = ctx.ps(f"{tag}_psb", [128, 512])
            psr = ctx.ps(f"{tag}_psr", [128, 512])
            def qkev(sec, ob, tc, gi, ps_t, ps_b):
                tsl2 = slice(tc * 512, (tc + 1) * 512)
                f, f_b = qf[gi % 3]
                sq, sq_b = q2[gi % 3]
                ctx.emit("V", lambda E, f=f, ps_t=ps_t: E.tensor_copy(
                    f[:], ps_t[:, :]),
                    reads=(ps_b,), writes=(f_b,))
                ctx.emit("S", lambda E, sq=sq, f=f: E.activation(
                    sq[:], f[:], AF.Square),
                    reads=(f_b,), writes=(sq_b,))
                mm_group(ctx, ps2[0][:, :], [(blk2[:, :], sq[:])],
                         reads=(blk2_b, sq_b), wbuf=ps2[1])
                ctx.emit("S", lambda E: E.activation(rr[0][:, :], ps2[0][:, :],
                                                     AF.Sqrt),
                         reads=(ps2[1],), writes=(rr[1],))
                ctx.emit("V", lambda E: E.tensor_scalar(rr[0][:, :], rr[0][:, :],
                                                        1e-12, None, ALU.max),
                         reads=(rr[1],), writes=(rr[1],))
                ctx.emit("V", lambda E: E.reciprocal(rr[0][:, :], rr[0][:, :]),
                         reads=(rr[1],), writes=(rr[1],))
                mm_group(ctx, psb[0][:, :], [(blk2T[:, :], rr[0][:, :])],
                         reads=(blk2T_b, rr[1]), wbuf=psb[1])
                ctx.emit("V", lambda E, f=f: E.tensor_tensor(
                    f[:], f[:], psb[0][:, :], ALU.mult),
                    reads=(f_b, psb[1]), writes=(f_b,))
                ctx.emit("V", lambda E, f=f, sec=sec: E.tensor_scalar(
                    f[:], f[:], qs8[0][:, sec:sec + 1], None, ALU.mult),
                    reads=(f_b, qs8[1]), writes=(f_b,))
                ia, ib = 0, 1
                for i, tblname in ((ia, "ropeA_q" if sec == 0 else "ropeA_k"),
                                   (ib, "ropeB_q" if sec == 0 else "ropeB_k")):
                    ctx.dma("QS", rope[i][0][:], W[tblname][0][:, tsl2],
                            reads=(W[tblname][1],), writes=(rope[i][1],))
                qh_, qh_b = qb2[gi % 3]
                ql_, ql_b = qlb[gi % 3]
                ctx.emit("V", lambda E, qh_=qh_, f=f: E.tensor_copy(qh_[:], f[:]),
                         reads=(f_b,), writes=(qh_b,))
                ctx.emit("V", lambda E, ql_=ql_, f=f, qh_=qh_: E.tensor_tensor(
                    ql_[:], f[:], qh_[:], ALU.subtract),
                    reads=(f_b, qh_b), writes=(ql_b,))

                def fnrot(E, qh_=qh_, ql_=ql_):
                    E.matmul(psr[0][:, :], crot[:, :], qh_[:],
                             start=True, stop=False)
                    return E.matmul(psr[0][:, :], crot[:, :], ql_[:],
                                    start=False, stop=True)

                ctx.emit("T", fnrot, reads=(crot_b, qh_b, ql_b),
                         writes=(psr[1],))
                ctx.emit("V", lambda E, f=f, ia=ia: E.tensor_tensor(
                    f[:], f[:], rope[ia][0][:], ALU.mult),
                    reads=(f_b, rope[ia][1]), writes=(f_b,))
                ctx.emit("V", lambda E, ib=ib: E.tensor_tensor(
                    t2[0][:], psr[0][:, :], rope[ib][0][:], ALU.mult),
                    reads=(psr[1], rope[ib][1]), writes=(t2[1],))
                ctx.emit("V", lambda E, f=f: E.tensor_tensor(
                    f[:], f[:], t2[0][:], ALU.add),
                    reads=(f_b, t2[1]), writes=(f_b,))
                ctx.emit("V", lambda E, qh_=qh_, f=f: E.tensor_copy(qh_[:], f[:]),
                         reads=(f_b,), writes=(qh_b,))
                ctx.emit("V", lambda E, ql_=ql_, f=f, qh_=qh_: E.tensor_tensor(
                    ql_[:], f[:], qh_[:], ALU.subtract),
                    reads=(f_b, qh_b), writes=(ql_b,))
                dram = W["q_sc"] if sec == 0 else W["k_sc"]
                for half in range(2):
                    h = ob * 2 + half
                    ctx.dma("QS", dram[0][h * 64:(h + 1) * 64, tsl2],
                            qh_[64 * half:64 * half + 64, :],
                            reads=(qh_b,), writes=(dram[1],))
                    ctx.dma("QS", dram[0][512 + h * 64:512 + (h + 1) * 64, tsl2],
                            ql_[64 * half:64 * half + 64, :],
                            reads=(ql_b,), writes=(dram[1],))

            from collections import deque
            for sec in range(2):
                pend = deque()
                gi = 0
                for ob in range(4):
                    col = slice(sec * 512 + ob * 128, sec * 512 + (ob + 1) * 128)
                    for tc in range(ntc):
                        tsl = slice(tc * 512, (tc + 1) * 512)
                        ps_t, ps_b = psm[gi % 3]
                        pieces = []
                        reads = []
                        for kc in range(NPT):
                            pieces += [(wqk[kc][0][:, col], hz[kc][0][:, tsl]),
                                       (wqk[kc][0][:, col], hl[kc][0][:, tsl]),
                                       (wql[kc][0][:, col], hz[kc][0][:, tsl])]
                            reads += [wqk[kc][1], wql[kc][1], hz[kc][1], hl[kc][1]]
                        mm_group(ctx, ps_t[:, :], pieces, reads=tuple(reads), wbuf=ps_b)
                        pend.append((sec, ob, tc, gi, ps_t, ps_b))
                        gi += 1
                        if len(pend) > 2:
                            qkev(*pend.popleft())
                while pend:
                    qkev(*pend.popleft())
            vstg = [ctx.sb(f"{tag}_vs{i}", [128, 512], BF16) for i in range(3)]
            for tb in range(ntt):
                ps_t, ps_b = psm[tb % 3]
                pieces = [(hz[kc][0][:, tb * 128:(tb + 1) * 128], wv_t[kc][0][:, :])
                          for kc in range(NPT)]
                reads = tuple(hz[kc][1] for kc in range(NPT)) + \
                    tuple(wv_t[kc][1] for kc in range(NPT))
                mm_group(ctx, ps_t[:, :], pieces, reads=reads, wbuf=ps_b)
                vs, vs_b = vstg[tb % 3]
                ctx.emit("V", lambda E, vs=vs, ps_t=ps_t: E.tensor_copy(vs[:], ps_t[:, :]),
                         reads=(ps_b,), writes=(vs_b,))
                for hh in range(8):
                    ctx.dma("QS", W["v_sc"][0][:, hh * hstride + tb * 64:
                                               hh * hstride + (tb + 1) * 64],
                            vs[:, hh * 64:(hh + 1) * 64],
                            reads=(vs_b,), writes=(W["v_sc"][1],))
        attn64_core(ctx, consts, f"{tag}_a{l}", W["q_sc"], W["k_sc"], W["v_sc"],
                    W["za_sc"], heads, width, causal=False, local=True)
        with ctx.scope():
            zT = [ctx.sb(f"{tag}_zT{p}", [128, width], BF16) for p in range(4)]
            for p in range(4):
                for half in range(2):
                    ctx.dma("QS", zT[p][0][64 * half:64 * half + 64, :],
                            W["za_sc"][0][(2 * p + half) * 64:(2 * p + half + 1) * 64, :],
                            reads=(W["za_sc"][1],), writes=(zT[p][1],))
            wo = [ctx.sb(f"{tag}_wo{p}", [128, 1024], BF16) for p in range(4)]
            for p in range(4):
                ctx.dma("QP", wo[p][0][:],
                        W["wo"][0][l * 512 + p * 128:l * 512 + (p + 1) * 128, :],
                        reads=(W["wo"][1],), writes=(wo[p][1],))
            psm = [ctx.ps(f"{tag}_omm{i}", [128, 512]) for i in range(4)]
            zbufs = [z_sc[1]] * NPT
            rp = ResidPt(ctx, tag + "o", z_sc[0], zbufs, z_sc[0], zbufs, cw=width)

            def o_evict(ob, tc, ps_t, ps_b):
                dst, dst_b = rp.sl(ob, tc)
                ctx.emit("V", lambda E: E.tensor_tensor(dst, ps_t[:, :], dst, ALU.add),
                         reads=(ps_b, dst_b), writes=(dst_b,))

            fm_linear(ctx, wo, zT, 4, range(NPT), range(ntc), psm, o_evict,
                      ob_begin=rp.begin, ob_end=rp.end)
        with ctx.scope():
            shstg = [ctx.sb(f"{tag}_shs{i}", [128, width], F32) for i in range(2)]
            zcol = ctx.sb(f"{tag}_zc", [128, 1], F32)
            ctx.emit("V", lambda E: E.memset(zcol[0][:], 0.0), writes=(zcol[1],))
            for p in range(NPT):
                sg, sg_b = shstg[p % 2]
                if p >= 4:
                    ctx.dma("QS", sg[:, 1:width],
                            z_sc[0][p * 128:(p + 1) * 128, 0:width - 1],
                            reads=(z_sc[1],), writes=(sg_b,))
                    ctx.dma("QS", sg[:, 0:1], zcol[0][:], reads=(zcol[1],),
                            writes=(sg_b,))
                else:
                    ctx.dma("QS", sg[:], z_sc[0][p * 128:(p + 1) * 128, :],
                            reads=(z_sc[1],), writes=(sg_b,))
                ctx.dma("QS", zsh_sc[0][p * 128:(p + 1) * 128, :], sg[:],
                        reads=(sg_b,), writes=(zsh_sc[1],))
        with ctx.scope():
            with ctx.scope():
                xn = [ctx.sb(f"{tag}_fxn{p}", [128, width], BF16) for p in range(NPT)]
                with ctx.scope():
                    sc = ln_scratch(ctx, f"{tag}_fln", width)
                    ln_produce(ctx, consts, sc,
                               lambda pt: zsh_sc[0][pt * 128:(pt + 1) * 128, :],
                               [zsh_sc[1]] * NPT, xn, xn, width=width)
                for p in range(NPT):
                    ctx.dma("QS", W["xnb_sc"][0][p * 128:(p + 1) * 128, :], xn[p][0][:],
                            reads=(xn[p][1],), writes=(W["xnb_sc"][1],))
            bh = _hid_blocks()
            nbh = len(bh)
            w1t = [ctx.sb(f"{tag}_w1{p}", [128, 5460], BF16) for p in range(NPT)]
            w2t = [ctx.sb(f"{tag}_w2{i}", [128, 1024], BF16) for i in range(nbh)]
            gcl = ctx.sb(f"{tag}_gcl", [128, 16], F32)
            ctx.dma("QS", gcl[0][:], W["lngb_cols"][0][:, l * 16:(l + 1) * 16],
                    reads=(W["lngb_cols"][1],), writes=(gcl[1],))
            b1c = ctx.sb(f"{tag}_b1c", [128, 2 * nbh], F32)
            ctx.dma("QS", b1c[0][:], W["b1_cols"][0][:, l * 2 * nbh:(l + 1) * 2 * nbh],
                    reads=(W["b1_cols"][1],), writes=(b1c[1],))
            bclb = ctx.sb(f"{tag}_bclb", [128, 8], BF16)
            ctx.emit("V", lambda E: E.tensor_copy(bclb[0][:], gcl[0][:, 8:16]),
                     reads=(gcl[1],), writes=(bclb[1],))
            b2c = ctx.sb(f"{tag}_b2c", [128, 8], F32)
            ctx.dma("QS", b2c[0][:], W["b2_cols"][0][:, l * 8:(l + 1) * 8],
                    reads=(W["b2_cols"][1],), writes=(b2c[1],))
            psf = ctx.ps(f"{tag}_psf", [128, 1], F32)
            for p in range(NPT):
                ctx.dma("QP", w1t[p][0][:],
                        W["w1"][0][l * 1024 + p * 128:l * 1024 + (p + 1) * 128, :],
                        reads=(W["w1"][1],), writes=(w1t[p][1],))
            for bi, (c0, cw) in enumerate(bh):
                ctx.dma("QP", w2t[bi][0][0:cw, :],
                        W["w2"][0][l * HID + c0:l * HID + c0 + cw, :],
                        reads=(W["w2"][1],), writes=(w2t[bi][1],))
            for bi, (c0, cw) in enumerate(bh):
                for which in range(2):
                    off = which * HID
                    pieces = [(w1t[kc][0][:, off + c0:off + c0 + cw],
                               bclb[0][:, kc:kc + 1]) for kc in range(NPT)]
                    reads = tuple(w1t[kc][1] for kc in range(NPT)) + (bclb[1],)
                    mm_group(ctx, psf[0][0:cw, :], pieces, reads=reads, wbuf=psf[1])
                    ctx.emit("V", lambda E, bi=bi, which=which, cw=cw:
                             E.tensor_tensor(
                                 b1c[0][0:cw, 2 * bi + which:2 * bi + which + 1],
                                 b1c[0][0:cw, 2 * bi + which:2 * bi + which + 1],
                                 psf[0][0:cw, :], ALU.add),
                             reads=(b1c[1], psf[1]), writes=(b1c[1],))
            for p in range(NPT):
                ctx.emit("V", lambda E, p=p: E.tensor_scalar(
                    w1t[p][0][:], w1t[p][0][:], gcl[0][:, p:p + 1], None, ALU.mult),
                    reads=(w1t[p][1], gcl[1]), writes=(w1t[p][1],))
            h2t = [ctx.sb(f"{tag}_h2{i}", [128, 512], BF16) for i in range(nbh)]
            xnt = [ctx.sb(f"{tag}_xnt{p}", [128, 512], BF16) for p in range(NPT)]
            g1 = [ctx.sb(f"{tag}_g1{i}", [128, 512], F32) for i in range(2)]
            zslp = [ctx.sb(f"{tag}_zsl{i}", [128, 512], F32) for i in range(NPT)]
            psm = [ctx.ps(f"{tag}_fmm{i}", [128, 512]) for i in range(4)]
            for tc in range(ntc):
                tsl = slice(tc * 512, (tc + 1) * 512)
                for ob in range(NPT):
                    xt, xb = zslp[ob]
                    ctx.dma("QS", xt[:], z_sc[0][ob * 128:(ob + 1) * 128, tsl],
                            reads=(zbufs2[ob],), writes=(xb,))
                for p in range(NPT):
                    ctx.dma("QS", xnt[p][0][:],
                            W["xnb_sc"][0][p * 128:(p + 1) * 128, tsl],
                            reads=(W["xnb_sc"][1],), writes=(xnt[p][1],))
                for bi, (c0, cw) in enumerate(bh):
                    psA, psA_b = psm[0 + 2 * (bi % 2)]
                    psG, psG_b = psm[1 + 2 * (bi % 2)]
                    for which, (pst, psb_) in ((0, (psA, psA_b)), (1, (psG, psG_b))):
                        off = which * HID
                        pieces = [(w1t[kc][0][:, off + c0:off + c0 + cw],
                                   xnt[kc][0][:]) for kc in range(NPT)]
                        reads = tuple(w1t[kc][1] for kc in range(NPT)) + \
                            tuple(xnt[kc][1] for kc in range(NPT))
                        mm_group(ctx, pst[0:cw, :], pieces, reads=reads, wbuf=psb_)
                    gg, gg_b = g1[bi % 2]
                    ctx.emit("S", lambda E, gg=gg, psG=psG, bi=bi, cw=cw: E.activation(
                        gg[0:cw, :], psG[0:cw, :], AF.Gelu,
                        bias=b1c[0][0:cw, 2 * bi + 1:2 * bi + 2]),
                        reads=(psG_b, b1c[1]), writes=(gg_b,))
                    ht, ht_b = h2t[bi]
                    ctx.emit("V", lambda E, ht=ht, psA=psA, gg=gg, bi=bi, cw=cw:
                             E.scalar_tensor_tensor(ht[0:cw, :], psA[0:cw, :],
                                                    b1c[0][0:cw, 2 * bi:2 * bi + 1],
                                                    gg[0:cw, :], ALU.add, ALU.mult),
                             reads=(psA_b, b1c[1], gg_b), writes=(ht_b,))
                for ob in range(NPT):
                    ps_t, ps_b = psm[ob % 4]

                    def fnw2(E, ob=ob, ps_t=ps_t):
                        ins = None
                        for bi, (c0, cw) in enumerate(bh):
                            ins = E.matmul(ps_t[:, :],
                                           w2t[bi][0][0:cw, ob * 128:(ob + 1) * 128],
                                           h2t[bi][0][0:cw, :],
                                           start=(bi == 0), stop=(bi == nbh - 1))
                        return ins

                    reads = tuple(w2t[bi][1] for bi in range(nbh)) + \
                        tuple(h2t[bi][1] for bi in range(nbh))
                    ctx.emit("T", fnw2, reads=reads, writes=(ps_b,))

                    xt, xb = zslp[ob]
                    ctx.emit("V", lambda E, xt=xt, ps_t=ps_t, ob=ob:
                             E.scalar_tensor_tensor(xt[:], ps_t[:, :],
                                                    b2c[0][:, ob:ob + 1], xt[:],
                                                    ALU.add, ALU.add),
                             reads=(ps_b, b2c[1], xb), writes=(xb,))
                    ctx.dma("QS", z_sc[0][ob * 128:(ob + 1) * 128, tsl], xt[:],
                            reads=(xb,), writes=(zbufs2[ob],))

    zbufs2 = [z_sc[1]] * NPT
    with ctx.scope():
        xn0 = [ctx.sb(f"{tag}_xn0{p}", [128, width], BF16) for p in range(NPT)]
        with ctx.scope():
            sc = ln_scratch(ctx, f"{tag}_ln0", width)
            ln_produce(ctx, consts, sc, lambda pt: xin_ap[pt * 128:(pt + 1) * 128, :],
                       xin_bufs, xn0, xn0, width=width, zout=z_sc)
    for l in range(2):
        inner_layer(l)
    with ctx.scope():
        zst = [ctx.sb(f"{tag}_zfin{i}", [128, width], F32) for i in range(2)]
        rp = ResidPt(ctx, tag + "f", xin_ap, xin_bufs, xout_ap, xout_bufs, cw=width)
        for pt in range(NPT):
            rp.begin(pt)
            zt, zt_b = zst[pt % 2]
            ctx.dma("QS", zt[:], z_sc[0][pt * 128:(pt + 1) * 128, :],
                    reads=(z_sc[1],), writes=(zt_b,))
            for tc in range(ntc):
                dst, dst_b = rp.sl(pt, tc)
                tsl = slice(tc * 512, (tc + 1) * 512)
                ctx.emit("V", lambda E, zt=zt, tsl=tsl, dst=dst: E.scalar_tensor_tensor(
                    dst, zt[:, tsl], 0.5, dst, ALU.mult, ALU.add),
                    reads=(zt_b, dst_b), writes=(dst_b,))
            rp.end(pt)


def l12_ffn_nc(width=TT, tag="l12"):
    hstride = (width // 128) * 64

    def fn(ctx, consts, X, W):
        emit_l12_ffn(ctx, consts, X, W, width=width, tag=tag)
    return build_sublayer_nc(
        fn,
        [("wqkv", [2048, 1536], F32), ("wo", [1024, 1024], F32),
         ("w1", [2048, 5460], F32), ("w2", [2 * HID, 1024], F32),
         ("qk_cols", [128, 4], F32), ("lngb_cols", [128, 32], F32),
         ("b1_cols", [128, 4 * len(_hid_blocks())], F32),
         ("b2_cols", [128, 16], F32),
         ("ropeA_q", [128, width], F32), ("ropeB_q", [128, width], F32),
         ("ropeA_k", [128, width], F32), ("ropeB_k", [128, width], F32)],
        [("z_sc", [1024, width], F32), ("zsh_sc", [1024, width], F32),
         ("xnb_sc", [1024, width], BF16),
         ("q_sc", [1024, width], BF16), ("k_sc", [1024, width], BF16),
         ("v_sc", [128, 8 * hstride], BF16), ("za_sc", [512, width], BF16)],
        width=width)


def rope_tables(width):
    DIM_HEAD, WINDOW = 64, 512
    half_idx = np.arange(0, DIM_HEAD, 2, dtype=np.float64)
    inv_freq = 1.0 / 10000.0 ** (half_idx / DIM_HEAD)
    t = np.arange(width, dtype=np.float64)
    freqs = t[:, None] * inv_freq[None, :]
    freqs = np.concatenate([freqs, freqs], -1)
    base = (half_idx + 0.4 * DIM_HEAD) / (1.4 * DIM_HEAD)
    power = (t - width // 2) / float(WINDOW // 2)
    scale = base[None, :] ** power[:, None]
    scale = np.concatenate([scale, scale], -1)
    cos, sin = np.cos(freqs), np.sin(freqs)
    Aq = (cos * scale).T; Bq = (sin * scale).T
    Ak = (cos / scale).T; Bk = (sin / scale).T
    out = {}
    for name, a in (("ropeA_q", Aq), ("ropeB_q", Bq), ("ropeA_k", Ak), ("ropeB_k", Bk)):
        out[name] = np.ascontiguousarray(np.tile(a, (2, 1)).astype(np.float32))
    return out


def prep_l12_ffn(wqkv, q_scale, k_scale, wo, ln_g, ln_b, w1, b1, w2, b2, width=TT):
    bh = _hid_blocks()
    nbh = len(bh)
    b1c = np.zeros((128, 4 * nbh), np.float32)
    for l in range(2):
        for bi, (c0, cw) in enumerate(bh):
            b1c[0:cw, l * 2 * nbh + 2 * bi] = b1[l, c0:c0 + cw]
            b1c[0:cw, l * 2 * nbh + 2 * bi + 1] = b1[l, HID + c0:HID + c0 + cw]
    lngb = np.zeros((128, 32), np.float32)
    for l in range(2):
        lngb[:, l * 16:l * 16 + 8] = ln_g[l].reshape(8, 128).T
        lngb[:, l * 16 + 8:l * 16 + 16] = ln_b[l].reshape(8, 128).T
    b2c = np.zeros((128, 16), np.float32)
    for l in range(2):
        b2c[:, l * 8:(l + 1) * 8] = b2[l].reshape(8, 128).T
    qk = np.zeros((128, 4), np.float32)
    for l in range(2):
        qk[:, l * 2] = np.tile(q_scale[l], 2)
        qk[:, l * 2 + 1] = np.tile(k_scale[l], 2)
    d = {
        "wqkv": np.ascontiguousarray(wqkv.reshape(2048, 1536)),
        "wo": np.ascontiguousarray(wo.reshape(1024, 1024)),
        "w1": np.ascontiguousarray(w1.reshape(2048, 5460)),
        "w2": np.ascontiguousarray(w2.reshape(2 * HID, 1024)),
        "qk_cols": qk, "lngb_cols": lngb, "b1_cols": b1c, "b2_cols": b2c,
    }
    d.update(rope_tables(width))
    return d


_NC_CACHE = {}


def _get_nc(name, width=TT):
    key = (name, width)
    if key not in _NC_CACHE:
        _NC_CACHE[key] = {"l0f": lambda: l0_ffn_nc(first=True, width=width),
                          "l0a": lambda: l0_attn_nc(width=width),
                          "l12": lambda: l12_ffn_nc(width=width),
                          "l1a": lambda: l1_attn_nc(width=width),
                          "l2a": lambda: l2_attn_nc(width=width),
                          "l3f": lambda: l3_ffn_nc(width=width),
                          "l3a": lambda: l3_attn_nc(width=width)}[name]()
    return _NC_CACHE[key]


def kernel(**inputs):
    i = inputs
    n_cores = 8
    consts = host_consts()
    w_l0f = {"w_upT": np.ascontiguousarray(i["L0_ffn_w_up"].T),
             "w_downT": np.ascontiguousarray(i["L0_ffn_w_down"].T)}
    w_l0a = prep_l0_attn(i["L0_attn_W_spin"], i["L0_attn_b_spin"],
                         i["L0_attn_W_global"], i["L0_attn_W_J"], i["L0_attn_beta"])
    w_l1f = prep_l12_ffn(i["L1_ffn_wqkv"], i["L1_ffn_q_scale"], i["L1_ffn_k_scale"],
                         i["L1_ffn_wo"], i["L1_ffn_ln_g"], i["L1_ffn_ln_b"],
                         i["L1_ffn_w1"], i["L1_ffn_b1"], i["L1_ffn_w2"], i["L1_ffn_b2"])
    w_l1a = prep_l1_attn(i["L1_attn_Wqkv"], i["L1_attn_bqkv"], i["L1_attn_Wo"],
                         i["L1_attn_bo"], i["L1_attn_wq"], i["L1_attn_wk"])
    w_l2f = prep_l12_ffn(i["L2_ffn_wqkv"], i["L2_ffn_q_scale"], i["L2_ffn_k_scale"],
                         i["L2_ffn_wo"], i["L2_ffn_ln_g"], i["L2_ffn_ln_b"],
                         i["L2_ffn_w1"], i["L2_ffn_b1"], i["L2_ffn_w2"], i["L2_ffn_b2"])
    w_l2a = prep_l2_attn(i["L2_attn_time_mix_r"], i["L2_attn_time_mix_k"],
                         i["L2_attn_time_mix_v"], i["L2_attn_W_r"], i["L2_attn_W_k"],
                         i["L2_attn_W_v"], i["L2_attn_W_o"], i["L2_attn_time_decay"],
                         i["L2_attn_time_first"], i["L2_attn_ln_gamma"],
                         i["L2_attn_ln_beta"])
    w_l3f = prep_l3_ffn(i["L3_ffn_Wg"], i["L3_ffn_W1"], i["L3_ffn_b1"],
                        i["L3_ffn_W2"], i["L3_ffn_b2"])
    w_l3a = prep_l3_attn(i["L3_attn_Wq"], i["L3_attn_Wk"], i["L3_attn_Wv"],
                         i["L3_attn_Wo"], i["L3_attn_q_scale"], i["L3_attn_k_scale"])

    chain = [("l0f", w_l0f), ("l0a", w_l0a), ("l0f", w_l0f),
             ("l12", w_l1f), ("l1a", w_l1a), ("l12", w_l1f),
             ("l12", w_l2f), ("l2a", w_l2a), ("l12", w_l2f),
             ("l3f", w_l3f), ("l3a", w_l3a), ("l3f", w_l3f)]

    x = inputs["L0_attn_x"].astype(np.float32, copy=False)
    xres = [np.ascontiguousarray(x[b].T) for b in range(n_cores)]
    for name, wd in chain:
        nc = _get_nc(name)
        in_maps = []
        for b in range(n_cores):
            m = dict(wd)
            m.update(consts)
            m["xres_in"] = xres[b]
            in_maps.append(m)
        res = run_bass_kernel_spmd(nc, in_maps, core_ids=list(range(n_cores)))
        xres = [r["xres_out"] for r in res.results]
    out = np.stack([np.ascontiguousarray(xr.T) for xr in xres], 0)
    return out.astype(np.float32)


def _decl_w(nc, prefix, weight_specs, internal_specs):
    W = {}
    for name, shape, dt in weight_specs:
        ap = nc.dram_tensor(prefix + name, list(shape), dt, kind="ExternalInput").ap()
        W[name] = (ap, Buf(prefix + name))
    for name, shape, dt in internal_specs:
        ap = nc.dram_tensor(prefix + name, list(shape), dt, kind="Internal").ap()
        W[name] = (ap, Buf(prefix + name))
    return W


FUSED_SPECS = {
    "l0f": ([("w_upT", [1024, 4096], F32), ("w_downT", [4096, 1024], F32)],
            [("cu", [128, 8 * 4096], BF16), ("cd", [128, 32 * 1024], BF16)]),
    "l0a": ([("wspinT", [1024, 1024], F32), ("bspin_cols", [128, 8], F32),
             ("wglobT", [1024, 1024], F32), ("wjT", [1024, 1024], F32),
             ("beta128", [128, 1], F32)], []),
    "l1a": ([("wqkvT", [1024, 3072], F32), ("bqk_cols", [128, 16], F32),
             ("bv_row", [1, 1024], F32), ("woT", [1024, 1024], F32),
             ("bo_cols", [128, 8], F32), ("wqwk128", [128, 2], F32)],
            [("q_sc", [2048, TT], BF16), ("k_sc", [2048, TT], BF16),
             ("v_sc", [128, 16 * (TT // 128) * 64], BF16), ("z_sc", [1024, TT], BF16)]),
    "l12": ([("wqkv", [2048, 1536], F32), ("wo", [1024, 1024], F32),
             ("w1", [2048, 5460], F32), ("w2", [2 * HID, 1024], F32),
             ("qk_cols", [128, 4], F32), ("lngb_cols", [128, 32], F32),
             ("b1_cols", [128, 4 * 22], F32), ("b2_cols", [128, 16], F32),
             ("ropeA_q", [128, TT], F32), ("ropeB_q", [128, TT], F32),
             ("ropeA_k", [128, TT], F32), ("ropeB_k", [128, TT], F32)],
            [("z_sc", [1024, TT], F32), ("zsh_sc", [1024, TT], F32),
             ("xnb_sc", [1024, TT], BF16),
             ("q_sc", [1024, TT], BF16), ("k_sc", [1024, TT], BF16),
             ("v_sc", [128, 8 * (TT // 128) * 64], BF16), ("za_sc", [512, TT], BF16)]),
    "l2a": ([("wrT", [1024, 1024], F32), ("wkT", [1024, 1024], F32),
             ("wvT", [1024, 1024], F32), ("woT", [1024, 1024], F32),
             ("tm_cols", [128, 24], F32), ("td_row", [1, 1024], F32),
             ("tf_row", [1, 1024], F32), ("gb_cols", [128, 16], F32)],
            [("r_sc", [TT, 1024], F32), ("k_sc", [TT, 1024], F32),
             ("v_sc", [TT, 1024], F32), ("wkv_sc", [TT, 1024], BF16)]),
    "l3f": ([("wgT", [128, 64], F32), ("w1s", [8 * 1024, 4096], F32),
             ("w2s", [128, 8 * 32 * 1024], F32), ("b1cols", [8 * 128, 32], F32),
             ("b2rows", [1, 8 * 1024], F32)],
            [("xn_sc", [1024, TT], BF16)]),
    "l3a": ([("wqkvT", [1024, 3072], F32), ("woT", [1024, 1024], F32),
             ("qk_scales", [2, 16], F32)],
            [("q_sc", [2048, TT], BF16), ("k_sc", [2048, TT], BF16),
             ("v_sc", [128, 16 * (TT // 128) * 64], BF16), ("z_sc", [1024, TT], BF16)]),
}

FUSED_EMIT = {
    "l0a": emit_l0_attn, "l1a": emit_l1_attn, "l2a": emit_l2_attn,
    "l3a": emit_l3_attn, "l3f": emit_l3_ffn,
}


def build_fused_nc(width=TT):
    nc = bass.Bass("TRN2", target_bir_lowering=False)
    with ExitStack() as stack:
        ctx = Ctx(nc, stack)
        xin = nc.dram_tensor("xres_in", [DM, width], F32, kind="ExternalInput").ap()
        xout = nc.dram_tensor("xres_out", [DM, width], F32, kind="ExternalOutput").ap()
        xa = nc.dram_tensor("xres_a", [DM, width], F32, kind="Internal").ap()
        xb = nc.dram_tensor("xres_b", [DM, width], F32, kind="Internal").ap()
        xs = [(xin, [Buf(f"xi{p}") for p in range(NPT)]),
              (xa, [Buf(f"xa{p}") for p in range(NPT)]),
              (xb, [Buf(f"xb{p}") for p in range(NPT)]),
              (xout, [Buf(f"xo{p}") for p in range(NPT)])]
        consts = declare_consts(nc, ctx, stack)
        Wd = {}
        for key, pref in (("l0f", "l0f_"), ("l0a", "l0a_"), ("f1", "f1_"),
                          ("l1a", "l1a_"), ("f2", "f2_"), ("l2a", "l2a_"),
                          ("l3f", "l3f_"), ("l3a", "l3a_")):
            spec = FUSED_SPECS["l12"] if key in ("f1", "f2") else FUSED_SPECS[key]
            Wd[key] = _decl_w(nc, pref, spec[0], spec[1])
        chain = [("l0f", 0), ("l0a", 0), ("l0f", 1),
                 ("f1", 0), ("l1a", 0), ("f1", 1),
                 ("f2", 0), ("l2a", 0), ("f2", 1),
                 ("l3f", 0), ("l3a", 0), ("l3f", 1)]
        cur = 0
        for si, (key, app) in enumerate(chain):
            src = xs[cur]
            if si == len(chain) - 1:
                dst = xs[3]
            else:
                dst = xs[1] if cur != 1 else xs[2]
            X = {"in": src, "out": dst}
            if key == "l0f":
                Wx = dict(Wd["l0f"])
                Wx["first"] = (app == 0)
                emit_l0_ffn(ctx, consts, X, Wx, width=width)
            elif key in ("f1", "f2"):
                emit_l12_ffn(ctx, consts, X, Wd[key], width=width, tag=key + str(app))
            else:
                FUSED_EMIT[key](ctx, consts, X, Wd[key], width=width)
            cur = 1 if dst is xs[1] else (2 if dst is xs[2] else 3)
        ctx.barrier()
    return nc


_FUSED_NC = [None]


def kernel(**inputs):
    i = inputs
    n_cores = 8
    base = dict(host_consts())
    w_l0f = {"w_upT": np.ascontiguousarray(i["L0_ffn_w_up"].T),
             "w_downT": np.ascontiguousarray(i["L0_ffn_w_down"].T)}
    preps = {
        "l0f_": w_l0f,
        "l0a_": prep_l0_attn(i["L0_attn_W_spin"], i["L0_attn_b_spin"],
                             i["L0_attn_W_global"], i["L0_attn_W_J"], i["L0_attn_beta"]),
        "f1_": prep_l12_ffn(i["L1_ffn_wqkv"], i["L1_ffn_q_scale"], i["L1_ffn_k_scale"],
                            i["L1_ffn_wo"], i["L1_ffn_ln_g"], i["L1_ffn_ln_b"],
                            i["L1_ffn_w1"], i["L1_ffn_b1"], i["L1_ffn_w2"], i["L1_ffn_b2"]),
        "l1a_": prep_l1_attn(i["L1_attn_Wqkv"], i["L1_attn_bqkv"], i["L1_attn_Wo"],
                             i["L1_attn_bo"], i["L1_attn_wq"], i["L1_attn_wk"]),
        "f2_": prep_l12_ffn(i["L2_ffn_wqkv"], i["L2_ffn_q_scale"], i["L2_ffn_k_scale"],
                            i["L2_ffn_wo"], i["L2_ffn_ln_g"], i["L2_ffn_ln_b"],
                            i["L2_ffn_w1"], i["L2_ffn_b1"], i["L2_ffn_w2"], i["L2_ffn_b2"]),
        "l2a_": prep_l2_attn(i["L2_attn_time_mix_r"], i["L2_attn_time_mix_k"],
                             i["L2_attn_time_mix_v"], i["L2_attn_W_r"], i["L2_attn_W_k"],
                             i["L2_attn_W_v"], i["L2_attn_W_o"], i["L2_attn_time_decay"],
                             i["L2_attn_time_first"], i["L2_attn_ln_gamma"],
                             i["L2_attn_ln_beta"]),
        "l3f_": prep_l3_ffn(i["L3_ffn_Wg"], i["L3_ffn_W1"], i["L3_ffn_b1"],
                            i["L3_ffn_W2"], i["L3_ffn_b2"]),
        "l3a_": prep_l3_attn(i["L3_attn_Wq"], i["L3_attn_Wk"], i["L3_attn_Wv"],
                             i["L3_attn_Wo"], i["L3_attn_q_scale"], i["L3_attn_k_scale"]),
    }
    for pref, d in preps.items():
        for k, v in d.items():
            if k == "first":
                continue
            base[pref + k] = v
    if _FUSED_NC[0] is None:
        _FUSED_NC[0] = build_fused_nc()
    nc = _FUSED_NC[0]
    x = inputs["L0_attn_x"].astype(np.float32, copy=False)
    in_maps = []
    for b in range(n_cores):
        m = dict(base)
        m["xres_in"] = np.ascontiguousarray(x[b].T)
        in_maps.append(m)
    res = run_bass_kernel_spmd(nc, in_maps, core_ids=list(range(n_cores)))
    out = np.stack([np.ascontiguousarray(r["xres_out"].T) for r in res.results], 0)
    return out.astype(np.float32)
